# version 18
# baseline (speedup 1.0000x reference)
import contextlib
import numpy as np
import ml_dtypes
import concourse.bass as bass
import concourse.mybir as mybir
from concourse.bass_utils import run_bass_kernel_spmd

F32 = mybir.dt.float32
BF16 = mybir.dt.bfloat16
AX = mybir.AxisListType.X
ALU = mybir.AluOpType
ACT = mybir.ActivationFunctionType

ENGINES = ("tensor", "vector", "scalar", "sync", "gpsimd")

E = 1024
H = 16
D = 64
NCH = 8
SEQ = 8192
TOK = 4096
HT = TOK + 512
NKT = HT // 128
NQT = TOK // 128
NG = NKT // 4
WW = 640
EPS = 1e-5
INTER = 4096
ST_TOK = 1024
NST = TOK // ST_TOK
STT = ST_TOK // 128

KRING = 12
QRING = 12
XRING = 6
X2RING = 10
D1RING = 2
W1RING = 3
W2RING = 2


class Buf:
    __slots__ = ("writer", "readers", "name")

    def __init__(self, name=""):
        self.writer = None
        self.readers = []
        self.name = name


class Prog:
    def __init__(self, nc):
        self.nc = nc
        self.sem = {}
        self.cnt = {}
        self.q = {e: [] for e in ENGINES}
        for e in ENGINES:
            self.sem[e] = nc.alloc_semaphore(f"sem_{e}")
            self.cnt[e] = 0
        self.mark = {e: {} for e in ENGINES}
        self.dma_sems = {}

    def dma_sem(self, name):
        if name not in self.dma_sems:
            self.dma_sems[name] = [self.nc.alloc_semaphore(f"dsem_{name}"), 0]
        return self.dma_sems[name]

    def _needed(self, ename, reads, writes):
        toks = []
        for b in reads:
            if b.writer is not None:
                toks.append(b.writer)
        for b in writes:
            if b.writer is not None:
                toks.append(b.writer)
            toks.extend(b.readers)
        merged = {}
        for (s, v) in toks:
            if merged.get(s.name, (None, -1))[1] < v:
                merged[s.name] = (s, v)
        wm = self.mark[ename]
        return [(s, v) for nm, (s, v) in merged.items() if wm.get(nm, 0) < v]

    def _note(self, ename, s, v):
        wm = self.mark[ename]
        if wm.get(s.name, 0) < v:
            wm[s.name] = v

    def emit(self, ename, build, reads=(), writes=(), inc=True, track=True):
        needed = self._needed(ename, reads, writes)
        for (s, v) in needed:
            self._note(ename, s, v)
        tok = None
        if inc:
            self.cnt[ename] += 1
            tok = (self.sem[ename], self.cnt[ename])

        def thunk(eng, build=build, needed=needed, tok=tok):
            for (s, v) in needed[:-1]:
                eng.wait_ge(s, v)
            inst = build(eng)
            if needed:
                inst._wait_ge(*needed[-1])
            if tok is not None:
                inst.then_inc(tok[0], 1)
        self.q[ename].append(thunk)
        if tok is not None and track:
            for b in writes:
                b.writer = tok
                b.readers = []
            for b in reads:
                b.readers.append(tok)
        return tok

    def dma(self, qname, semname, out_ap, in_ap, reads=(), writes=()):
        needed = self._needed(qname, reads, writes)
        ent = self.dma_sem(semname)
        if ent[1] > 0 and self.mark[qname].get(ent[0].name, 0) < ent[1]:
            needed.append((ent[0], ent[1]))
        for (s, v) in needed:
            self._note(qname, s, v)
        ent[1] += 16
        tok = (ent[0], ent[1])

        def thunk(eng, needed=needed, tok=tok, out_ap=out_ap, in_ap=in_ap):
            for (s, v) in needed[:-1]:
                eng.wait_ge(s, v)
            inst = eng.dma_start(out_ap, in_ap)
            if needed:
                inst._wait_ge(*needed[-1])
            inst.then_inc(tok[0], 16)
        self.q[qname].append(thunk)
        for b in writes:
            b.writer = tok
            b.readers = []
        for b in reads:
            b.readers.append(tok)
        return tok

    def group(self, ename, pieces, reads_all, writes, inc=True):
        n = len(pieces)
        for i, (bf, rd) in enumerate(pieces):
            if n == 1:
                self.emit(ename, bf, reads=reads_all, writes=writes, inc=inc)
            elif i == 0:
                self.emit(ename, bf, reads=rd, writes=writes, inc=False, track=False)
            elif i == n - 1:
                self.emit(ename, bf, reads=reads_all, writes=writes, inc=inc)
            else:
                self.emit(ename, bf, reads=rd, writes=(), inc=False)

    def wait(self, ename, tok):
        if tok is None:
            return
        s, v = tok
        if self.mark[ename].get(s.name, 0) >= v:
            return
        self._note(ename, s, v)

        def thunk(eng, s=s, v=v):
            eng.wait_ge(s, v)
        self.q[ename].append(thunk)

    def barrier(self):
        toks = [(self.sem[e], self.cnt[e]) for e in ENGINES if self.cnt[e] > 0]
        toks += [(h, c) for (h, c) in self.dma_sems.values() if c > 0]
        for e in ENGINES:
            for t in toks:
                self.wait(e, t)

    def flush(self):
        nc = self.nc
        with nc.Block() as block:
            for ename in ENGINES:
                thunks = self.q[ename]
                if not thunks:
                    continue

                def section(eng, thunks=thunks):
                    for th in thunks:
                        th(eng)
                getattr(block, ename)(section)
        self.q = {e: [] for e in ENGINES}


def mask_slot(qt):
    return {0: 0, 1: 1, NQT - 2: 3, NQT - 1: 4}.get(qt, 2)


def s_pieces(qt):
    out = []
    j = 0
    while j < WW:
        slot0 = (qt + j // 128) % KRING
        run = 128
        while j + run < WW and (qt + (j + run) // 128) % KRING == slot0 + run // 128:
            run += 128
        if j < 512 < j + run:
            run = 512 - j
        out.append((j, j + run, slot0 * 128))
        j += run
    return out


def build_program():
    nc = bass.Bass("TRN2", target_bir_lowering=False)

    x_d = nc.dram_tensor("x", [HT, E], F32, kind="ExternalInput")
    wqt_d = nc.dram_tensor("wqt", [E, E], F32, kind="ExternalInput")
    wkt_d = nc.dram_tensor("wkt", [E, E], F32, kind="ExternalInput")
    wvt_d = nc.dram_tensor("wvt", [E, E], F32, kind="ExternalInput")
    w1_d = nc.dram_tensor("w1", [E, INTER], F32, kind="ExternalInput")
    w2_d = nc.dram_tensor("w2", [INTER, E], F32, kind="ExternalInput")
    bq_d = nc.dram_tensor("bq", [128, 8], F32, kind="ExternalInput")
    bk_d = nc.dram_tensor("bk", [128, 8], F32, kind="ExternalInput")
    bv_d = nc.dram_tensor("bv", [1, E], F32, kind="ExternalInput")
    b1_d = nc.dram_tensor("b1", [128, 32], F32, kind="ExternalInput")
    b2_d = nc.dram_tensor("b2", [128, 8], F32, kind="ExternalInput")
    masks_d = nc.dram_tensor("masks", [5, 128, WW], BF16, kind="ExternalInput")
    ident_d = nc.dram_tensor("ident", [128, 128], BF16, kind="ExternalInput")
    o_d = nc.dram_tensor("o", [TOK, E], F32, kind="ExternalOutput")
    d1_d = nc.dram_tensor("d1", [TOK, E], F32, kind="Internal")

    p = Prog(nc)

    ident_sb = nc.alloc_sbuf_tensor("ident_sb", [128, 128], BF16, side="left")
    bq_sb = nc.alloc_sbuf_tensor("bq_sb", [128, 8], F32, side="left")
    bk_sb = nc.alloc_sbuf_tensor("bk_sb", [128, 8], F32, side="left")
    bv_sb = nc.alloc_sbuf_tensor("bv_sb", [1, E], BF16, side="left")
    b1_sb = nc.alloc_sbuf_tensor("b1_sb", [128, 32], F32, side="left")
    b2_sb = nc.alloc_sbuf_tensor("b2_sb", [128, 8], F32, side="left")
    ones_sb = nc.alloc_sbuf_tensor("ones_sb", [1, 128], BF16, side="left")
    c18_sb = nc.alloc_sbuf_tensor("c18_sb", [128, 1], F32, side="left")
    cm05_sb = nc.alloc_sbuf_tensor("cm05_sb", [128, 1], F32, side="left")
    bqs_sb = nc.alloc_sbuf_tensor("bqs_sb", [128, 8], F32, side="left")
    st_sb = nc.alloc_sbuf_tensor("st_sb", [128, 24], F32, side="left")
    mv_sb = nc.alloc_sbuf_tensor("mv_sb", [128, 4], F32, side="left")
    aux_sb = nc.alloc_sbuf_tensor("aux_sb", [128, 8], F32, side="left")

    b_const = Buf("const")
    b_lnp = [Buf("ln0"), Buf("ln1")]
    ln_par = [0]

    p.dma("sync", "ci", ident_sb[:], ident_d[:, :], writes=[b_const])
    p.dma("sync", "ci", bq_sb[:], bq_d[:, :], writes=[b_const])
    p.dma("sync", "ci", bk_sb[:], bk_d[:, :], writes=[b_const])
    p.dma("sync", "ci", b1_sb[:], b1_d[:, :], writes=[b_const])
    p.dma("sync", "ci", b2_sb[:], b2_d[:, :], writes=[b_const])
    p.dma("gpsimd", "cg", bv_sb[:], bv_d[:, :], writes=[b_const])
    p.emit("vector", lambda e: e.memset(ones_sb[:], 1.0), reads=[b_const], writes=[b_const])
    p.emit("vector", lambda e: e.memset(c18_sb[:], 0.125), reads=[b_const], writes=[b_const])
    p.emit("vector", lambda e: e.memset(cm05_sb[:], -0.5), reads=[b_const], writes=[b_const])
    p.emit("vector", lambda e: e.tensor_scalar_mul(bqs_sb[:], bq_sb[:], 0.125),
           reads=[b_const], writes=[b_const])

    def ln_chain(src_ap, src_buf, dst_ap, dst_buf):
        par = ln_par[0]
        ln_par[0] ^= 1
        lb = b_lnp[par]
        st = st_sb[:, 12 * par:12 * par + 12]
        mv = mv_sb[:, 2 * par:2 * par + 2]
        vpe = aux_sb[:, 4 * par + 0:4 * par + 1]
        sd = aux_sb[:, 4 * par + 1:4 * par + 2]
        r = aux_sb[:, 4 * par + 2:4 * par + 3]
        nmr = aux_sb[:, 4 * par + 3:4 * par + 4]
        p.emit("vector", lambda e: e.bn_stats(st[:, 0:6], src_ap[:, 0:512]),
               reads=[src_buf], writes=[lb])
        p.emit("vector", lambda e: e.bn_stats(st[:, 6:12], src_ap[:, 512:1024]),
               reads=[src_buf, lb], writes=[lb])
        p.emit("vector", lambda e: e.bn_aggr(mv, st),
               reads=[lb], writes=[lb])
        p.emit("vector", lambda e: e.tensor_scalar_add(vpe, mv[:, 1:2], float(EPS)),
               reads=[lb], writes=[lb])
        p.emit("scalar", lambda e: e.activation(sd, vpe, ACT.Ln),
               reads=[lb], writes=[lb])
        p.emit("scalar", lambda e: e.activation(r, sd, ACT.Exp, scale=cm05_sb[:, 0:1]),
               reads=[lb, b_const], writes=[lb])
        p.emit("vector", lambda e: e.scalar_tensor_tensor(
            nmr, mv[:, 0:1], -1.0, r, op0=ALU.mult, op1=ALU.mult),
            reads=[lb], writes=[lb])
        p.emit("scalar", lambda e: e.activation(
            dst_ap, src_ap, ACT.Identity, bias=nmr, scale=r),
            reads=[lb, src_buf], writes=[dst_buf])

    stack1 = contextlib.ExitStack()

    def sb1(name, shape, dtype):
        return stack1.enter_context(nc.sbuf_tensor(name, shape, dtype, side="right"))

    def ps1(name, shape):
        return stack1.enter_context(nc.psum_tensor(name, shape, F32))

    wq_sb = sb1("wq_sb", [128, NCH * E], BF16)
    wk_sb = sb1("wk_sb", [128, NCH * E], BF16)
    wv_sb = sb1("wv_sb", [128, NCH * E], BF16)
    masks_sb = sb1("masks_sb", [128, 5 * WW], BF16)
    x_sb = sb1("x_sb", [128, XRING * E], F32)
    xn_sb = sb1("xn_sb", [128, 2 * E], BF16)
    xnT_sb = sb1("xnT_sb", [128, NCH * 512], BF16)
    qT_sb = sb1("qT_sb", [128, NCH * QRING * 128], BF16)
    kT_sb = sb1("kT_sb", [128, NCH * KRING * 128], BF16)
    v_sb = sb1("v_sb", [128, KRING * H * 65], BF16)
    pe_sb = [sb1(f"pe_sb{i}", [128, WW], BF16) for i in range(2)]
    ph_sb = [sb1(f"ph_sb{i}", [128, WW], BF16) for i in range(4)]
    pt_sb = [sb1(f"pt_sb{i}", [128, WW], BF16) for i in range(2)]
    cpair_sb = sb1("cpair_sb", [128, 256], BF16)
    negm_sb = sb1("negm_sb", [128, 2], F32)
    rl_sb = sb1("rl_sb", [128, H], F32)
    ctx_sb = sb1("ctx_sb", [128, 2 * E], F32)

    ps_pj0 = ps1("ps_pj0", [128, 512])
    ps_s = [ps1(f"ps_s{i}", [128, WW]) for i in range(2)]
    ps_ptA = ps1("ps_ptA", [128, 512])
    ps_m = [ps1(f"ps_m{i}", [128, 512]) for i in range(2)]

    b_w = Buf("w")
    b_masks = Buf("masks")
    b_x = [Buf(f"x{i}") for i in range(XRING)]
    b_xn = [Buf(f"xn{i}") for i in range(2)]
    b_xnT = Buf("xnT")
    b_q = [Buf(f"q{i}") for i in range(QRING)]
    b_k = [Buf(f"k{i}") for i in range(KRING)]
    b_v = [Buf(f"v{i}") for i in range(KRING)]
    b_pj = [Buf("pj0"), Buf("pj1")]
    b_s = [Buf("s0"), Buf("s1")]
    b_ptA = Buf("ptA")
    b_m = [Buf("m0"), Buf("m1")]
    b_pe = [Buf("pe0"), Buf("pe1")]
    b_ph = [Buf(f"ph{i}") for i in range(4)]
    b_pt = [Buf("pt0"), Buf("pt1")]
    b_cp = [Buf("cp0"), Buf("cp1")]
    b_negm = [Buf("negm0"), Buf("negm1")]
    b_rl = Buf("rl")
    b_ctx = [Buf("ctx0"), Buf("ctx1")]
    b_d1 = [Buf(f"d1_{i}") for i in range(NQT)]

    p.dma("gpsimd", "cg", wq_sb[:].rearrange("p (c e) -> p c e", c=NCH),
          wqt_d[:, :].rearrange("(c p) e -> p c e", p=128), writes=[b_w])
    p.dma("gpsimd", "cg", wk_sb[:].rearrange("p (c e) -> p c e", c=NCH),
          wkt_d[:, :].rearrange("(c p) e -> p c e", p=128), writes=[b_w])
    p.dma("gpsimd", "cg", wv_sb[:].rearrange("p (c e) -> p c e", c=NCH),
          wvt_d[:, :].rearrange("(c p) e -> p c e", p=128), writes=[b_w])
    p.dma("sync", "ci", masks_sb[:].rearrange("p (m w) -> p m w", m=5),
          masks_d[:, :, :].rearrange("m p w -> p m w"), writes=[b_masks])

    for slot in range(KRING):
        p.emit("vector", lambda e, slot=slot: e.memset(
            v_sb[:, slot * H * 65:(slot + 1) * H * 65]
            .rearrange("p (h d) -> p h d", d=65)[:, :, 64:65], 1.0),
            reads=[b_const], writes=[b_v[slot]])

    def next_pj():
        return (ps_pj0, b_pj[0])

    def tile_ln_t1(t):
        xs = x_sb[:, (t % XRING) * E:(t % XRING + 1) * E]
        xn = xn_sb[:, (t % 2) * E:(t % 2 + 1) * E]
        ln_chain(xs, b_x[t % XRING], xn, b_xn[t % 2])
        for hh in range(2):
            pst, pbb = next_pj()
            pieces = []
            for c in range(4):
                cc = hh * 4 + c
                pieces.append((lambda e, xn=xn, pst=pst, c=c, cc=cc: e.matmul(
                    pst[:, c * 128:(c + 1) * 128],
                    xn[:, cc * 128:(cc + 1) * 128], ident_sb[:],
                    start=True, stop=True), [b_xn[t % 2], b_const]))
            p.group("tensor", pieces, [b_xn[t % 2], b_const], [pbb])
            p.emit("scalar", lambda e, hh=hh, t=t, pst=pst: e.copy(
                xnT_sb[:].rearrange("p (c t) -> p c t", c=NCH)[:, hh * 4:(hh + 1) * 4, (t % 4) * 128:(t % 4 + 1) * 128],
                pst[:].rearrange("p (c t) -> p c t", c=4)),
                reads=[pbb], writes=[b_xnT])

    def group_proj_units(g):
        us = []
        t0 = 4 * g
        qts = [t - 2 for t in range(t0, t0 + 4) if 2 <= t < 2 + NQT]
        kc0 = (t0 % KRING) * 128
        def k_chunk(c):
            pst, pbb = next_pj()
            pieces = []
            for e8 in range(NCH):
                pieces.append((lambda e, c=c, e8=e8, pst=pst: e.matmul(
                    pst[:, 0:512],
                    wk_sb[:, e8 * E + c * 128:e8 * E + (c + 1) * 128],
                    xnT_sb[:, e8 * 512:(e8 + 1) * 512],
                    start=(e8 == 0), stop=(e8 == NCH - 1)),
                    [b_w, b_xnT]))
            kbufs = [b_k[(t0 + i) % KRING] for i in range(4)]
            p.group("tensor", pieces, [b_w, b_xnT], [pbb])

            def ev():
                p.emit("scalar", lambda e, c=c, kc0=kc0, pst=pst: e.activation(
                    kT_sb[:, c * (KRING * 128) + kc0:c * (KRING * 128) + kc0 + 512],
                    pst[:, 0:512], ACT.Identity, bias=bk_sb[:, c:c + 1]),
                    reads=[pbb, b_const], writes=kbufs)
            return ev
        def q_chunk(c):
            if not qts:
                return None
            if True:
                qn = len(qts) * 128
                xoff = ((qts[0] + 2) % 4) * 128
                qc0 = ((qts[0] + 2) % QRING) * 128
                pst, pbb = next_pj()
                pieces = []
                for e8 in range(NCH):
                    pieces.append((lambda e, c=c, e8=e8, xoff=xoff, qn=qn, pst=pst: e.matmul(
                        pst[:, 0:qn],
                        wq_sb[:, e8 * E + c * 128:e8 * E + (c + 1) * 128],
                        xnT_sb[:, e8 * 512 + xoff:e8 * 512 + xoff + qn],
                        start=(e8 == 0), stop=(e8 == NCH - 1)),
                        [b_w, b_xnT]))
                qbufs = [b_q[(qt + 2) % QRING] for qt in qts]
                p.group("tensor", pieces, [b_w, b_xnT], [pbb])

                def ev():
                    p.emit("scalar", lambda e, c=c, qc0=qc0, qn=qn, pst=pst: e.activation(
                        qT_sb[:, c * (QRING * 128) + qc0:c * (QRING * 128) + qc0 + qn],
                        pst[:, 0:qn], ACT.Identity, bias=bqs_sb[:, c:c + 1],
                        scale=c18_sb[:, 0:1]),
                        reads=[pbb, b_const], writes=qbufs)
                return ev
        def v_half(i, half):
            slot = (t0 + i) % KRING
            if True:
                pst, pbb = next_pj()
                pieces = [(lambda e, half=half, pst=pst: e.matmul(
                    pst[:, 0:512], ones_sb[:, 0:128],
                    bv_sb[:, half * 512:(half + 1) * 512],
                    start=True, stop=False), [b_const])]
                for e8 in range(NCH):
                    pieces.append((lambda e, i=i, half=half, e8=e8, pst=pst: e.matmul(
                        pst[:, 0:512],
                        xnT_sb[:, e8 * 512 + i * 128:e8 * 512 + (i + 1) * 128],
                        wv_sb[:, e8 * E + half * 512:e8 * E + (half + 1) * 512],
                        start=False, stop=(e8 == NCH - 1)),
                        [b_w, b_xnT, b_const]))
                p.group("tensor", pieces, [b_w, b_xnT, b_const], [pbb])

                def vev(e, slot=slot, half=half, pst=pst):
                    base = slot * H * 65 + half * 8 * 65
                    return e.copy(
                        v_sb[:, base:base + 8 * 65].rearrange(
                            "p (h d) -> p h d", d=65)[:, :, 0:64],
                        pst[:, 0:512].rearrange("p (h d) -> p h d", d=64))

                def ev():
                    p.emit("scalar", vev, reads=[pbb], writes=[b_v[slot]])
                return ev

        for c in range(NCH):
            us.append(lambda c=c: k_chunk(c))
            us.append(lambda c=c: q_chunk(c))
        for i in range(4):
            for half in range(2):
                us.append(lambda i=i, half=half: v_half(i, half))
        stag = {"pend": None}
        out = []

        def wrap(u):
            def run(u=u):
                if stag["pend"] is not None:
                    stag["pend"]()
                    stag["pend"] = None
                r = u()
                stag["pend"] = r if callable(r) else None
            return run

        def final():
            if stag["pend"] is not None:
                stag["pend"]()
                stag["pend"] = None
        return [wrap(u) for u in us] + [final]

    def attn_stage_A(qt, h):
        qslot = (qt + 2) % QRING
        wbufs = [b_k[(qt + i) % KRING] for i in range(5)]
        pcs = s_pieces(qt)
        j = h // 2
        lo = (h % 2) * 64
        pi = h % 2
        ps = ps_s[pi]
        pieces = []
        for (w0, w1, rc0) in pcs:
            pieces.append((lambda e, ps=ps, lo=lo, j=j, w0=w0, w1=w1, rc0=rc0, qslot=qslot: e.matmul(
                ps[:, w0:w1],
                qT_sb[lo:lo + 64, j * (QRING * 128) + qslot * 128:
                      j * (QRING * 128) + (qslot + 1) * 128],
                kT_sb[lo:lo + 64, j * (KRING * 128) + rc0:
                      j * (KRING * 128) + rc0 + (w1 - w0)],
                start=True, stop=True),
                [b_q[qslot]] + wbufs))
        p.group("tensor", pieces, [b_q[qslot]] + wbufs, [b_s[pi]])

    def attn_stage_Bmax(qt, h):
        pi = h % 2
        ps = ps_s[pi]
        nm = negm_sb[:, pi:pi + 1]
        p.emit("vector", lambda e, ps=ps, nm=nm: e.reduce_max(
            nm, ps[:, 0:WW], axis=AX, negate=True),
            reads=[b_s[pi]], writes=[b_negm[pi]])

    def attn_stage_Bexp(qt, h):
        ms = mask_slot(qt)
        mask = masks_sb[:, ms * WW:(ms + 1) * WW]
        pi = h % 2
        ps = ps_s[pi]
        nm = negm_sb[:, pi:pi + 1]
        pe = pe_sb[pi]
        ph = ph_sb[h % 4]
        p.emit("scalar", lambda e, ps=ps, pe=pe, nm=nm: e.activation(
            pe[:, 0:WW], ps[:, 0:WW], ACT.Exp, bias=nm),
            reads=[b_s[pi], b_negm[pi]], writes=[b_pe[pi]])
        p.emit("gpsimd", lambda e, pe=pe, ph=ph, mask=mask: e.tensor_tensor(
            ph[:, 0:WW], pe[:, 0:WW], mask[:, 0:WW], op=ALU.mult),
            reads=[b_pe[pi], b_masks], writes=[b_ph[h % 4]])

    def attn_stage_C(qt, h):
        pi = h % 2
        ph = ph_sb[h % 4]
        pt = pt_sb[pi]
        pieces = []
        for c in range(4):
            pieces.append((lambda e, ph=ph, c=c: e.matmul(
                ps_ptA[:, c * 128:(c + 1) * 128],
                ph[:, c * 128:(c + 1) * 128], ident_sb[:],
                start=True, stop=True),
                [b_ph[h % 4], b_const]))
        p.group("tensor", pieces, [b_ph[h % 4], b_const], [b_ptA])
        p.emit("tensor", lambda e, ph=ph, pi=pi: e.matmul(
            ps_m[pi][:, 0:128], ph[:, 512:640], ident_sb[:],
            start=True, stop=True),
            reads=[b_ph[h % 4], b_const], writes=[b_m[pi]])
        p.emit("scalar", lambda e, pt=pt: e.copy(pt[:, 0:512], ps_ptA[:, 0:512]),
               reads=[b_ptA], writes=[b_pt[pi]])
        p.emit("scalar", lambda e, pt=pt, pi=pi: e.copy(pt[:, 512:WW], ps_m[pi][:, 0:128]),
               reads=[b_m[pi]], writes=[b_pt[pi]])

    def attn_stage_D(qt, h):
        pi = h % 2
        pt = pt_sb[pi]
        vbufs = [b_v[(qt + i) % KRING] for i in range(5)]
        pieces = []
        for c in range(5):
            slot = (qt + c) % KRING
            pieces.append((lambda e, pt=pt, c=c, h=h, slot=slot, pi=pi: e.matmul(
                ps_m[pi][0:65, 128:256],
                v_sb[:, slot * H * 65 + h * 65:slot * H * 65 + (h + 1) * 65],
                pt[:, c * 128:(c + 1) * 128],
                start=(c == 0), stop=(c == 4)),
                [b_pt[pi]] + vbufs))
        p.group("tensor", pieces, [b_pt[pi]] + vbufs, [b_m[pi]])

    def attn_stage_Et(qt, h):
        pi = h % 2
        cp = cpair_sb[0:65, pi * 128:(pi + 1) * 128]
        p.emit("vector", lambda e, cp=cp, pi=pi: e.tensor_copy(cp, ps_m[pi][0:65, 128:256]),
               reads=[b_m[pi]], writes=[b_cp[pi]])
        p.emit("tensor", lambda e, cp=cp, pi=pi: e.matmul(
            ps_m[pi][:, 256:321], cp, ident_sb[0:65, 0:65], start=True, stop=True),
            reads=[b_cp[pi], b_const], writes=[b_m[pi]])

    def attn_stage_Ev(qt, h):
        pi = h % 2
        p.emit("vector", lambda e, h=h, pi=pi: e.reciprocal(
            rl_sb[:, h:h + 1], ps_m[pi][:, 320:321]),
            reads=[b_m[pi]], writes=[b_rl])
        p.emit("vector", lambda e, h=h, qt=qt, pi=pi: e.tensor_scalar(
            ctx_sb[:, (qt % 2) * E + h * 64:(qt % 2) * E + (h + 1) * 64],
            ps_m[pi][:, 256:320],
            rl_sb[:, h:h + 1], None, op0=ALU.mult),
            reads=[b_m[pi], b_rl], writes=[b_ctx[qt % 2]])

    def attn_finish(qt):
        cs = ctx_sb[:, (qt % 2) * E:(qt % 2 + 1) * E]
        p.dma("gpsimd", "d1w", d1_d[qt * 128:(qt + 1) * 128, :], cs,
              reads=[b_ctx[qt % 2]], writes=[b_d1[qt]])

    def x_dma(t):
        p.dma("sync", f"x{t % XRING}", x_sb[:, (t % XRING) * E:(t % XRING + 1) * E],
              x_d[t * 128:(t + 1) * 128, :], writes=[b_x[t % XRING]])

    def proj_units(g):
        us = [(lambda t=t: tile_ln_t1(t)) for t in range(4 * g, 4 * g + 4)]
        us += group_proj_units(g)
        return us

    cell_state = {"cells": [], "i": 0, "a": 0, "mx": 0, "em": 0}

    def _adv_a(limit):
        cs = cell_state["cells"]
        while cell_state["a"] <= limit and cell_state["a"] < len(cs):
            attn_stage_A(*cs[cell_state["a"]])
            cell_state["a"] += 1

    def _adv_mx(limit):
        cs = cell_state["cells"]
        while cell_state["mx"] <= limit and cell_state["mx"] < len(cs):
            _adv_a(cell_state["mx"])
            attn_stage_Bmax(*cs[cell_state["mx"]])
            cell_state["mx"] += 1

    def _adv_em(limit):
        cs = cell_state["cells"]
        while cell_state["em"] <= limit and cell_state["em"] < len(cs):
            _adv_mx(cell_state["em"])
            attn_stage_Bexp(*cs[cell_state["em"]])
            cell_state["em"] += 1

    def cell_unit():
        cs = cell_state["cells"]
        i = cell_state["i"]
        _adv_em(i + 2)
        if i >= 2:
            attn_stage_Ev(*cs[i - 2])
            if cs[i - 2][1] == H - 1:
                attn_finish(cs[i - 2][0])
        attn_stage_C(*cs[i])
        _adv_mx(i + 3)
        _adv_a(i + 4)
        attn_stage_D(*cs[i])
        attn_stage_Et(*cs[i])
        cell_state["i"] = i + 1

    def cell_push_and_emit(new_cells, tu):
        cell_state["cells"].extend(new_cells)
        na, nb = len(new_cells), len(tu)
        ia = ib = 0
        while ia < na or ib < nb:
            if ia < na:
                cell_unit()
                ia += 1
            while ib < nb and (na == 0 or ib * na <= ia * nb):
                tu[ib]()
                ib += 1

    def cell_drain():
        cs = cell_state["cells"]
        for j in (len(cs) - 2, len(cs) - 1):
            attn_stage_Ev(*cs[j])
            if cs[j][1] == H - 1:
                attn_finish(cs[j][0])

    for t in range(0, 4):
        x_dma(t)
    for u in proj_units(0):
        u()
    for g in range(NG):
        new_cells = []
        for qt in range(max(0, 4 * g - 4), min(NQT, 4 * g)):
            for h in range(H):
                new_cells.append((qt, h))
        tu = []
        if g + 1 < NG:
            for t in range(4 * (g + 1), 4 * (g + 1) + 4):
                x_dma(t)
            tu = proj_units(g + 1)
        cell_push_and_emit(new_cells, tu)
    cell_drain()

    p.flush()
    stack1.close()

    p.barrier()

    w1r_sb = nc.alloc_sbuf_tensor("w1r_sb", [128, W1RING * NCH * 512], BF16, side="right")
    w2r_sb = nc.alloc_sbuf_tensor("w2r_sb", [128, W2RING * 32 * 128], BF16, side="right")
    x2_sb = nc.alloc_sbuf_tensor("x2_sb", [128, X2RING * E], F32, side="right")
    d1l_sb = nc.alloc_sbuf_tensor("d1l_sb", [128, D1RING * E], F32, side="right")
    xn2_sb = nc.alloc_sbuf_tensor("xn2_sb", [128, 2 * E], BF16, side="right")
    xn2T_sb = nc.alloc_sbuf_tensor("xn2T_sb", [128, NCH * ST_TOK], BF16, side="right")
    h_sb = nc.alloc_sbuf_tensor("h_sb", [128, 32 * ST_TOK], BF16, side="right")
    o2_sb = nc.alloc_sbuf_tensor("o2_sb", [128, NCH * ST_TOK], BF16, side="right")
    ffn_sb = nc.alloc_sbuf_tensor("ffn_sb", [128, 2 * E], F32, side="right")
    ob_sb = nc.alloc_sbuf_tensor("ob_sb", [128, 2 * E], F32, side="right")

    ps_t4a = nc.alloc_psum_tensor("ps_t4a", [128, 512], F32)
    ps_t4b = nc.alloc_psum_tensor("ps_t4b", [128, 512], F32)
    ps_h0 = nc.alloc_psum_tensor("ps_h0", [128, 1024], F32)
    ps_h1 = nc.alloc_psum_tensor("ps_h1", [128, 1024], F32)
    ps_o2 = nc.alloc_psum_tensor("ps_o2", [128, 1024], F32)

    b_w1 = [Buf(f"w1_{i}") for i in range(W1RING)]
    b_w2 = [Buf(f"w2_{i}") for i in range(W2RING)]
    b_x2 = [Buf(f"x2_{i}") for i in range(X2RING)]
    b_d1l = [Buf(f"d1l_{i}") for i in range(D1RING)]
    b_xn2 = [Buf("xn2_0"), Buf("xn2_1")]
    b_xn2T = [Buf("xn2T0"), Buf("xn2T1")]
    b_h = Buf("h")
    b_o2 = Buf("o2")
    b_ffn = [Buf("ffn0"), Buf("ffn1")]
    b_ob = [Buf("ob0"), Buf("ob1")]
    b_pt4 = [Buf("pt4a"), Buf("pt4b")]
    b_ph = [Buf(f"ph{i}") for i in range(4)]
    b_po2 = Buf("po2")

    def dma_w1(s, k4):
        slot = (s * 8 + k4) % W1RING
        ci0 = k4 * 4
        p.dma("gpsimd", f"w1_{slot}",
              w1r_sb[:, slot * NCH * 512:(slot + 1) * NCH * 512].rearrange("p (c i) -> p c i", c=NCH),
              w1_d[:, ci0 * 128:(ci0 + 4) * 128].rearrange("(c p) i -> p c i", p=128),
              writes=[b_w1[slot]])

    def dma_w2(s, eo):
        slot = (s * 8 + eo) % W2RING
        p.dma("gpsimd", f"w2_{slot}",
              w2r_sb[:, slot * 32 * 128:(slot + 1) * 32 * 128].rearrange("p (c e) -> p c e", c=32),
              w2_d[:, eo * 128:(eo + 1) * 128].rearrange("(c p) e -> p c e", p=128),
              writes=[b_w2[slot]])

    def tile_unit_a(s, tt):
        ti = s * STT + tt
        xsl = ti % X2RING
        dsl = ti % D1RING
        p.dma("sync", f"x2_{xsl}", x2_sb[:, xsl * E:(xsl + 1) * E],
              x_d[256 + ti * 128:256 + (ti + 1) * 128, :], writes=[b_x2[xsl]])
        p.dma("sync", f"d1l_{dsl}", d1l_sb[:, dsl * E:(dsl + 1) * E],
              d1_d[ti * 128:(ti + 1) * 128, :],
              reads=[b_d1[ti]], writes=[b_d1l[dsl]])
        dsv = d1l_sb[:, dsl * E:(dsl + 1) * E]
        ln_chain(dsv, b_d1l[dsl], dsv, b_d1l[dsl])

    def tile_unit(s, tt):
        ti = s * STT + tt
        xsl = ti % X2RING
        dsl = ti % D1RING
        dsv = d1l_sb[:, dsl * E:(dsl + 1) * E]
        p.emit("vector", lambda e, xsl=xsl, dsv=dsv: e.tensor_tensor(
            x2_sb[:, xsl * E:(xsl + 1) * E], x2_sb[:, xsl * E:(xsl + 1) * E],
            dsv, op=ALU.add),
            reads=[b_x2[xsl], b_d1l[dsl]], writes=[b_x2[xsl]])
        xn2 = xn2_sb[:, (tt % 2) * E:(tt % 2 + 1) * E]
        ln_chain(x2_sb[:, xsl * E:(xsl + 1) * E], b_x2[xsl], xn2, b_xn2[tt % 2])
        for hh in range(2):
            pst = ps_t4a if hh == 0 else ps_t4b
            pbb = b_pt4[hh]
            pieces = []
            for c in range(4):
                cc = hh * 4 + c
                pieces.append((lambda e, xn2=xn2, pst=pst, c=c, cc=cc: e.matmul(
                    pst[:, c * 128:(c + 1) * 128],
                    xn2[:, cc * 128:(cc + 1) * 128], ident_sb[:],
                    start=True, stop=True), [b_xn2[tt % 2], b_const]))
            p.group("tensor", pieces, [b_xn2[tt % 2], b_const], [pbb])
            p.emit("scalar", lambda e, hh=hh, tt=tt, pst=pst: e.copy(
                xn2T_sb[:].rearrange("p (c t) -> p c t", c=NCH)[:, hh * 4:(hh + 1) * 4, tt * 128:(tt + 1) * 128],
                pst[:].rearrange("p (c t) -> p c t", c=4)),
                reads=[pbb], writes=[b_xn2T[tt // 4]])

    def w1_unit(s, ci, half):
        if True:
            slot = (s * 8 + ci // 4) % W1RING
            wbase = slot * NCH * 512 + (ci % 4) * 128
            psh = ps_h0 if ci % 2 == 0 else ps_h1
            pbh = b_ph[ci % 2]
            pieces = []
            for e8 in range(NCH):
                pieces.append((lambda e, e8=e8, half=half, psh=psh, wbase=wbase: e.matmul(
                    psh[:, half * 512:(half + 1) * 512],
                    w1r_sb[:, wbase + e8 * 512:wbase + e8 * 512 + 128],
                    xn2T_sb[:, e8 * ST_TOK + half * 512:e8 * ST_TOK + (half + 1) * 512],
                    start=(e8 == 0), stop=(e8 == NCH - 1)),
                    [b_w1[slot], b_xn2T[half]]))
            p.group("tensor", pieces, [b_w1[slot], b_xn2T[half]], [pbh],
                    inc=(half == 1))
            if half == 1:
                p.emit("vector", lambda e, ci=ci, psh=psh: e.tensor_scalar(
                    h_sb[:, ci * ST_TOK:(ci + 1) * ST_TOK], psh[:],
                    b1_sb[:, ci:ci + 1], 0.0, op0=ALU.add, op1=ALU.max),
                    reads=[pbh, b_const], writes=[b_h])
                if ci % 4 == 3:
                    k4n = ci // 4 + W1RING
                    if k4n < 8:
                        dma_w1(s, k4n)
                    elif s + 1 < NST:
                        dma_w1(s + 1, k4n - 8)
                    if ci // 4 < W2RING:
                        dma_w2(s, ci // 4)
    def w2_unit(s, eo):
        if True:
            slot = (s * 8 + eo) % W2RING
            pieces = []
            for ic in range(32):
                for half in range(2):
                    pieces.append((lambda e, ic=ic, half=half, slot=slot: e.matmul(
                        ps_o2[:, half * 512:(half + 1) * 512],
                        w2r_sb[:, slot * 32 * 128 + ic * 128:slot * 32 * 128 + (ic + 1) * 128],
                        h_sb[:, ic * ST_TOK + half * 512:ic * ST_TOK + (half + 1) * 512],
                        start=(ic == 0), stop=(ic == 31)),
                        [b_w2[slot], b_h]))
            p.group("tensor", pieces, [b_w2[slot], b_h], [b_po2])
            p.emit("scalar", lambda e, eo=eo: e.activation(
                o2_sb[:, eo * ST_TOK:(eo + 1) * ST_TOK], ps_o2[:], ACT.Identity,
                bias=b2_sb[:, eo:eo + 1]),
                reads=[b_po2, b_const], writes=[b_o2])
            if eo + W2RING < 8:
                dma_w2(s, eo + W2RING)
    def t5_unit(s, tt):
        if True:
            ti = s * STT + tt
            xsl = ti % X2RING
            pieces = []
            for eo in range(NCH):
                pieces.append((lambda e, eo=eo, tt=tt: e.matmul(
                    ps_o2[:, eo * 128:(eo + 1) * 128],
                    o2_sb[:, eo * ST_TOK + tt * 128:eo * ST_TOK + (tt + 1) * 128],
                    ident_sb[:], start=True, stop=True),
                    [b_o2, b_const]))
            p.group("tensor", pieces, [b_o2, b_const], [b_po2])
            fs = ffn_sb[:, (tt % 2) * E:(tt % 2 + 1) * E]
            p.emit("scalar", lambda e, fs=fs: e.copy(fs, ps_o2[:]),
                   reads=[b_po2], writes=[b_ffn[tt % 2]])
            ob = ob_sb[:, (tt % 2) * E:(tt % 2 + 1) * E]
            ln_chain(fs, b_ffn[tt % 2], ob, b_ob[tt % 2])
            p.emit("vector", lambda e, ob=ob, xsl=xsl: e.tensor_tensor(
                ob, ob, x2_sb[:, xsl * E:(xsl + 1) * E], op=ALU.add),
                reads=[b_ob[tt % 2], b_x2[xsl]], writes=[b_ob[tt % 2]])
            p.dma("sync", "out", o_d[ti * 128:(ti + 1) * 128, :], ob,
                  reads=[b_ob[tt % 2]])

    def interleave(ua, ub):
        na, nb = len(ua), len(ub)
        ia = ib = 0
        while ia < na or ib < nb:
            if ia < na:
                ua[ia]()
                ia += 1
            while ib < nb and (na == 0 or ib * na <= ia * nb):
                ub[ib]()
                ib += 1

    def tiles_and_t5_units(s):
        seq = []
        if s > 0:
            for i in range(3):
                seq.append(lambda s=s, i=i: t5_unit(s - 1, i))
        seq.append(lambda s=s: tile_unit_a(s, 0))
        for tt in range(STT):
            if tt + 1 < STT:
                seq.append(lambda s=s, tt=tt: tile_unit_a(s, tt + 1))
            seq.append(lambda s=s, tt=tt: tile_unit(s, tt))
            if s > 0 and tt + 3 < STT:
                seq.append(lambda s=s, tt=tt: t5_unit(s - 1, tt + 3))
        return seq

    for k4 in range(W1RING):
        dma_w1(0, k4)
    for s in range(NST):
        for u in tiles_and_t5_units(s):
            u()
        for ci in range(32):
            w1_unit(s, ci, 0)
            w1_unit(s, ci, 1)
        for eo in range(8):
            w2_unit(s, eo)
    for tt in range(STT):
        t5_unit(NST - 1, tt)

    sem, cntv = p.dma_sems["out"]
    p.wait("sync", (sem, cntv))
    p.flush()
    return nc


_NC_CACHE = {}


def _get_nc():
    if "nc" not in _NC_CACHE:
        _NC_CACHE["nc"] = build_program()
    return _NC_CACHE["nc"]


def _make_masks(h):
    i = np.arange(128)[:, None]
    out = np.zeros((5, 128, WW), np.float32)
    for si, qt in enumerate((0, 1, 15, NQT - 2, NQT - 1)):
        j = np.arange(WW)[None, :]
        band = np.abs(j - 256 - i) <= 256
        g = h * 4096 + qt * 128 - 256 + j
        valid = (g >= 0) & (g < SEQ)
        out[si] = (band & valid).astype(np.float32)
    return out.astype(ml_dtypes.bfloat16)


def kernel(L0_attn_hidden_states, L0_attn_Wq, L0_attn_bq, L0_attn_Wk, L0_attn_bk,
           L0_attn_Wv, L0_attn_bv, L0_ffn_W1, L0_ffn_b1, L0_ffn_W2, L0_ffn_b2):
    x = np.ascontiguousarray(np.asarray(L0_attn_hidden_states, dtype=np.float32))
    wq = np.asarray(L0_attn_Wq, np.float32)
    wk = np.asarray(L0_attn_Wk, np.float32)
    wv = np.asarray(L0_attn_Wv, np.float32)
    w1 = np.ascontiguousarray(np.asarray(L0_ffn_W1, np.float32))
    w2 = np.ascontiguousarray(np.asarray(L0_ffn_W2, np.float32))
    bq = np.asarray(L0_attn_bq, np.float32)
    bk = np.asarray(L0_attn_bk, np.float32)
    bv = np.asarray(L0_attn_bv, np.float32)
    b1 = np.asarray(L0_ffn_b1, np.float32)
    b2 = np.asarray(L0_ffn_b2, np.float32)

    wqt = np.ascontiguousarray(wq.T)
    wkt = np.ascontiguousarray(wk.T)
    wvt = np.ascontiguousarray(wv.T)
    bq_l = np.ascontiguousarray(bq.reshape(8, 128).T)
    bk_l = np.ascontiguousarray(bk.reshape(8, 128).T)
    bv_l = np.ascontiguousarray(bv.reshape(1, E))
    b1_l = np.ascontiguousarray(b1.reshape(32, 128).T)
    b2_l = np.ascontiguousarray(b2.reshape(8, 128).T)
    ident = np.eye(128, dtype=ml_dtypes.bfloat16)
    masks_h = [_make_masks(0), _make_masks(1)]

    in_maps = []
    for c in range(8):
        b, h = c // 2, c % 2
        xs = np.zeros((HT, E), np.float32)
        if h == 0:
            xs[256:] = x[b, 0:HT - 256]
        else:
            xs[:HT - 256] = x[b, SEQ - (HT - 256):SEQ]
        in_maps.append({
            "x": xs, "wqt": wqt, "wkt": wkt, "wvt": wvt, "w1": w1, "w2": w2,
            "bq": bq_l, "bk": bk_l, "bv": bv_l, "b1": b1_l, "b2": b2_l,
            "masks": masks_h[h], "ident": ident,
        })

    nc = _get_nc()
    res = run_bass_kernel_spmd(nc, in_maps, core_ids=list(range(8)))
    out = np.empty((4, SEQ, E), np.float32)
    for c in range(8):
        b, h = c // 2, c % 2
        out[b, h * 4096:(h + 1) * 4096] = np.asarray(res.results[c]["o"], np.float32)
    return out


# version 20
# speedup vs baseline: 1.0804x; 1.0804x over previous
import contextlib
import numpy as np
import ml_dtypes
import concourse.bass as bass
import concourse.mybir as mybir
from concourse.bass_utils import run_bass_kernel_spmd

F32 = mybir.dt.float32
BF16 = mybir.dt.bfloat16
AX = mybir.AxisListType.X
ALU = mybir.AluOpType
ACT = mybir.ActivationFunctionType

ENGINES = ("tensor", "vector", "scalar", "sync", "gpsimd")

E = 1024
H = 16
D = 64
NCH = 8
SEQ = 8192
TOK = 4096
HT = TOK + 512
NKT = HT // 128
NQT = TOK // 128
NG = NKT // 4
WW = 640
EPS = 1e-5
INTER = 4096
ST_TOK = 1024
NST = TOK // ST_TOK
STT = ST_TOK // 128

KRING = 12
QRING = 12
XRING = 6
X2RING = 10
D1RING = 2
W1RING = 3
W2RING = 2


class Buf:
    __slots__ = ("writer", "readers", "name")

    def __init__(self, name=""):
        self.writer = None
        self.readers = []
        self.name = name


class Prog:
    def __init__(self, nc):
        self.nc = nc
        self.sem = {}
        self.cnt = {}
        self.q = {e: [] for e in ENGINES}
        for e in ENGINES:
            self.sem[e] = nc.alloc_semaphore(f"sem_{e}")
            self.cnt[e] = 0
        self.mark = {e: {} for e in ENGINES}
        self.dma_sems = {}

    def dma_sem(self, name):
        if name not in self.dma_sems:
            self.dma_sems[name] = [self.nc.alloc_semaphore(f"dsem_{name}"), 0]
        return self.dma_sems[name]

    def _needed(self, ename, reads, writes):
        toks = []
        for b in reads:
            if b.writer is not None:
                toks.append(b.writer)
        for b in writes:
            if b.writer is not None:
                toks.append(b.writer)
            toks.extend(b.readers)
        merged = {}
        for (s, v) in toks:
            if merged.get(s.name, (None, -1))[1] < v:
                merged[s.name] = (s, v)
        wm = self.mark[ename]
        return [(s, v) for nm, (s, v) in merged.items() if wm.get(nm, 0) < v]

    def _note(self, ename, s, v):
        wm = self.mark[ename]
        if wm.get(s.name, 0) < v:
            wm[s.name] = v

    def emit(self, ename, build, reads=(), writes=(), inc=True, track=True):
        needed = self._needed(ename, reads, writes)
        for (s, v) in needed:
            self._note(ename, s, v)
        tok = None
        if inc:
            self.cnt[ename] += 1
            tok = (self.sem[ename], self.cnt[ename])

        def thunk(eng, build=build, needed=needed, tok=tok):
            for (s, v) in needed[:-1]:
                eng.wait_ge(s, v)
            inst = build(eng)
            if needed:
                inst._wait_ge(*needed[-1])
            if tok is not None:
                inst.then_inc(tok[0], 1)
        self.q[ename].append(thunk)
        if tok is not None and track:
            for b in writes:
                b.writer = tok
                b.readers = []
            for b in reads:
                b.readers.append(tok)
        return tok

    def dma(self, qname, semname, out_ap, in_ap, reads=(), writes=()):
        needed = self._needed(qname, reads, writes)
        ent = self.dma_sem(semname)
        if ent[1] > 0 and self.mark[qname].get(ent[0].name, 0) < ent[1]:
            needed.append((ent[0], ent[1]))
        for (s, v) in needed:
            self._note(qname, s, v)
        ent[1] += 16
        tok = (ent[0], ent[1])

        def thunk(eng, needed=needed, tok=tok, out_ap=out_ap, in_ap=in_ap):
            for (s, v) in needed[:-1]:
                eng.wait_ge(s, v)
            inst = eng.dma_start(out_ap, in_ap)
            if needed:
                inst._wait_ge(*needed[-1])
            inst.then_inc(tok[0], 16)
        self.q[qname].append(thunk)
        for b in writes:
            b.writer = tok
            b.readers = []
        for b in reads:
            b.readers.append(tok)
        return tok

    def group(self, ename, pieces, reads_all, writes, inc=True):
        n = len(pieces)
        for i, (bf, rd) in enumerate(pieces):
            if n == 1:
                self.emit(ename, bf, reads=reads_all, writes=writes, inc=inc)
            elif i == 0:
                self.emit(ename, bf, reads=rd, writes=writes, inc=False, track=False)
            elif i == n - 1:
                self.emit(ename, bf, reads=reads_all, writes=writes, inc=inc)
            else:
                self.emit(ename, bf, reads=rd, writes=(), inc=False)

    def wait(self, ename, tok):
        if tok is None:
            return
        s, v = tok
        if self.mark[ename].get(s.name, 0) >= v:
            return
        self._note(ename, s, v)

        def thunk(eng, s=s, v=v):
            eng.wait_ge(s, v)
        self.q[ename].append(thunk)

    def barrier(self):
        toks = [(self.sem[e], self.cnt[e]) for e in ENGINES if self.cnt[e] > 0]
        toks += [(h, c) for (h, c) in self.dma_sems.values() if c > 0]
        for e in ENGINES:
            for t in toks:
                self.wait(e, t)

    def flush(self):
        nc = self.nc
        with nc.Block() as block:
            for ename in ENGINES:
                thunks = self.q[ename]
                if not thunks:
                    continue

                def section(eng, thunks=thunks):
                    for th in thunks:
                        th(eng)
                getattr(block, ename)(section)
        self.q = {e: [] for e in ENGINES}


def mask_slot(qt):
    return {0: 0, 1: 1, NQT - 2: 3, NQT - 1: 4}.get(qt, 2)


def s_pieces(qt):
    out = []
    j = 0
    while j < WW:
        slot0 = (qt + j // 128) % KRING
        run = 128
        while j + run < WW and (qt + (j + run) // 128) % KRING == slot0 + run // 128:
            run += 128
        if j < 512 < j + run:
            run = 512 - j
        out.append((j, j + run, slot0 * 128))
        j += run
    return out


def build_program():
    nc = bass.Bass("TRN2", target_bir_lowering=False)

    x_d = nc.dram_tensor("x", [HT, E], F32, kind="ExternalInput")
    wqt_d = nc.dram_tensor("wqt", [E, E], F32, kind="ExternalInput")
    wkt_d = nc.dram_tensor("wkt", [E, E], F32, kind="ExternalInput")
    wvt_d = nc.dram_tensor("wvt", [E, E], F32, kind="ExternalInput")
    w1_d = nc.dram_tensor("w1", [E, INTER], F32, kind="ExternalInput")
    w2_d = nc.dram_tensor("w2", [INTER, E], F32, kind="ExternalInput")
    bq_d = nc.dram_tensor("bq", [128, 8], F32, kind="ExternalInput")
    bk_d = nc.dram_tensor("bk", [128, 8], F32, kind="ExternalInput")
    bv_d = nc.dram_tensor("bv", [1, E], F32, kind="ExternalInput")
    b1_d = nc.dram_tensor("b1", [128, 32], F32, kind="ExternalInput")
    b2_d = nc.dram_tensor("b2", [128, 8], F32, kind="ExternalInput")
    masks_d = nc.dram_tensor("masks", [5, 128, WW], BF16, kind="ExternalInput")
    ident_d = nc.dram_tensor("ident", [128, 128], BF16, kind="ExternalInput")
    o_d = nc.dram_tensor("o", [TOK, E], F32, kind="ExternalOutput")
    d1_d = nc.dram_tensor("d1", [TOK, E], F32, kind="Internal")

    p = Prog(nc)

    ident_sb = nc.alloc_sbuf_tensor("ident_sb", [128, 128], BF16, side="left")
    bq_sb = nc.alloc_sbuf_tensor("bq_sb", [128, 8], F32, side="left")
    bk_sb = nc.alloc_sbuf_tensor("bk_sb", [128, 8], F32, side="left")
    bv_sb = nc.alloc_sbuf_tensor("bv_sb", [1, E], BF16, side="left")
    b1_sb = nc.alloc_sbuf_tensor("b1_sb", [128, 32], F32, side="left")
    b2_sb = nc.alloc_sbuf_tensor("b2_sb", [128, 8], F32, side="left")
    ones_sb = nc.alloc_sbuf_tensor("ones_sb", [1, 128], BF16, side="left")
    c18_sb = nc.alloc_sbuf_tensor("c18_sb", [128, 1], F32, side="left")
    cm05_sb = nc.alloc_sbuf_tensor("cm05_sb", [128, 1], F32, side="left")
    bqs_sb = nc.alloc_sbuf_tensor("bqs_sb", [128, 8], F32, side="left")
    st_sb = nc.alloc_sbuf_tensor("st_sb", [128, 24], F32, side="left")
    mv_sb = nc.alloc_sbuf_tensor("mv_sb", [128, 4], F32, side="left")
    aux_sb = nc.alloc_sbuf_tensor("aux_sb", [128, 8], F32, side="left")

    b_const = Buf("const")
    b_lnp = [Buf("ln0"), Buf("ln1")]
    ln_par = [0]

    p.dma("sync", "ci", ident_sb[:], ident_d[:, :], writes=[b_const])
    p.dma("sync", "ci", bq_sb[:], bq_d[:, :], writes=[b_const])
    p.dma("sync", "ci", bk_sb[:], bk_d[:, :], writes=[b_const])
    p.dma("sync", "ci", b1_sb[:], b1_d[:, :], writes=[b_const])
    p.dma("sync", "ci", b2_sb[:], b2_d[:, :], writes=[b_const])
    p.dma("gpsimd", "cg", bv_sb[:], bv_d[:, :], writes=[b_const])
    p.emit("vector", lambda e: e.memset(ones_sb[:], 1.0), reads=[b_const], writes=[b_const])
    p.emit("vector", lambda e: e.memset(c18_sb[:], 0.125), reads=[b_const], writes=[b_const])
    p.emit("vector", lambda e: e.memset(cm05_sb[:], -0.5), reads=[b_const], writes=[b_const])
    p.emit("vector", lambda e: e.tensor_scalar_mul(bqs_sb[:], bq_sb[:], 0.125),
           reads=[b_const], writes=[b_const])

    def ln_chain(src_ap, src_buf, dst_ap, dst_buf):
        par = ln_par[0]
        ln_par[0] ^= 1
        lb = b_lnp[par]
        st = st_sb[:, 12 * par:12 * par + 12]
        mv = mv_sb[:, 2 * par:2 * par + 2]
        vpe = aux_sb[:, 4 * par + 0:4 * par + 1]
        sd = aux_sb[:, 4 * par + 1:4 * par + 2]
        r = aux_sb[:, 4 * par + 2:4 * par + 3]
        nmr = aux_sb[:, 4 * par + 3:4 * par + 4]
        p.emit("vector", lambda e: e.bn_stats(st[:, 0:6], src_ap[:, 0:512]),
               reads=[src_buf], writes=[lb])
        p.emit("vector", lambda e: e.bn_stats(st[:, 6:12], src_ap[:, 512:1024]),
               reads=[src_buf, lb], writes=[lb])
        p.emit("vector", lambda e: e.bn_aggr(mv, st),
               reads=[lb], writes=[lb])
        p.emit("vector", lambda e: e.tensor_scalar_add(vpe, mv[:, 1:2], float(EPS)),
               reads=[lb], writes=[lb])
        p.emit("scalar", lambda e: e.activation(sd, vpe, ACT.Ln),
               reads=[lb], writes=[lb])
        p.emit("scalar", lambda e: e.activation(r, sd, ACT.Exp, scale=cm05_sb[:, 0:1]),
               reads=[lb, b_const], writes=[lb])
        p.emit("vector", lambda e: e.scalar_tensor_tensor(
            nmr, mv[:, 0:1], -1.0, r, op0=ALU.mult, op1=ALU.mult),
            reads=[lb], writes=[lb])
        p.emit("scalar", lambda e: e.activation(
            dst_ap, src_ap, ACT.Identity, bias=nmr, scale=r),
            reads=[lb, src_buf], writes=[dst_buf])

    stack1 = contextlib.ExitStack()

    def sb1(name, shape, dtype):
        return stack1.enter_context(nc.sbuf_tensor(name, shape, dtype, side="right"))

    def ps1(name, shape):
        return stack1.enter_context(nc.psum_tensor(name, shape, F32))

    wq_sb = sb1("wq_sb", [128, NCH * E], BF16)
    wk_sb = sb1("wk_sb", [128, NCH * E], BF16)
    wv_sb = sb1("wv_sb", [128, NCH * E], BF16)
    masks_sb = sb1("masks_sb", [128, 5 * WW], BF16)
    x_sb = sb1("x_sb", [128, XRING * E], F32)
    xn_sb = sb1("xn_sb", [128, 2 * E], BF16)
    xnT_sb = sb1("xnT_sb", [128, NCH * 512], BF16)
    qT_sb = sb1("qT_sb", [128, NCH * QRING * 128], BF16)
    kT_sb = sb1("kT_sb", [128, NCH * KRING * 128], BF16)
    v_sb = sb1("v_sb", [128, KRING * H * 65], BF16)
    pe_sb = [sb1(f"pe_sb{i}", [128, WW], BF16) for i in range(2)]
    ph_sb = [sb1(f"ph_sb{i}", [128, WW], BF16) for i in range(4)]
    pt_sb = [sb1(f"pt_sb{i}", [128, WW], BF16) for i in range(2)]
    cpair_sb = sb1("cpair_sb", [128, 256], BF16)
    negm_sb = sb1("negm_sb", [128, 2], F32)
    rl_sb = sb1("rl_sb", [128, H], F32)
    ctx_sb = sb1("ctx_sb", [128, 2 * E], F32)

    ps_pj0 = ps1("ps_pj0", [128, 512])
    ps_s = [ps1(f"ps_s{i}", [128, WW]) for i in range(2)]
    ps_ptA = ps1("ps_ptA", [128, 512])
    ps_m = [ps1(f"ps_m{i}", [128, 512]) for i in range(2)]

    b_w = Buf("w")
    b_masks = Buf("masks")
    b_x = [Buf(f"x{i}") for i in range(XRING)]
    b_xn = [Buf(f"xn{i}") for i in range(2)]
    b_xnT = Buf("xnT")
    b_q = [Buf(f"q{i}") for i in range(QRING)]
    b_k = [Buf(f"k{i}") for i in range(KRING)]
    b_v = [Buf(f"v{i}") for i in range(KRING)]
    b_pj = [Buf("pj0"), Buf("pj1")]
    b_s = [Buf("s0"), Buf("s1")]
    b_ptA = Buf("ptA")
    b_m = [Buf("m0"), Buf("m1")]
    b_pe = [Buf("pe0"), Buf("pe1")]
    b_ph = [Buf(f"ph{i}") for i in range(4)]
    b_pt = [Buf("pt0"), Buf("pt1")]
    b_cp = [Buf("cp0"), Buf("cp1")]
    b_negm = [Buf("negm0"), Buf("negm1")]
    b_rl = Buf("rl")
    b_ctx = [Buf("ctx0"), Buf("ctx1")]
    b_d1 = [Buf(f"d1_{i}") for i in range(NQT)]

    p.dma("gpsimd", "cg", wq_sb[:].rearrange("p (c e) -> p c e", c=NCH),
          wqt_d[:, :].rearrange("(c p) e -> p c e", p=128), writes=[b_w])
    p.dma("gpsimd", "cg", wk_sb[:].rearrange("p (c e) -> p c e", c=NCH),
          wkt_d[:, :].rearrange("(c p) e -> p c e", p=128), writes=[b_w])
    p.dma("gpsimd", "cg", wv_sb[:].rearrange("p (c e) -> p c e", c=NCH),
          wvt_d[:, :].rearrange("(c p) e -> p c e", p=128), writes=[b_w])
    p.dma("sync", "ci", masks_sb[:].rearrange("p (m w) -> p m w", m=5),
          masks_d[:, :, :].rearrange("m p w -> p m w"), writes=[b_masks])

    for slot in range(KRING):
        p.emit("vector", lambda e, slot=slot: e.memset(
            v_sb[:, slot * H * 65:(slot + 1) * H * 65]
            .rearrange("p (h d) -> p h d", d=65)[:, :, 64:65], 1.0),
            reads=[b_const], writes=[b_v[slot]])

    def next_pj():
        return (ps_pj0, b_pj[0])

    def tile_ln_t1(t):
        xs = x_sb[:, (t % XRING) * E:(t % XRING + 1) * E]
        xn = xn_sb[:, (t % 2) * E:(t % 2 + 1) * E]
        ln_chain(xs, b_x[t % XRING], xn, b_xn[t % 2])
        for hh in range(2):
            pst, pbb = next_pj()
            pieces = []
            for c in range(4):
                cc = hh * 4 + c
                pieces.append((lambda e, xn=xn, pst=pst, c=c, cc=cc: e.matmul(
                    pst[:, c * 128:(c + 1) * 128],
                    xn[:, cc * 128:(cc + 1) * 128], ident_sb[:],
                    start=True, stop=True), [b_xn[t % 2], b_const]))
            p.group("tensor", pieces, [b_xn[t % 2], b_const], [pbb])
            p.emit("scalar", lambda e, hh=hh, t=t, pst=pst: e.copy(
                xnT_sb[:].rearrange("p (c t) -> p c t", c=NCH)[:, hh * 4:(hh + 1) * 4, (t % 4) * 128:(t % 4 + 1) * 128],
                pst[:].rearrange("p (c t) -> p c t", c=4)),
                reads=[pbb], writes=[b_xnT])

    def group_proj_units(g):
        us = []
        t0 = 4 * g
        qts = [t - 2 for t in range(t0, t0 + 4) if 2 <= t < 2 + NQT]
        kc0 = (t0 % KRING) * 128
        def k_chunk(c):
            pst, pbb = next_pj()
            pieces = []
            for e8 in range(NCH):
                pieces.append((lambda e, c=c, e8=e8, pst=pst: e.matmul(
                    pst[:, 0:512],
                    wk_sb[:, e8 * E + c * 128:e8 * E + (c + 1) * 128],
                    xnT_sb[:, e8 * 512:(e8 + 1) * 512],
                    start=(e8 == 0), stop=(e8 == NCH - 1)),
                    [b_w, b_xnT]))
            kbufs = [b_k[(t0 + i) % KRING] for i in range(4)]
            p.group("tensor", pieces, [b_w, b_xnT], [pbb])

            def ev():
                p.emit("scalar", lambda e, c=c, kc0=kc0, pst=pst: e.activation(
                    kT_sb[:, c * (KRING * 128) + kc0:c * (KRING * 128) + kc0 + 512],
                    pst[:, 0:512], ACT.Identity, bias=bk_sb[:, c:c + 1]),
                    reads=[pbb, b_const], writes=kbufs)
            return ev
        def q_chunk(c):
            if not qts:
                return None
            if True:
                qn = len(qts) * 128
                xoff = ((qts[0] + 2) % 4) * 128
                qc0 = ((qts[0] + 2) % QRING) * 128
                pst, pbb = next_pj()
                pieces = []
                for e8 in range(NCH):
                    pieces.append((lambda e, c=c, e8=e8, xoff=xoff, qn=qn, pst=pst: e.matmul(
                        pst[:, 0:qn],
                        wq_sb[:, e8 * E + c * 128:e8 * E + (c + 1) * 128],
                        xnT_sb[:, e8 * 512 + xoff:e8 * 512 + xoff + qn],
                        start=(e8 == 0), stop=(e8 == NCH - 1)),
                        [b_w, b_xnT]))
                qbufs = [b_q[(qt + 2) % QRING] for qt in qts]
                p.group("tensor", pieces, [b_w, b_xnT], [pbb])

                def ev():
                    p.emit("scalar", lambda e, c=c, qc0=qc0, qn=qn, pst=pst: e.activation(
                        qT_sb[:, c * (QRING * 128) + qc0:c * (QRING * 128) + qc0 + qn],
                        pst[:, 0:qn], ACT.Identity, bias=bqs_sb[:, c:c + 1],
                        scale=c18_sb[:, 0:1]),
                        reads=[pbb, b_const], writes=qbufs)
                return ev
        def v_half(i, half):
            slot = (t0 + i) % KRING
            if True:
                pst, pbb = next_pj()
                pieces = [(lambda e, half=half, pst=pst: e.matmul(
                    pst[:, 0:512], ones_sb[:, 0:128],
                    bv_sb[:, half * 512:(half + 1) * 512],
                    start=True, stop=False), [b_const])]
                for e8 in range(NCH):
                    pieces.append((lambda e, i=i, half=half, e8=e8, pst=pst: e.matmul(
                        pst[:, 0:512],
                        xnT_sb[:, e8 * 512 + i * 128:e8 * 512 + (i + 1) * 128],
                        wv_sb[:, e8 * E + half * 512:e8 * E + (half + 1) * 512],
                        start=False, stop=(e8 == NCH - 1)),
                        [b_w, b_xnT, b_const]))
                p.group("tensor", pieces, [b_w, b_xnT, b_const], [pbb])

                def vev(e, slot=slot, half=half, pst=pst):
                    base = slot * H * 65 + half * 8 * 65
                    return e.copy(
                        v_sb[:, base:base + 8 * 65].rearrange(
                            "p (h d) -> p h d", d=65)[:, :, 0:64],
                        pst[:, 0:512].rearrange("p (h d) -> p h d", d=64))

                def ev():
                    p.emit("scalar", vev, reads=[pbb], writes=[b_v[slot]])
                return ev

        for c in range(NCH):
            us.append(lambda c=c: k_chunk(c))
            us.append(lambda c=c: q_chunk(c))
        for i in range(4):
            for half in range(2):
                us.append(lambda i=i, half=half: v_half(i, half))
        stag = {"pend": None}
        out = []

        def wrap(u):
            def run(u=u):
                if stag["pend"] is not None:
                    stag["pend"]()
                    stag["pend"] = None
                r = u()
                stag["pend"] = r if callable(r) else None
            return run

        def final():
            if stag["pend"] is not None:
                stag["pend"]()
                stag["pend"] = None
        return [wrap(u) for u in us] + [final]

    def attn_stage_A(qt, h):
        qslot = (qt + 2) % QRING
        wbufs = [b_k[(qt + i) % KRING] for i in range(5)]
        pcs = s_pieces(qt)
        j = h // 2
        lo = (h % 2) * 64
        pi = h % 2
        ps = ps_s[pi]
        pieces = []
        for (w0, w1, rc0) in pcs:
            pieces.append((lambda e, ps=ps, lo=lo, j=j, w0=w0, w1=w1, rc0=rc0, qslot=qslot: e.matmul(
                ps[:, w0:w1],
                qT_sb[lo:lo + 64, j * (QRING * 128) + qslot * 128:
                      j * (QRING * 128) + (qslot + 1) * 128],
                kT_sb[lo:lo + 64, j * (KRING * 128) + rc0:
                      j * (KRING * 128) + rc0 + (w1 - w0)],
                start=True, stop=True),
                [b_q[qslot]] + wbufs))
        p.group("tensor", pieces, [b_q[qslot]] + wbufs, [b_s[pi]])

    def attn_stage_Bmax(qt, h):
        pi = h % 2
        ps = ps_s[pi]
        nm = negm_sb[:, pi:pi + 1]
        p.emit("vector", lambda e, ps=ps, nm=nm: e.reduce_max(
            nm, ps[:, 0:WW], axis=AX, negate=True),
            reads=[b_s[pi]], writes=[b_negm[pi]])

    def attn_stage_Bexp(qt, h):
        ms = mask_slot(qt)
        mask = masks_sb[:, ms * WW:(ms + 1) * WW]
        pi = h % 2
        ps = ps_s[pi]
        nm = negm_sb[:, pi:pi + 1]
        pe = pe_sb[pi]
        ph = ph_sb[h % 4]
        p.emit("scalar", lambda e, ps=ps, pe=pe, nm=nm: e.activation(
            pe[:, 0:WW], ps[:, 0:WW], ACT.Exp, bias=nm),
            reads=[b_s[pi], b_negm[pi]], writes=[b_pe[pi]])
        p.emit("gpsimd", lambda e, pe=pe, ph=ph, mask=mask: e.tensor_tensor(
            ph[:, 0:WW], pe[:, 0:WW], mask[:, 0:WW], op=ALU.mult),
            reads=[b_pe[pi], b_masks], writes=[b_ph[h % 4]])

    def attn_stage_C(qt, h):
        pi = h % 2
        ph = ph_sb[h % 4]
        pt = pt_sb[pi]
        pieces = []
        for c in range(4):
            pieces.append((lambda e, ph=ph, c=c: e.matmul(
                ps_ptA[:, c * 128:(c + 1) * 128],
                ph[:, c * 128:(c + 1) * 128], ident_sb[:],
                start=True, stop=True),
                [b_ph[h % 4], b_const]))
        p.group("tensor", pieces, [b_ph[h % 4], b_const], [b_ptA])
        p.emit("tensor", lambda e, ph=ph, pi=pi: e.matmul(
            ps_m[pi][:, 0:128], ph[:, 512:640], ident_sb[:],
            start=True, stop=True),
            reads=[b_ph[h % 4], b_const], writes=[b_m[pi]])
        p.emit("scalar", lambda e, pt=pt: e.copy(pt[:, 0:512], ps_ptA[:, 0:512]),
               reads=[b_ptA], writes=[b_pt[pi]])
        p.emit("scalar", lambda e, pt=pt, pi=pi: e.copy(pt[:, 512:WW], ps_m[pi][:, 0:128]),
               reads=[b_m[pi]], writes=[b_pt[pi]])

    def attn_stage_D(qt, h):
        pi = h % 2
        pt = pt_sb[pi]
        vbufs = [b_v[(qt + i) % KRING] for i in range(5)]
        pieces = []
        for c in range(5):
            slot = (qt + c) % KRING
            pieces.append((lambda e, pt=pt, c=c, h=h, slot=slot, pi=pi: e.matmul(
                ps_m[pi][0:65, 128:256],
                v_sb[:, slot * H * 65 + h * 65:slot * H * 65 + (h + 1) * 65],
                pt[:, c * 128:(c + 1) * 128],
                start=(c == 0), stop=(c == 4)),
                [b_pt[pi]] + vbufs))
        p.group("tensor", pieces, [b_pt[pi]] + vbufs, [b_m[pi]])

    def attn_stage_Et(qt, h):
        pi = h % 2
        cp = cpair_sb[0:65, pi * 128:(pi + 1) * 128]
        p.emit("vector", lambda e, cp=cp, pi=pi: e.tensor_copy(cp, ps_m[pi][0:65, 128:256]),
               reads=[b_m[pi]], writes=[b_cp[pi]])
        p.emit("tensor", lambda e, cp=cp, pi=pi: e.matmul(
            ps_m[pi][:, 256:321], cp, ident_sb[0:65, 0:65], start=True, stop=True),
            reads=[b_cp[pi], b_const], writes=[b_m[pi]])

    def attn_stage_Ev(qt, h):
        pi = h % 2
        p.emit("vector", lambda e, h=h, pi=pi: e.reciprocal(
            rl_sb[:, h:h + 1], ps_m[pi][:, 320:321]),
            reads=[b_m[pi]], writes=[b_rl])
        p.emit("vector", lambda e, h=h, qt=qt, pi=pi: e.tensor_scalar(
            ctx_sb[:, (qt % 2) * E + h * 64:(qt % 2) * E + (h + 1) * 64],
            ps_m[pi][:, 256:320],
            rl_sb[:, h:h + 1], None, op0=ALU.mult),
            reads=[b_m[pi], b_rl], writes=[b_ctx[qt % 2]])

    def attn_finish(qt):
        cs = ctx_sb[:, (qt % 2) * E:(qt % 2 + 1) * E]
        p.dma("gpsimd", "d1w", d1_d[qt * 128:(qt + 1) * 128, :], cs,
              reads=[b_ctx[qt % 2]], writes=[b_d1[qt]])

    def x_dma(t):
        p.dma("sync", f"x{t % XRING}", x_sb[:, (t % XRING) * E:(t % XRING + 1) * E],
              x_d[t * 128:(t + 1) * 128, :], writes=[b_x[t % XRING]])

    def proj_units(g):
        us = [(lambda t=t: tile_ln_t1(t)) for t in range(4 * g, 4 * g + 4)]
        us += group_proj_units(g)
        return us

    cell_state = {"cells": [], "i": 0, "a": 0, "mx": 0, "em": 0}

    def _adv_a(limit):
        cs = cell_state["cells"]
        while cell_state["a"] <= limit and cell_state["a"] < len(cs):
            attn_stage_A(*cs[cell_state["a"]])
            cell_state["a"] += 1

    def _adv_mx(limit):
        cs = cell_state["cells"]
        while cell_state["mx"] <= limit and cell_state["mx"] < len(cs):
            _adv_a(cell_state["mx"])
            attn_stage_Bmax(*cs[cell_state["mx"]])
            cell_state["mx"] += 1

    def _adv_em(limit):
        cs = cell_state["cells"]
        while cell_state["em"] <= limit and cell_state["em"] < len(cs):
            _adv_mx(cell_state["em"])
            attn_stage_Bexp(*cs[cell_state["em"]])
            cell_state["em"] += 1

    def cell_unit():
        cs = cell_state["cells"]
        i = cell_state["i"]
        _adv_em(i + 2)
        if i >= 2:
            attn_stage_Ev(*cs[i - 2])
            if cs[i - 2][1] == H - 1:
                attn_finish(cs[i - 2][0])
        attn_stage_C(*cs[i])
        _adv_mx(i + 3)
        _adv_a(i + 4)
        if i >= 1:
            attn_stage_D(*cs[i - 1])
            attn_stage_Et(*cs[i - 1])
        cell_state["i"] = i + 1

    def cell_push_and_emit(new_cells, tu):
        cell_state["cells"].extend(new_cells)
        na, nb = len(new_cells), len(tu)
        ia = ib = 0
        while ia < na or ib < nb:
            if ia < na:
                cell_unit()
                ia += 1
            while ib < nb and (na == 0 or ib * na <= ia * nb):
                tu[ib]()
                ib += 1

    def cell_drain():
        cs = cell_state["cells"]
        n = len(cs)

        def ev(j):
            attn_stage_Ev(*cs[j])
            if cs[j][1] == H - 1:
                attn_finish(cs[j][0])
        ev(n - 2)
        attn_stage_D(*cs[n - 1])
        attn_stage_Et(*cs[n - 1])
        ev(n - 1)

    for t in range(0, 4):
        x_dma(t)
    for u in proj_units(0):
        u()
    for g in range(NG):
        new_cells = []
        for qt in range(max(0, 4 * g - 4), min(NQT, 4 * g)):
            for h in range(H):
                new_cells.append((qt, h))
        tu = []
        if g + 1 < NG:
            for t in range(4 * (g + 1), 4 * (g + 1) + 4):
                x_dma(t)
            tu = proj_units(g + 1)
        cell_push_and_emit(new_cells, tu)
    cell_drain()

    p.flush()
    stack1.close()

    p.barrier()

    w1r_sb = nc.alloc_sbuf_tensor("w1r_sb", [128, W1RING * NCH * 512], BF16, side="right")
    w2r_sb = nc.alloc_sbuf_tensor("w2r_sb", [128, W2RING * 32 * 128], BF16, side="right")
    x2_sb = nc.alloc_sbuf_tensor("x2_sb", [128, X2RING * E], F32, side="right")
    d1l_sb = nc.alloc_sbuf_tensor("d1l_sb", [128, D1RING * E], F32, side="right")
    xn2_sb = nc.alloc_sbuf_tensor("xn2_sb", [128, 2 * E], BF16, side="right")
    xn2T_sb = nc.alloc_sbuf_tensor("xn2T_sb", [128, NCH * ST_TOK], BF16, side="right")
    h_sb = nc.alloc_sbuf_tensor("h_sb", [128, 32 * ST_TOK], BF16, side="right")
    o2_sb = nc.alloc_sbuf_tensor("o2_sb", [128, NCH * ST_TOK], BF16, side="right")
    ffn_sb = nc.alloc_sbuf_tensor("ffn_sb", [128, 2 * E], F32, side="right")
    ob_sb = nc.alloc_sbuf_tensor("ob_sb", [128, 2 * E], F32, side="right")

    ps_t4a = nc.alloc_psum_tensor("ps_t4a", [128, 512], F32)
    ps_t4b = nc.alloc_psum_tensor("ps_t4b", [128, 512], F32)
    ps_h0 = nc.alloc_psum_tensor("ps_h0", [128, 1024], F32)
    ps_h1 = nc.alloc_psum_tensor("ps_h1", [128, 1024], F32)
    ps_o2 = nc.alloc_psum_tensor("ps_o2", [128, 1024], F32)

    b_w1 = [Buf(f"w1_{i}") for i in range(W1RING)]
    b_w2 = [Buf(f"w2_{i}") for i in range(W2RING)]
    b_x2 = [Buf(f"x2_{i}") for i in range(X2RING)]
    b_d1l = [Buf(f"d1l_{i}") for i in range(D1RING)]
    b_xn2 = [Buf("xn2_0"), Buf("xn2_1")]
    b_xn2T = [Buf("xn2T0"), Buf("xn2T1")]
    b_h = Buf("h")
    b_o2 = Buf("o2")
    b_ffn = [Buf("ffn0"), Buf("ffn1")]
    b_ob = [Buf("ob0"), Buf("ob1")]
    b_pt4 = [Buf("pt4a"), Buf("pt4b")]
    b_ph = [Buf(f"ph{i}") for i in range(4)]
    b_po2 = Buf("po2")

    def dma_w1(s, k4):
        slot = (s * 8 + k4) % W1RING
        ci0 = k4 * 4
        p.dma("gpsimd", f"w1_{slot}",
              w1r_sb[:, slot * NCH * 512:(slot + 1) * NCH * 512].rearrange("p (c i) -> p c i", c=NCH),
              w1_d[:, ci0 * 128:(ci0 + 4) * 128].rearrange("(c p) i -> p c i", p=128),
              writes=[b_w1[slot]])

    def dma_w2(s, eo):
        slot = (s * 8 + eo) % W2RING
        p.dma("gpsimd", f"w2_{slot}",
              w2r_sb[:, slot * 32 * 128:(slot + 1) * 32 * 128].rearrange("p (c e) -> p c e", c=32),
              w2_d[:, eo * 128:(eo + 1) * 128].rearrange("(c p) e -> p c e", p=128),
              writes=[b_w2[slot]])

    def tile_unit_a(s, tt):
        ti = s * STT + tt
        xsl = ti % X2RING
        dsl = ti % D1RING
        p.dma("sync", f"x2_{xsl}", x2_sb[:, xsl * E:(xsl + 1) * E],
              x_d[256 + ti * 128:256 + (ti + 1) * 128, :], writes=[b_x2[xsl]])
        p.dma("sync", f"d1l_{dsl}", d1l_sb[:, dsl * E:(dsl + 1) * E],
              d1_d[ti * 128:(ti + 1) * 128, :],
              reads=[b_d1[ti]], writes=[b_d1l[dsl]])
        dsv = d1l_sb[:, dsl * E:(dsl + 1) * E]
        ln_chain(dsv, b_d1l[dsl], dsv, b_d1l[dsl])

    def tile_unit(s, tt):
        ti = s * STT + tt
        xsl = ti % X2RING
        dsl = ti % D1RING
        dsv = d1l_sb[:, dsl * E:(dsl + 1) * E]
        p.emit("vector", lambda e, xsl=xsl, dsv=dsv: e.tensor_tensor(
            x2_sb[:, xsl * E:(xsl + 1) * E], x2_sb[:, xsl * E:(xsl + 1) * E],
            dsv, op=ALU.add),
            reads=[b_x2[xsl], b_d1l[dsl]], writes=[b_x2[xsl]])
        xn2 = xn2_sb[:, (tt % 2) * E:(tt % 2 + 1) * E]
        ln_chain(x2_sb[:, xsl * E:(xsl + 1) * E], b_x2[xsl], xn2, b_xn2[tt % 2])
        for hh in range(2):
            pst = ps_t4a if hh == 0 else ps_t4b
            pbb = b_pt4[hh]
            pieces = []
            for c in range(4):
                cc = hh * 4 + c
                pieces.append((lambda e, xn2=xn2, pst=pst, c=c, cc=cc: e.matmul(
                    pst[:, c * 128:(c + 1) * 128],
                    xn2[:, cc * 128:(cc + 1) * 128], ident_sb[:],
                    start=True, stop=True), [b_xn2[tt % 2], b_const]))
            p.group("tensor", pieces, [b_xn2[tt % 2], b_const], [pbb])
            p.emit("scalar", lambda e, hh=hh, tt=tt, pst=pst: e.copy(
                xn2T_sb[:].rearrange("p (c t) -> p c t", c=NCH)[:, hh * 4:(hh + 1) * 4, tt * 128:(tt + 1) * 128],
                pst[:].rearrange("p (c t) -> p c t", c=4)),
                reads=[pbb], writes=[b_xn2T[tt // 4]])

    def w1_unit(s, ci, half):
        if True:
            slot = (s * 8 + ci // 4) % W1RING
            wbase = slot * NCH * 512 + (ci % 4) * 128
            psh = ps_h0 if ci % 2 == 0 else ps_h1
            pbh = b_ph[ci % 2]
            pieces = []
            for e8 in range(NCH):
                pieces.append((lambda e, e8=e8, half=half, psh=psh, wbase=wbase: e.matmul(
                    psh[:, half * 512:(half + 1) * 512],
                    w1r_sb[:, wbase + e8 * 512:wbase + e8 * 512 + 128],
                    xn2T_sb[:, e8 * ST_TOK + half * 512:e8 * ST_TOK + (half + 1) * 512],
                    start=(e8 == 0), stop=(e8 == NCH - 1)),
                    [b_w1[slot], b_xn2T[half]]))
            p.group("tensor", pieces, [b_w1[slot], b_xn2T[half]], [pbh],
                    inc=(half == 1))
            if half == 1:
                p.emit("vector", lambda e, ci=ci, psh=psh: e.tensor_scalar(
                    h_sb[:, ci * ST_TOK:(ci + 1) * ST_TOK], psh[:],
                    b1_sb[:, ci:ci + 1], 0.0, op0=ALU.add, op1=ALU.max),
                    reads=[pbh, b_const], writes=[b_h])
                if ci % 4 == 3:
                    k4n = ci // 4 + W1RING
                    if k4n < 8:
                        dma_w1(s, k4n)
                    elif s + 1 < NST:
                        dma_w1(s + 1, k4n - 8)
                    if ci // 4 < W2RING:
                        dma_w2(s, ci // 4)
    def w2_unit(s, eo):
        if True:
            slot = (s * 8 + eo) % W2RING
            pieces = []
            for ic in range(32):
                for half in range(2):
                    pieces.append((lambda e, ic=ic, half=half, slot=slot: e.matmul(
                        ps_o2[:, half * 512:(half + 1) * 512],
                        w2r_sb[:, slot * 32 * 128 + ic * 128:slot * 32 * 128 + (ic + 1) * 128],
                        h_sb[:, ic * ST_TOK + half * 512:ic * ST_TOK + (half + 1) * 512],
                        start=(ic == 0), stop=(ic == 31)),
                        [b_w2[slot], b_h]))
            p.group("tensor", pieces, [b_w2[slot], b_h], [b_po2])
            p.emit("scalar", lambda e, eo=eo: e.activation(
                o2_sb[:, eo * ST_TOK:(eo + 1) * ST_TOK], ps_o2[:], ACT.Identity,
                bias=b2_sb[:, eo:eo + 1]),
                reads=[b_po2, b_const], writes=[b_o2])
            if eo + W2RING < 8:
                dma_w2(s, eo + W2RING)
    def t5_unit(s, tt):
        if True:
            ti = s * STT + tt
            xsl = ti % X2RING
            pieces = []
            for eo in range(NCH):
                pieces.append((lambda e, eo=eo, tt=tt: e.matmul(
                    ps_o2[:, eo * 128:(eo + 1) * 128],
                    o2_sb[:, eo * ST_TOK + tt * 128:eo * ST_TOK + (tt + 1) * 128],
                    ident_sb[:], start=True, stop=True),
                    [b_o2, b_const]))
            p.group("tensor", pieces, [b_o2, b_const], [b_po2])
            fs = ffn_sb[:, (tt % 2) * E:(tt % 2 + 1) * E]
            p.emit("scalar", lambda e, fs=fs: e.copy(fs, ps_o2[:]),
                   reads=[b_po2], writes=[b_ffn[tt % 2]])
            ob = ob_sb[:, (tt % 2) * E:(tt % 2 + 1) * E]
            ln_chain(fs, b_ffn[tt % 2], ob, b_ob[tt % 2])
            p.emit("vector", lambda e, ob=ob, xsl=xsl: e.tensor_tensor(
                ob, ob, x2_sb[:, xsl * E:(xsl + 1) * E], op=ALU.add),
                reads=[b_ob[tt % 2], b_x2[xsl]], writes=[b_ob[tt % 2]])
            p.dma("sync", "out", o_d[ti * 128:(ti + 1) * 128, :], ob,
                  reads=[b_ob[tt % 2]])

    def interleave(ua, ub):
        na, nb = len(ua), len(ub)
        ia = ib = 0
        while ia < na or ib < nb:
            if ia < na:
                ua[ia]()
                ia += 1
            while ib < nb and (na == 0 or ib * na <= ia * nb):
                ub[ib]()
                ib += 1

    def tiles_and_t5_units(s):
        seq = []
        if s > 0:
            for i in range(3):
                seq.append(lambda s=s, i=i: t5_unit(s - 1, i))
        seq.append(lambda s=s: tile_unit_a(s, 0))
        for tt in range(STT):
            if tt + 1 < STT:
                seq.append(lambda s=s, tt=tt: tile_unit_a(s, tt + 1))
            seq.append(lambda s=s, tt=tt: tile_unit(s, tt))
            if s > 0 and tt + 3 < STT:
                seq.append(lambda s=s, tt=tt: t5_unit(s - 1, tt + 3))
        return seq

    for k4 in range(W1RING):
        dma_w1(0, k4)
    for s in range(NST):
        for u in tiles_and_t5_units(s):
            u()
        for ci in range(32):
            w1_unit(s, ci, 0)
            w1_unit(s, ci, 1)
        for eo in range(8):
            w2_unit(s, eo)
    for tt in range(STT):
        t5_unit(NST - 1, tt)

    sem, cntv = p.dma_sems["out"]
    p.wait("sync", (sem, cntv))
    p.flush()
    return nc


_NC_CACHE = {}


def _get_nc():
    if "nc" not in _NC_CACHE:
        _NC_CACHE["nc"] = build_program()
    return _NC_CACHE["nc"]


def _make_masks(h):
    i = np.arange(128)[:, None]
    out = np.zeros((5, 128, WW), np.float32)
    for si, qt in enumerate((0, 1, 15, NQT - 2, NQT - 1)):
        j = np.arange(WW)[None, :]
        band = np.abs(j - 256 - i) <= 256
        g = h * 4096 + qt * 128 - 256 + j
        valid = (g >= 0) & (g < SEQ)
        out[si] = (band & valid).astype(np.float32)
    return out.astype(ml_dtypes.bfloat16)


def kernel(L0_attn_hidden_states, L0_attn_Wq, L0_attn_bq, L0_attn_Wk, L0_attn_bk,
           L0_attn_Wv, L0_attn_bv, L0_ffn_W1, L0_ffn_b1, L0_ffn_W2, L0_ffn_b2):
    x = np.ascontiguousarray(np.asarray(L0_attn_hidden_states, dtype=np.float32))
    wq = np.asarray(L0_attn_Wq, np.float32)
    wk = np.asarray(L0_attn_Wk, np.float32)
    wv = np.asarray(L0_attn_Wv, np.float32)
    w1 = np.ascontiguousarray(np.asarray(L0_ffn_W1, np.float32))
    w2 = np.ascontiguousarray(np.asarray(L0_ffn_W2, np.float32))
    bq = np.asarray(L0_attn_bq, np.float32)
    bk = np.asarray(L0_attn_bk, np.float32)
    bv = np.asarray(L0_attn_bv, np.float32)
    b1 = np.asarray(L0_ffn_b1, np.float32)
    b2 = np.asarray(L0_ffn_b2, np.float32)

    wqt = np.ascontiguousarray(wq.T)
    wkt = np.ascontiguousarray(wk.T)
    wvt = np.ascontiguousarray(wv.T)
    bq_l = np.ascontiguousarray(bq.reshape(8, 128).T)
    bk_l = np.ascontiguousarray(bk.reshape(8, 128).T)
    bv_l = np.ascontiguousarray(bv.reshape(1, E))
    b1_l = np.ascontiguousarray(b1.reshape(32, 128).T)
    b2_l = np.ascontiguousarray(b2.reshape(8, 128).T)
    ident = np.eye(128, dtype=ml_dtypes.bfloat16)
    masks_h = [_make_masks(0), _make_masks(1)]

    in_maps = []
    for c in range(8):
        b, h = c // 2, c % 2
        xs = np.zeros((HT, E), np.float32)
        if h == 0:
            xs[256:] = x[b, 0:HT - 256]
        else:
            xs[:HT - 256] = x[b, SEQ - (HT - 256):SEQ]
        in_maps.append({
            "x": xs, "wqt": wqt, "wkt": wkt, "wvt": wvt, "w1": w1, "w2": w2,
            "bq": bq_l, "bk": bk_l, "bv": bv_l, "b1": b1_l, "b2": b2_l,
            "masks": masks_h[h], "ident": ident,
        })

    nc = _get_nc()
    res = run_bass_kernel_spmd(nc, in_maps, core_ids=list(range(8)))
    out = np.empty((4, SEQ, E), np.float32)
    for c in range(8):
        b, h = c // 2, c % 2
        out[b, h * 4096:(h + 1) * 4096] = np.asarray(res.results[c]["o"], np.float32)
    return out


# version 21
# speedup vs baseline: 1.0828x; 1.0022x over previous
import contextlib
import numpy as np
import ml_dtypes
import concourse.bass as bass
import concourse.mybir as mybir
from concourse.bass_utils import run_bass_kernel_spmd

F32 = mybir.dt.float32
BF16 = mybir.dt.bfloat16
AX = mybir.AxisListType.X
ALU = mybir.AluOpType
ACT = mybir.ActivationFunctionType

ENGINES = ("tensor", "vector", "scalar", "sync", "gpsimd")

E = 1024
H = 16
D = 64
NCH = 8
SEQ = 8192
TOK = 4096
HT = TOK + 512
NKT = HT // 128
NQT = TOK // 128
NG = NKT // 4
WW = 640
EPS = 1e-5
INTER = 4096
ST_TOK = 1024
NST = TOK // ST_TOK
STT = ST_TOK // 128

KRING = 12
QRING = 12
XRING = 6
X2RING = 10
D1RING = 2
W1RING = 3
W2RING = 2


class Buf:
    __slots__ = ("writer", "readers", "name")

    def __init__(self, name=""):
        self.writer = None
        self.readers = []
        self.name = name


class Prog:
    def __init__(self, nc):
        self.nc = nc
        self.sem = {}
        self.cnt = {}
        self.q = {e: [] for e in ENGINES}
        for e in ENGINES:
            self.sem[e] = nc.alloc_semaphore(f"sem_{e}")
            self.cnt[e] = 0
        self.mark = {e: {} for e in ENGINES}
        self.dma_sems = {}

    def dma_sem(self, name):
        if name not in self.dma_sems:
            self.dma_sems[name] = [self.nc.alloc_semaphore(f"dsem_{name}"), 0]
        return self.dma_sems[name]

    def _needed(self, ename, reads, writes):
        toks = []
        for b in reads:
            if b.writer is not None:
                toks.append(b.writer)
        for b in writes:
            if b.writer is not None:
                toks.append(b.writer)
            toks.extend(b.readers)
        merged = {}
        for (s, v) in toks:
            if merged.get(s.name, (None, -1))[1] < v:
                merged[s.name] = (s, v)
        wm = self.mark[ename]
        return [(s, v) for nm, (s, v) in merged.items() if wm.get(nm, 0) < v]

    def _note(self, ename, s, v):
        wm = self.mark[ename]
        if wm.get(s.name, 0) < v:
            wm[s.name] = v

    def emit(self, ename, build, reads=(), writes=(), inc=True, track=True):
        needed = self._needed(ename, reads, writes)
        for (s, v) in needed:
            self._note(ename, s, v)
        tok = None
        if inc:
            self.cnt[ename] += 1
            tok = (self.sem[ename], self.cnt[ename])

        def thunk(eng, build=build, needed=needed, tok=tok):
            for (s, v) in needed[:-1]:
                eng.wait_ge(s, v)
            inst = build(eng)
            if needed:
                inst._wait_ge(*needed[-1])
            if tok is not None:
                inst.then_inc(tok[0], 1)
        self.q[ename].append(thunk)
        if tok is not None and track:
            for b in writes:
                b.writer = tok
                b.readers = []
            for b in reads:
                b.readers.append(tok)
        return tok

    def dma(self, qname, semname, out_ap, in_ap, reads=(), writes=()):
        needed = self._needed(qname, reads, writes)
        ent = self.dma_sem(semname)
        if ent[1] > 0 and self.mark[qname].get(ent[0].name, 0) < ent[1]:
            needed.append((ent[0], ent[1]))
        for (s, v) in needed:
            self._note(qname, s, v)
        ent[1] += 16
        tok = (ent[0], ent[1])

        def thunk(eng, needed=needed, tok=tok, out_ap=out_ap, in_ap=in_ap):
            for (s, v) in needed[:-1]:
                eng.wait_ge(s, v)
            inst = eng.dma_start(out_ap, in_ap)
            if needed:
                inst._wait_ge(*needed[-1])
            inst.then_inc(tok[0], 16)
        self.q[qname].append(thunk)
        for b in writes:
            b.writer = tok
            b.readers = []
        for b in reads:
            b.readers.append(tok)
        return tok

    def group(self, ename, pieces, reads_all, writes, inc=True):
        n = len(pieces)
        for i, (bf, rd) in enumerate(pieces):
            if n == 1:
                self.emit(ename, bf, reads=reads_all, writes=writes, inc=inc)
            elif i == 0:
                self.emit(ename, bf, reads=rd, writes=writes, inc=False, track=False)
            elif i == n - 1:
                self.emit(ename, bf, reads=reads_all, writes=writes, inc=inc)
            else:
                self.emit(ename, bf, reads=rd, writes=(), inc=False)

    def wait(self, ename, tok):
        if tok is None:
            return
        s, v = tok
        if self.mark[ename].get(s.name, 0) >= v:
            return
        self._note(ename, s, v)

        def thunk(eng, s=s, v=v):
            eng.wait_ge(s, v)
        self.q[ename].append(thunk)

    def barrier(self):
        toks = [(self.sem[e], self.cnt[e]) for e in ENGINES if self.cnt[e] > 0]
        toks += [(h, c) for (h, c) in self.dma_sems.values() if c > 0]
        for e in ENGINES:
            for t in toks:
                self.wait(e, t)

    def flush(self):
        nc = self.nc
        with nc.Block() as block:
            for ename in ENGINES:
                thunks = self.q[ename]
                if not thunks:
                    continue

                def section(eng, thunks=thunks):
                    for th in thunks:
                        th(eng)
                getattr(block, ename)(section)
        self.q = {e: [] for e in ENGINES}


def mask_slot(qt):
    return {0: 0, 1: 1, NQT - 2: 3, NQT - 1: 4}.get(qt, 2)


def s_pieces(qt):
    out = []
    j = 0
    while j < WW:
        slot0 = (qt + j // 128) % KRING
        run = 128
        while j + run < WW and (qt + (j + run) // 128) % KRING == slot0 + run // 128:
            run += 128
        if j < 512 < j + run:
            run = 512 - j
        out.append((j, j + run, slot0 * 128))
        j += run
    return out


def build_program():
    nc = bass.Bass("TRN2", target_bir_lowering=False)

    x_d = nc.dram_tensor("x", [HT, E], F32, kind="ExternalInput")
    wqt_d = nc.dram_tensor("wqt", [E, E], F32, kind="ExternalInput")
    wkt_d = nc.dram_tensor("wkt", [E, E], F32, kind="ExternalInput")
    wvt_d = nc.dram_tensor("wvt", [E, E], F32, kind="ExternalInput")
    w1_d = nc.dram_tensor("w1", [E, INTER], F32, kind="ExternalInput")
    w2_d = nc.dram_tensor("w2", [INTER, E], F32, kind="ExternalInput")
    bq_d = nc.dram_tensor("bq", [128, 8], F32, kind="ExternalInput")
    bk_d = nc.dram_tensor("bk", [128, 8], F32, kind="ExternalInput")
    bv_d = nc.dram_tensor("bv", [1, E], F32, kind="ExternalInput")
    b1_d = nc.dram_tensor("b1", [128, 32], F32, kind="ExternalInput")
    b2_d = nc.dram_tensor("b2", [128, 8], F32, kind="ExternalInput")
    masks_d = nc.dram_tensor("masks", [5, 128, WW], BF16, kind="ExternalInput")
    ident_d = nc.dram_tensor("ident", [128, 128], BF16, kind="ExternalInput")
    o_d = nc.dram_tensor("o", [TOK, E], F32, kind="ExternalOutput")
    d1_d = nc.dram_tensor("d1", [TOK, E], F32, kind="Internal")

    p = Prog(nc)

    ident_sb = nc.alloc_sbuf_tensor("ident_sb", [128, 128], BF16, side="left")
    bq_sb = nc.alloc_sbuf_tensor("bq_sb", [128, 8], F32, side="left")
    bk_sb = nc.alloc_sbuf_tensor("bk_sb", [128, 8], F32, side="left")
    bv_sb = nc.alloc_sbuf_tensor("bv_sb", [1, E], BF16, side="left")
    b1_sb = nc.alloc_sbuf_tensor("b1_sb", [128, 32], F32, side="left")
    b2_sb = nc.alloc_sbuf_tensor("b2_sb", [128, 8], F32, side="left")
    ones_sb = nc.alloc_sbuf_tensor("ones_sb", [1, 128], BF16, side="left")
    c18_sb = nc.alloc_sbuf_tensor("c18_sb", [128, 1], F32, side="left")
    cm05_sb = nc.alloc_sbuf_tensor("cm05_sb", [128, 1], F32, side="left")
    bqs_sb = nc.alloc_sbuf_tensor("bqs_sb", [128, 8], F32, side="left")
    st_sb = nc.alloc_sbuf_tensor("st_sb", [128, 24], F32, side="left")
    mv_sb = nc.alloc_sbuf_tensor("mv_sb", [128, 4], F32, side="left")
    aux_sb = nc.alloc_sbuf_tensor("aux_sb", [128, 8], F32, side="left")

    b_const = Buf("const")
    b_lnp = [Buf("ln0"), Buf("ln1")]
    ln_par = [0]

    p.dma("sync", "ci", ident_sb[:], ident_d[:, :], writes=[b_const])
    p.dma("sync", "ci", bq_sb[:], bq_d[:, :], writes=[b_const])
    p.dma("sync", "ci", bk_sb[:], bk_d[:, :], writes=[b_const])
    p.dma("sync", "ci", b1_sb[:], b1_d[:, :], writes=[b_const])
    p.dma("sync", "ci", b2_sb[:], b2_d[:, :], writes=[b_const])
    p.dma("gpsimd", "cg", bv_sb[:], bv_d[:, :], writes=[b_const])
    p.emit("vector", lambda e: e.memset(ones_sb[:], 1.0), reads=[b_const], writes=[b_const])
    p.emit("vector", lambda e: e.memset(c18_sb[:], 0.125), reads=[b_const], writes=[b_const])
    p.emit("vector", lambda e: e.memset(cm05_sb[:], -0.5), reads=[b_const], writes=[b_const])
    p.emit("vector", lambda e: e.tensor_scalar_mul(bqs_sb[:], bq_sb[:], 0.125),
           reads=[b_const], writes=[b_const])

    def ln_chain(src_ap, src_buf, dst_ap, dst_buf):
        par = ln_par[0]
        ln_par[0] ^= 1
        lb = b_lnp[par]
        st = st_sb[:, 12 * par:12 * par + 12]
        mv = mv_sb[:, 2 * par:2 * par + 2]
        vpe = aux_sb[:, 4 * par + 0:4 * par + 1]
        sd = aux_sb[:, 4 * par + 1:4 * par + 2]
        r = aux_sb[:, 4 * par + 2:4 * par + 3]
        nmr = aux_sb[:, 4 * par + 3:4 * par + 4]
        p.emit("vector", lambda e: e.bn_stats(st[:, 0:6], src_ap[:, 0:512]),
               reads=[src_buf], writes=[lb])
        p.emit("vector", lambda e: e.bn_stats(st[:, 6:12], src_ap[:, 512:1024]),
               reads=[src_buf, lb], writes=[lb])
        p.emit("vector", lambda e: e.bn_aggr(mv, st),
               reads=[lb], writes=[lb])
        p.emit("vector", lambda e: e.tensor_scalar_add(vpe, mv[:, 1:2], float(EPS)),
               reads=[lb], writes=[lb])
        p.emit("scalar", lambda e: e.activation(sd, vpe, ACT.Ln),
               reads=[lb], writes=[lb])
        p.emit("scalar", lambda e: e.activation(r, sd, ACT.Exp, scale=cm05_sb[:, 0:1]),
               reads=[lb, b_const], writes=[lb])
        p.emit("vector", lambda e: e.scalar_tensor_tensor(
            nmr, mv[:, 0:1], -1.0, r, op0=ALU.mult, op1=ALU.mult),
            reads=[lb], writes=[lb])
        p.emit("scalar", lambda e: e.activation(
            dst_ap, src_ap, ACT.Identity, bias=nmr, scale=r),
            reads=[lb, src_buf], writes=[dst_buf])

    stack1 = contextlib.ExitStack()

    def sb1(name, shape, dtype):
        return stack1.enter_context(nc.sbuf_tensor(name, shape, dtype, side="right"))

    def ps1(name, shape):
        return stack1.enter_context(nc.psum_tensor(name, shape, F32))

    wq_sb = sb1("wq_sb", [128, NCH * E], BF16)
    wk_sb = sb1("wk_sb", [128, NCH * E], BF16)
    wv_sb = sb1("wv_sb", [128, NCH * E], BF16)
    masks_sb = sb1("masks_sb", [128, 5 * WW], BF16)
    x_sb = sb1("x_sb", [128, XRING * E], F32)
    xn_sb = sb1("xn_sb", [128, 2 * E], BF16)
    xnT_sb = sb1("xnT_sb", [128, NCH * 512], BF16)
    qT_sb = sb1("qT_sb", [128, NCH * QRING * 128], BF16)
    kT_sb = sb1("kT_sb", [128, NCH * KRING * 128], BF16)
    v_sb = sb1("v_sb", [128, KRING * H * 65], BF16)
    pe_sb = [sb1(f"pe_sb{i}", [128, WW], BF16) for i in range(2)]
    ph_sb = [sb1(f"ph_sb{i}", [128, WW], BF16) for i in range(4)]
    pt_sb = [sb1(f"pt_sb{i}", [128, WW], BF16) for i in range(2)]
    cpair_sb = sb1("cpair_sb", [128, 256], BF16)
    negm_sb = sb1("negm_sb", [128, 2], F32)
    rl_sb = sb1("rl_sb", [128, H], F32)
    ctx_sb = sb1("ctx_sb", [128, 2 * E], F32)

    ps_pj0 = ps1("ps_pj0", [128, 512])
    ps_s = [ps1(f"ps_s{i}", [128, WW]) for i in range(2)]
    ps_ptA = ps1("ps_ptA", [128, 512])
    ps_m = [ps1(f"ps_m{i}", [128, 512]) for i in range(2)]

    b_w = Buf("w")
    b_masks = Buf("masks")
    b_x = [Buf(f"x{i}") for i in range(XRING)]
    b_xn = [Buf(f"xn{i}") for i in range(2)]
    b_xnT = Buf("xnT")
    b_q = [Buf(f"q{i}") for i in range(QRING)]
    b_k = [Buf(f"k{i}") for i in range(KRING)]
    b_v = [Buf(f"v{i}") for i in range(KRING)]
    b_pj = [Buf("pj0"), Buf("pj1")]
    b_s = [Buf("s0"), Buf("s1")]
    b_ptA = Buf("ptA")
    b_m = [Buf("m0"), Buf("m1")]
    b_pe = [Buf("pe0"), Buf("pe1")]
    b_ph = [Buf(f"ph{i}") for i in range(4)]
    b_pt = [Buf("pt0"), Buf("pt1")]
    b_cp = [Buf("cp0"), Buf("cp1")]
    b_negm = [Buf("negm0"), Buf("negm1")]
    b_rl = Buf("rl")
    b_ctx = [Buf("ctx0"), Buf("ctx1")]
    b_d1 = [Buf(f"d1_{i}") for i in range(NQT)]

    p.dma("gpsimd", "cg", wq_sb[:].rearrange("p (c e) -> p c e", c=NCH),
          wqt_d[:, :].rearrange("(c p) e -> p c e", p=128), writes=[b_w])
    p.dma("gpsimd", "cg", wk_sb[:].rearrange("p (c e) -> p c e", c=NCH),
          wkt_d[:, :].rearrange("(c p) e -> p c e", p=128), writes=[b_w])
    p.dma("gpsimd", "cg", wv_sb[:].rearrange("p (c e) -> p c e", c=NCH),
          wvt_d[:, :].rearrange("(c p) e -> p c e", p=128), writes=[b_w])
    p.dma("sync", "ci", masks_sb[:].rearrange("p (m w) -> p m w", m=5),
          masks_d[:, :, :].rearrange("m p w -> p m w"), writes=[b_masks])

    for slot in range(KRING):
        p.emit("vector", lambda e, slot=slot: e.memset(
            v_sb[:, slot * H * 65:(slot + 1) * H * 65]
            .rearrange("p (h d) -> p h d", d=65)[:, :, 64:65], 1.0),
            reads=[b_const], writes=[b_v[slot]])

    def next_pj():
        return (ps_pj0, b_pj[0])

    def tile_ln_t1(t):
        xs = x_sb[:, (t % XRING) * E:(t % XRING + 1) * E]
        xn = xn_sb[:, (t % 2) * E:(t % 2 + 1) * E]
        ln_chain(xs, b_x[t % XRING], xn, b_xn[t % 2])
        for hh in range(2):
            pst, pbb = next_pj()
            pieces = []
            for c in range(4):
                cc = hh * 4 + c
                pieces.append((lambda e, xn=xn, pst=pst, c=c, cc=cc: e.matmul(
                    pst[:, c * 128:(c + 1) * 128],
                    xn[:, cc * 128:(cc + 1) * 128], ident_sb[:],
                    start=True, stop=True), [b_xn[t % 2], b_const]))
            p.group("tensor", pieces, [b_xn[t % 2], b_const], [pbb])
            p.emit("scalar", lambda e, hh=hh, t=t, pst=pst: e.copy(
                xnT_sb[:].rearrange("p (c t) -> p c t", c=NCH)[:, hh * 4:(hh + 1) * 4, (t % 4) * 128:(t % 4 + 1) * 128],
                pst[:].rearrange("p (c t) -> p c t", c=4)),
                reads=[pbb], writes=[b_xnT])

    def group_proj_units(g):
        us = []
        t0 = 4 * g
        qts = [t - 2 for t in range(t0, t0 + 4) if 2 <= t < 2 + NQT]
        kc0 = (t0 % KRING) * 128
        def k_chunk(c):
            pst, pbb = next_pj()
            pieces = []
            for e8 in range(NCH):
                pieces.append((lambda e, c=c, e8=e8, pst=pst: e.matmul(
                    pst[:, 0:512],
                    wk_sb[:, e8 * E + c * 128:e8 * E + (c + 1) * 128],
                    xnT_sb[:, e8 * 512:(e8 + 1) * 512],
                    start=(e8 == 0), stop=(e8 == NCH - 1)),
                    [b_w, b_xnT]))
            kbufs = [b_k[(t0 + i) % KRING] for i in range(4)]
            p.group("tensor", pieces, [b_w, b_xnT], [pbb])

            def ev():
                p.emit("scalar", lambda e, c=c, kc0=kc0, pst=pst: e.activation(
                    kT_sb[:, c * (KRING * 128) + kc0:c * (KRING * 128) + kc0 + 512],
                    pst[:, 0:512], ACT.Identity, bias=bk_sb[:, c:c + 1]),
                    reads=[pbb, b_const], writes=kbufs)
            return ev
        def q_chunk(c):
            if not qts:
                return None
            if True:
                qn = len(qts) * 128
                xoff = ((qts[0] + 2) % 4) * 128
                qc0 = ((qts[0] + 2) % QRING) * 128
                pst, pbb = next_pj()
                pieces = []
                for e8 in range(NCH):
                    pieces.append((lambda e, c=c, e8=e8, xoff=xoff, qn=qn, pst=pst: e.matmul(
                        pst[:, 0:qn],
                        wq_sb[:, e8 * E + c * 128:e8 * E + (c + 1) * 128],
                        xnT_sb[:, e8 * 512 + xoff:e8 * 512 + xoff + qn],
                        start=(e8 == 0), stop=(e8 == NCH - 1)),
                        [b_w, b_xnT]))
                qbufs = [b_q[(qt + 2) % QRING] for qt in qts]
                p.group("tensor", pieces, [b_w, b_xnT], [pbb])

                def ev():
                    p.emit("scalar", lambda e, c=c, qc0=qc0, qn=qn, pst=pst: e.activation(
                        qT_sb[:, c * (QRING * 128) + qc0:c * (QRING * 128) + qc0 + qn],
                        pst[:, 0:qn], ACT.Identity, bias=bqs_sb[:, c:c + 1],
                        scale=c18_sb[:, 0:1]),
                        reads=[pbb, b_const], writes=qbufs)
                return ev
        def v_half(i, half):
            slot = (t0 + i) % KRING
            if True:
                pst, pbb = next_pj()
                pieces = [(lambda e, half=half, pst=pst: e.matmul(
                    pst[:, 0:512], ones_sb[:, 0:128],
                    bv_sb[:, half * 512:(half + 1) * 512],
                    start=True, stop=False), [b_const])]
                for e8 in range(NCH):
                    pieces.append((lambda e, i=i, half=half, e8=e8, pst=pst: e.matmul(
                        pst[:, 0:512],
                        xnT_sb[:, e8 * 512 + i * 128:e8 * 512 + (i + 1) * 128],
                        wv_sb[:, e8 * E + half * 512:e8 * E + (half + 1) * 512],
                        start=False, stop=(e8 == NCH - 1)),
                        [b_w, b_xnT, b_const]))
                p.group("tensor", pieces, [b_w, b_xnT, b_const], [pbb])

                def vev(e, slot=slot, half=half, pst=pst):
                    base = slot * H * 65 + half * 8 * 65
                    return e.copy(
                        v_sb[:, base:base + 8 * 65].rearrange(
                            "p (h d) -> p h d", d=65)[:, :, 0:64],
                        pst[:, 0:512].rearrange("p (h d) -> p h d", d=64))

                def ev():
                    p.emit("scalar", vev, reads=[pbb], writes=[b_v[slot]])
                return ev

        for c in range(NCH):
            us.append(lambda c=c: k_chunk(c))
            us.append(lambda c=c: q_chunk(c))
        for i in range(4):
            for half in range(2):
                us.append(lambda i=i, half=half: v_half(i, half))
        stag = {"pend": None}
        out = []

        def wrap(u):
            def run(u=u):
                if stag["pend"] is not None:
                    stag["pend"]()
                    stag["pend"] = None
                r = u()
                stag["pend"] = r if callable(r) else None
            return run

        def final():
            if stag["pend"] is not None:
                stag["pend"]()
                stag["pend"] = None
        return [wrap(u) for u in us] + [final]

    def attn_stage_A(qt, h):
        qslot = (qt + 2) % QRING
        wbufs = [b_k[(qt + i) % KRING] for i in range(5)]
        pcs = s_pieces(qt)
        j = h // 2
        lo = (h % 2) * 64
        pi = h % 2
        ps = ps_s[pi]
        pieces = []
        for (w0, w1, rc0) in pcs:
            pieces.append((lambda e, ps=ps, lo=lo, j=j, w0=w0, w1=w1, rc0=rc0, qslot=qslot: e.matmul(
                ps[:, w0:w1],
                qT_sb[lo:lo + 64, j * (QRING * 128) + qslot * 128:
                      j * (QRING * 128) + (qslot + 1) * 128],
                kT_sb[lo:lo + 64, j * (KRING * 128) + rc0:
                      j * (KRING * 128) + rc0 + (w1 - w0)],
                start=True, stop=True),
                [b_q[qslot]] + wbufs))
        p.group("tensor", pieces, [b_q[qslot]] + wbufs, [b_s[pi]])

    def attn_stage_Bmax(qt, h):
        pi = h % 2
        ps = ps_s[pi]
        nm = negm_sb[:, pi:pi + 1]
        p.emit("vector", lambda e, ps=ps, nm=nm: e.reduce_max(
            nm, ps[:, 0:WW], axis=AX, negate=True),
            reads=[b_s[pi]], writes=[b_negm[pi]])

    def attn_stage_Bexp(qt, h):
        ms = mask_slot(qt)
        mask = masks_sb[:, ms * WW:(ms + 1) * WW]
        pi = h % 2
        ps = ps_s[pi]
        nm = negm_sb[:, pi:pi + 1]
        pe = pe_sb[pi]
        ph = ph_sb[h % 4]
        p.emit("scalar", lambda e, ps=ps, pe=pe, nm=nm: e.activation(
            pe[:, 0:WW], ps[:, 0:WW], ACT.Exp, bias=nm),
            reads=[b_s[pi], b_negm[pi]], writes=[b_pe[pi]])
        p.emit("gpsimd", lambda e, pe=pe, ph=ph, mask=mask: e.tensor_tensor(
            ph[:, 0:WW], pe[:, 0:WW], mask[:, 0:WW], op=ALU.mult),
            reads=[b_pe[pi], b_masks], writes=[b_ph[h % 4]])

    def attn_stage_C(qt, h):
        pi = h % 2
        ph = ph_sb[h % 4]
        pt = pt_sb[pi]
        pieces = []
        for c in range(4):
            pieces.append((lambda e, ph=ph, c=c: e.matmul(
                ps_ptA[:, c * 128:(c + 1) * 128],
                ph[:, c * 128:(c + 1) * 128], ident_sb[:],
                start=True, stop=True),
                [b_ph[h % 4], b_const]))
        p.group("tensor", pieces, [b_ph[h % 4], b_const], [b_ptA])
        p.emit("tensor", lambda e, ph=ph, pi=pi: e.matmul(
            ps_m[pi][:, 0:128], ph[:, 512:640], ident_sb[:],
            start=True, stop=True),
            reads=[b_ph[h % 4], b_const], writes=[b_m[pi]])
        p.emit("scalar", lambda e, pt=pt: e.copy(pt[:, 0:512], ps_ptA[:, 0:512]),
               reads=[b_ptA], writes=[b_pt[pi]])
        p.emit("scalar", lambda e, pt=pt, pi=pi: e.copy(pt[:, 512:WW], ps_m[pi][:, 0:128]),
               reads=[b_m[pi]], writes=[b_pt[pi]])

    def attn_stage_D(qt, h):
        pi = h % 2
        pt = pt_sb[pi]
        vbufs = [b_v[(qt + i) % KRING] for i in range(5)]
        pieces = []
        for c in range(5):
            slot = (qt + c) % KRING
            pieces.append((lambda e, pt=pt, c=c, h=h, slot=slot, pi=pi: e.matmul(
                ps_m[pi][0:65, 128:256],
                v_sb[:, slot * H * 65 + h * 65:slot * H * 65 + (h + 1) * 65],
                pt[:, c * 128:(c + 1) * 128],
                start=(c == 0), stop=(c == 4)),
                [b_pt[pi]] + vbufs))
        p.group("tensor", pieces, [b_pt[pi]] + vbufs, [b_m[pi]])

    def attn_stage_Et(qt, h):
        pi = h % 2
        cp = cpair_sb[0:65, pi * 128:(pi + 1) * 128]
        p.emit("vector", lambda e, cp=cp, pi=pi: e.tensor_copy(cp, ps_m[pi][0:65, 128:256]),
               reads=[b_m[pi]], writes=[b_cp[pi]])
        p.emit("tensor", lambda e, cp=cp, pi=pi: e.matmul(
            ps_m[pi][:, 256:321], cp, ident_sb[0:65, 0:65], start=True, stop=True),
            reads=[b_cp[pi], b_const], writes=[b_m[pi]])

    def attn_stage_Ev(qt, h):
        pi = h % 2
        p.emit("vector", lambda e, h=h, pi=pi: e.reciprocal(
            rl_sb[:, h:h + 1], ps_m[pi][:, 320:321]),
            reads=[b_m[pi]], writes=[b_rl])
        p.emit("vector", lambda e, h=h, qt=qt, pi=pi: e.tensor_scalar(
            ctx_sb[:, (qt % 2) * E + h * 64:(qt % 2) * E + (h + 1) * 64],
            ps_m[pi][:, 256:320],
            rl_sb[:, h:h + 1], None, op0=ALU.mult),
            reads=[b_m[pi], b_rl], writes=[b_ctx[qt % 2]])

    def attn_finish(qt):
        cs = ctx_sb[:, (qt % 2) * E:(qt % 2 + 1) * E]
        p.dma("gpsimd", "d1w", d1_d[qt * 128:(qt + 1) * 128, :], cs,
              reads=[b_ctx[qt % 2]], writes=[b_d1[qt]])

    def x_dma(t):
        p.dma("sync", f"x{t % XRING}", x_sb[:, (t % XRING) * E:(t % XRING + 1) * E],
              x_d[t * 128:(t + 1) * 128, :], writes=[b_x[t % XRING]])

    def proj_units(g):
        us = [(lambda t=t: tile_ln_t1(t)) for t in range(4 * g, 4 * g + 4)]
        us += group_proj_units(g)
        return us

    cell_state = {"cells": [], "i": 0, "a": 0, "mx": 0, "em": 0}

    def _adv_a(limit):
        cs = cell_state["cells"]
        while cell_state["a"] <= limit and cell_state["a"] < len(cs):
            attn_stage_A(*cs[cell_state["a"]])
            cell_state["a"] += 1

    def _adv_mx(limit):
        cs = cell_state["cells"]
        while cell_state["mx"] <= limit and cell_state["mx"] < len(cs):
            _adv_a(cell_state["mx"])
            attn_stage_Bmax(*cs[cell_state["mx"]])
            cell_state["mx"] += 1

    def _adv_em(limit):
        cs = cell_state["cells"]
        while cell_state["em"] <= limit and cell_state["em"] < len(cs):
            _adv_mx(cell_state["em"])
            attn_stage_Bexp(*cs[cell_state["em"]])
            cell_state["em"] += 1

    def cell_unit():
        cs = cell_state["cells"]
        i = cell_state["i"]
        _adv_em(i + 2)
        if i >= 2:
            attn_stage_Ev(*cs[i - 2])
            if cs[i - 2][1] == H - 1:
                attn_finish(cs[i - 2][0])
        attn_stage_C(*cs[i])
        _adv_mx(i + 3)
        _adv_a(i + 4)
        if i >= 1:
            attn_stage_D(*cs[i - 1])
            attn_stage_Et(*cs[i - 1])
        cell_state["i"] = i + 1

    def cell_push_and_emit(new_cells, tu):
        cell_state["cells"].extend(new_cells)
        na, nb = len(new_cells), len(tu)
        ia = ib = 0
        while ia < na or ib < nb:
            if ia < na:
                cell_unit()
                ia += 1
            while ib < nb and (na == 0 or ib * na <= ia * nb):
                tu[ib]()
                ib += 1

    def cell_drain():
        cs = cell_state["cells"]
        n = len(cs)

        def ev(j):
            attn_stage_Ev(*cs[j])
            if cs[j][1] == H - 1:
                attn_finish(cs[j][0])
        ev(n - 2)
        attn_stage_D(*cs[n - 1])
        attn_stage_Et(*cs[n - 1])
        ev(n - 1)

    for t in range(0, 4):
        x_dma(t)
    for u in proj_units(0):
        u()
    for g in range(NG):
        new_cells = []
        for qt in range(max(0, 4 * g - 4), min(NQT, 4 * g)):
            for h in range(H):
                new_cells.append((qt, h))
        tu = []
        if g + 1 < NG:
            for t in range(4 * (g + 1), 4 * (g + 1) + 4):
                x_dma(t)
            tu = proj_units(g + 1)
        cell_push_and_emit(new_cells, tu)
    cell_drain()

    p.flush()
    stack1.close()

    p.barrier()

    w1r_sb = nc.alloc_sbuf_tensor("w1r_sb", [128, W1RING * NCH * 512], BF16, side="right")
    w2r_sb = nc.alloc_sbuf_tensor("w2r_sb", [128, W2RING * 32 * 128], BF16, side="right")
    x2_sb = nc.alloc_sbuf_tensor("x2_sb", [128, X2RING * E], F32, side="right")
    d1l_sb = nc.alloc_sbuf_tensor("d1l_sb", [128, D1RING * E], F32, side="right")
    xn2_sb = nc.alloc_sbuf_tensor("xn2_sb", [128, 2 * E], BF16, side="right")
    xn2T_sb = nc.alloc_sbuf_tensor("xn2T_sb", [128, NCH * ST_TOK], BF16, side="right")
    h_sb = nc.alloc_sbuf_tensor("h_sb", [128, 32 * ST_TOK], BF16, side="right")
    o2_sb = nc.alloc_sbuf_tensor("o2_sb", [128, NCH * ST_TOK], BF16, side="right")
    ffn_sb = nc.alloc_sbuf_tensor("ffn_sb", [128, 2 * E], F32, side="right")
    ob_sb = nc.alloc_sbuf_tensor("ob_sb", [128, 2 * E], F32, side="right")

    ps_t4a = nc.alloc_psum_tensor("ps_t4a", [128, 512], F32)
    ps_t4b = nc.alloc_psum_tensor("ps_t4b", [128, 512], F32)
    ps_h0 = nc.alloc_psum_tensor("ps_h0", [128, 1024], F32)
    ps_h1 = nc.alloc_psum_tensor("ps_h1", [128, 1024], F32)
    ps_o2 = nc.alloc_psum_tensor("ps_o2", [128, 1024], F32)

    b_w1 = [Buf(f"w1_{i}") for i in range(W1RING)]
    b_w2 = [Buf(f"w2_{i}") for i in range(W2RING)]
    b_x2 = [Buf(f"x2_{i}") for i in range(X2RING)]
    b_d1l = [Buf(f"d1l_{i}") for i in range(D1RING)]
    b_xn2 = [Buf("xn2_0"), Buf("xn2_1")]
    b_xn2T = [Buf("xn2T0"), Buf("xn2T1")]
    b_h = Buf("h")
    b_o2 = Buf("o2")
    b_ffn = [Buf("ffn0"), Buf("ffn1")]
    b_ob = [Buf("ob0"), Buf("ob1")]
    b_pt4 = [Buf("pt4a"), Buf("pt4b")]
    b_h0 = [Buf("h0a"), Buf("h0b")]
    b_h1 = [Buf("h1a"), Buf("h1b")]
    b_po2 = Buf("po2")

    def dma_w1(s, k4):
        slot = (s * 8 + k4) % W1RING
        ci0 = k4 * 4
        p.dma("gpsimd", f"w1_{slot}",
              w1r_sb[:, slot * NCH * 512:(slot + 1) * NCH * 512].rearrange("p (c i) -> p c i", c=NCH),
              w1_d[:, ci0 * 128:(ci0 + 4) * 128].rearrange("(c p) i -> p c i", p=128),
              writes=[b_w1[slot]])

    def dma_w2(s, eo):
        slot = (s * 8 + eo) % W2RING
        p.dma("gpsimd", f"w2_{slot}",
              w2r_sb[:, slot * 32 * 128:(slot + 1) * 32 * 128].rearrange("p (c e) -> p c e", c=32),
              w2_d[:, eo * 128:(eo + 1) * 128].rearrange("(c p) e -> p c e", p=128),
              writes=[b_w2[slot]])

    def tile_unit_a(s, tt):
        ti = s * STT + tt
        xsl = ti % X2RING
        dsl = ti % D1RING
        p.dma("sync", f"x2_{xsl}", x2_sb[:, xsl * E:(xsl + 1) * E],
              x_d[256 + ti * 128:256 + (ti + 1) * 128, :], writes=[b_x2[xsl]])
        p.dma("sync", f"d1l_{dsl}", d1l_sb[:, dsl * E:(dsl + 1) * E],
              d1_d[ti * 128:(ti + 1) * 128, :],
              reads=[b_d1[ti]], writes=[b_d1l[dsl]])
        dsv = d1l_sb[:, dsl * E:(dsl + 1) * E]
        ln_chain(dsv, b_d1l[dsl], dsv, b_d1l[dsl])

    def tile_unit(s, tt):
        ti = s * STT + tt
        xsl = ti % X2RING
        dsl = ti % D1RING
        dsv = d1l_sb[:, dsl * E:(dsl + 1) * E]
        p.emit("vector", lambda e, xsl=xsl, dsv=dsv: e.tensor_tensor(
            x2_sb[:, xsl * E:(xsl + 1) * E], x2_sb[:, xsl * E:(xsl + 1) * E],
            dsv, op=ALU.add),
            reads=[b_x2[xsl], b_d1l[dsl]], writes=[b_x2[xsl]])
        xn2 = xn2_sb[:, (tt % 2) * E:(tt % 2 + 1) * E]
        ln_chain(x2_sb[:, xsl * E:(xsl + 1) * E], b_x2[xsl], xn2, b_xn2[tt % 2])
        for hh in range(2):
            pst = ps_t4a if hh == 0 else ps_t4b
            pbb = b_pt4[hh]
            pieces = []
            for c in range(4):
                cc = hh * 4 + c
                pieces.append((lambda e, xn2=xn2, pst=pst, c=c, cc=cc: e.matmul(
                    pst[:, c * 128:(c + 1) * 128],
                    xn2[:, cc * 128:(cc + 1) * 128], ident_sb[:],
                    start=True, stop=True), [b_xn2[tt % 2], b_const]))
            p.group("tensor", pieces, [b_xn2[tt % 2], b_const], [pbb])
            p.emit("scalar", lambda e, hh=hh, tt=tt, pst=pst: e.copy(
                xn2T_sb[:].rearrange("p (c t) -> p c t", c=NCH)[:, hh * 4:(hh + 1) * 4, tt * 128:(tt + 1) * 128],
                pst[:].rearrange("p (c t) -> p c t", c=4)),
                reads=[pbb], writes=[b_xn2T[tt // 4]])

    def w1_unit(s, ci, half):
        if True:
            slot = (s * 8 + ci // 4) % W1RING
            wbase = slot * NCH * 512 + (ci % 4) * 128
            psh = ps_h0 if half == 0 else ps_h1
            reg = psh[:, (ci % 2) * 512:(ci % 2 + 1) * 512]
            pbh = (b_h0 if half == 0 else b_h1)[ci % 2]
            pieces = []
            for e8 in range(NCH):
                pieces.append((lambda e, e8=e8, half=half, reg=reg, wbase=wbase: e.matmul(
                    reg,
                    w1r_sb[:, wbase + e8 * 512:wbase + e8 * 512 + 128],
                    xn2T_sb[:, e8 * ST_TOK + half * 512:e8 * ST_TOK + (half + 1) * 512],
                    start=(e8 == 0), stop=(e8 == NCH - 1)),
                    [b_w1[slot], b_xn2T[half]]))
            p.group("tensor", pieces, [b_w1[slot], b_xn2T[half]], [pbh])
            p.emit("vector", lambda e, ci=ci, half=half, reg=reg: e.tensor_scalar(
                h_sb[:, ci * ST_TOK + half * 512:ci * ST_TOK + (half + 1) * 512],
                reg, b1_sb[:, ci:ci + 1], 0.0, op0=ALU.add, op1=ALU.max),
                reads=[pbh, b_const], writes=[b_h])
            if half == 1 and ci % 4 == 3:
                k4n = ci // 4 + W1RING
                if k4n < 8:
                    dma_w1(s, k4n)
                elif s + 1 < NST:
                    dma_w1(s + 1, k4n - 8)
                if ci // 4 < W2RING:
                    dma_w2(s, ci // 4)
    def w2_unit(s, eo):
        if True:
            slot = (s * 8 + eo) % W2RING
            pieces = []
            for ic in range(32):
                for half in range(2):
                    pieces.append((lambda e, ic=ic, half=half, slot=slot: e.matmul(
                        ps_o2[:, half * 512:(half + 1) * 512],
                        w2r_sb[:, slot * 32 * 128 + ic * 128:slot * 32 * 128 + (ic + 1) * 128],
                        h_sb[:, ic * ST_TOK + half * 512:ic * ST_TOK + (half + 1) * 512],
                        start=(ic == 0), stop=(ic == 31)),
                        [b_w2[slot], b_h]))
            p.group("tensor", pieces, [b_w2[slot], b_h], [b_po2])
            p.emit("scalar", lambda e, eo=eo: e.activation(
                o2_sb[:, eo * ST_TOK:(eo + 1) * ST_TOK], ps_o2[:], ACT.Identity,
                bias=b2_sb[:, eo:eo + 1]),
                reads=[b_po2, b_const], writes=[b_o2])
            if eo + W2RING < 8:
                dma_w2(s, eo + W2RING)
    def t5_unit(s, tt):
        if True:
            ti = s * STT + tt
            xsl = ti % X2RING
            pieces = []
            for eo in range(NCH):
                pieces.append((lambda e, eo=eo, tt=tt: e.matmul(
                    ps_o2[:, eo * 128:(eo + 1) * 128],
                    o2_sb[:, eo * ST_TOK + tt * 128:eo * ST_TOK + (tt + 1) * 128],
                    ident_sb[:], start=True, stop=True),
                    [b_o2, b_const]))
            p.group("tensor", pieces, [b_o2, b_const], [b_po2])
            fs = ffn_sb[:, (tt % 2) * E:(tt % 2 + 1) * E]
            p.emit("scalar", lambda e, fs=fs: e.copy(fs, ps_o2[:]),
                   reads=[b_po2], writes=[b_ffn[tt % 2]])
            ob = ob_sb[:, (tt % 2) * E:(tt % 2 + 1) * E]
            ln_chain(fs, b_ffn[tt % 2], ob, b_ob[tt % 2])
            p.emit("vector", lambda e, ob=ob, xsl=xsl: e.tensor_tensor(
                ob, ob, x2_sb[:, xsl * E:(xsl + 1) * E], op=ALU.add),
                reads=[b_ob[tt % 2], b_x2[xsl]], writes=[b_ob[tt % 2]])
            p.dma("sync", "out", o_d[ti * 128:(ti + 1) * 128, :], ob,
                  reads=[b_ob[tt % 2]])

    def interleave(ua, ub):
        na, nb = len(ua), len(ub)
        ia = ib = 0
        while ia < na or ib < nb:
            if ia < na:
                ua[ia]()
                ia += 1
            while ib < nb and (na == 0 or ib * na <= ia * nb):
                ub[ib]()
                ib += 1

    def tiles_and_t5_units(s):
        front, back = [], []
        seq = front
        if s > 0:
            for i in range(3):
                seq.append(lambda s=s, i=i: t5_unit(s - 1, i))
        seq.append(lambda s=s: tile_unit_a(s, 0))
        for tt in range(STT):
            if tt + 1 < STT:
                seq.append(lambda s=s, tt=tt: tile_unit_a(s, tt + 1))
            seq.append(lambda s=s, tt=tt: tile_unit(s, tt))
            if tt == 3:
                seq = back
            if s > 0 and tt + 3 < STT:
                seq.append(lambda s=s, tt=tt: t5_unit(s - 1, tt + 3))
        return front, back

    EARLY_CI = 12
    for k4 in range(W1RING):
        dma_w1(0, k4)
    for s in range(NST):
        front, back = tiles_and_t5_units(s)
        for u in front:
            u()
        interleave([lambda s=s, ci=ci: w1_unit(s, ci, 0)
                    for ci in range(EARLY_CI)], back)
        for ci in range(32):
            if ci >= EARLY_CI:
                w1_unit(s, ci, 0)
            w1_unit(s, ci, 1)
        for eo in range(8):
            w2_unit(s, eo)
    for tt in range(STT):
        t5_unit(NST - 1, tt)

    sem, cntv = p.dma_sems["out"]
    p.wait("sync", (sem, cntv))
    p.flush()
    return nc


_NC_CACHE = {}


def _get_nc():
    if "nc" not in _NC_CACHE:
        _NC_CACHE["nc"] = build_program()
    return _NC_CACHE["nc"]


def _make_masks(h):
    i = np.arange(128)[:, None]
    out = np.zeros((5, 128, WW), np.float32)
    for si, qt in enumerate((0, 1, 15, NQT - 2, NQT - 1)):
        j = np.arange(WW)[None, :]
        band = np.abs(j - 256 - i) <= 256
        g = h * 4096 + qt * 128 - 256 + j
        valid = (g >= 0) & (g < SEQ)
        out[si] = (band & valid).astype(np.float32)
    return out.astype(ml_dtypes.bfloat16)


def kernel(L0_attn_hidden_states, L0_attn_Wq, L0_attn_bq, L0_attn_Wk, L0_attn_bk,
           L0_attn_Wv, L0_attn_bv, L0_ffn_W1, L0_ffn_b1, L0_ffn_W2, L0_ffn_b2):
    x = np.ascontiguousarray(np.asarray(L0_attn_hidden_states, dtype=np.float32))
    wq = np.asarray(L0_attn_Wq, np.float32)
    wk = np.asarray(L0_attn_Wk, np.float32)
    wv = np.asarray(L0_attn_Wv, np.float32)
    w1 = np.ascontiguousarray(np.asarray(L0_ffn_W1, np.float32))
    w2 = np.ascontiguousarray(np.asarray(L0_ffn_W2, np.float32))
    bq = np.asarray(L0_attn_bq, np.float32)
    bk = np.asarray(L0_attn_bk, np.float32)
    bv = np.asarray(L0_attn_bv, np.float32)
    b1 = np.asarray(L0_ffn_b1, np.float32)
    b2 = np.asarray(L0_ffn_b2, np.float32)

    wqt = np.ascontiguousarray(wq.T)
    wkt = np.ascontiguousarray(wk.T)
    wvt = np.ascontiguousarray(wv.T)
    bq_l = np.ascontiguousarray(bq.reshape(8, 128).T)
    bk_l = np.ascontiguousarray(bk.reshape(8, 128).T)
    bv_l = np.ascontiguousarray(bv.reshape(1, E))
    b1_l = np.ascontiguousarray(b1.reshape(32, 128).T)
    b2_l = np.ascontiguousarray(b2.reshape(8, 128).T)
    ident = np.eye(128, dtype=ml_dtypes.bfloat16)
    masks_h = [_make_masks(0), _make_masks(1)]

    in_maps = []
    for c in range(8):
        b, h = c // 2, c % 2
        xs = np.zeros((HT, E), np.float32)
        if h == 0:
            xs[256:] = x[b, 0:HT - 256]
        else:
            xs[:HT - 256] = x[b, SEQ - (HT - 256):SEQ]
        in_maps.append({
            "x": xs, "wqt": wqt, "wkt": wkt, "wvt": wvt, "w1": w1, "w2": w2,
            "bq": bq_l, "bk": bk_l, "bv": bv_l, "b1": b1_l, "b2": b2_l,
            "masks": masks_h[h], "ident": ident,
        })

    nc = _get_nc()
    res = run_bass_kernel_spmd(nc, in_maps, core_ids=list(range(8)))
    out = np.empty((4, SEQ, E), np.float32)
    for c in range(8):
        b, h = c // 2, c % 2
        out[b, h * 4096:(h + 1) * 4096] = np.asarray(res.results[c]["o"], np.float32)
    return out


# version 22
# speedup vs baseline: 1.0981x; 1.0142x over previous
import contextlib
import numpy as np
import ml_dtypes
import concourse.bass as bass
import concourse.mybir as mybir
from concourse.bass_utils import run_bass_kernel_spmd

F32 = mybir.dt.float32
BF16 = mybir.dt.bfloat16
AX = mybir.AxisListType.X
ALU = mybir.AluOpType
ACT = mybir.ActivationFunctionType

ENGINES = ("tensor", "vector", "scalar", "sync", "gpsimd")

E = 1024
H = 16
D = 64
NCH = 8
SEQ = 8192
TOK = 4096
HT = TOK + 512
NKT = HT // 128
NQT = TOK // 128
NG = NKT // 4
WW = 640
EPS = 1e-5
INTER = 4096
ST_TOK = 1024
NST = TOK // ST_TOK
STT = ST_TOK // 128

KRING = 12
QRING = 12
XRING = 6
X2RING = 10
D1RING = 2
W1RING = 3
W2RING = 2


class Buf:
    __slots__ = ("writer", "readers", "name")

    def __init__(self, name=""):
        self.writer = None
        self.readers = []
        self.name = name


class Prog:
    def __init__(self, nc):
        self.nc = nc
        self.sem = {}
        self.cnt = {}
        self.q = {e: [] for e in ENGINES}
        for e in ENGINES:
            self.sem[e] = nc.alloc_semaphore(f"sem_{e}")
            self.cnt[e] = 0
        self.mark = {e: {} for e in ENGINES}
        self.dma_sems = {}

    def dma_sem(self, name):
        if name not in self.dma_sems:
            self.dma_sems[name] = [self.nc.alloc_semaphore(f"dsem_{name}"), 0]
        return self.dma_sems[name]

    def _needed(self, ename, reads, writes):
        toks = []
        for b in reads:
            if b.writer is not None:
                toks.append(b.writer)
        for b in writes:
            if b.writer is not None:
                toks.append(b.writer)
            toks.extend(b.readers)
        merged = {}
        for (s, v) in toks:
            if merged.get(s.name, (None, -1))[1] < v:
                merged[s.name] = (s, v)
        wm = self.mark[ename]
        return [(s, v) for nm, (s, v) in merged.items() if wm.get(nm, 0) < v]

    def _note(self, ename, s, v):
        wm = self.mark[ename]
        if wm.get(s.name, 0) < v:
            wm[s.name] = v

    def emit(self, ename, build, reads=(), writes=(), inc=True, track=True):
        needed = self._needed(ename, reads, writes)
        for (s, v) in needed:
            self._note(ename, s, v)
        tok = None
        if inc:
            self.cnt[ename] += 1
            tok = (self.sem[ename], self.cnt[ename])

        def thunk(eng, build=build, needed=needed, tok=tok):
            for (s, v) in needed[:-1]:
                eng.wait_ge(s, v)
            inst = build(eng)
            if needed:
                inst._wait_ge(*needed[-1])
            if tok is not None:
                inst.then_inc(tok[0], 1)
        self.q[ename].append(thunk)
        if tok is not None and track:
            for b in writes:
                b.writer = tok
                b.readers = []
            for b in reads:
                b.readers.append(tok)
        return tok

    def dma(self, qname, semname, out_ap, in_ap, reads=(), writes=()):
        needed = self._needed(qname, reads, writes)
        ent = self.dma_sem(semname)
        if ent[1] > 0 and self.mark[qname].get(ent[0].name, 0) < ent[1]:
            needed.append((ent[0], ent[1]))
        for (s, v) in needed:
            self._note(qname, s, v)
        ent[1] += 16
        tok = (ent[0], ent[1])

        def thunk(eng, needed=needed, tok=tok, out_ap=out_ap, in_ap=in_ap):
            for (s, v) in needed[:-1]:
                eng.wait_ge(s, v)
            inst = eng.dma_start(out_ap, in_ap)
            if needed:
                inst._wait_ge(*needed[-1])
            inst.then_inc(tok[0], 16)
        self.q[qname].append(thunk)
        for b in writes:
            b.writer = tok
            b.readers = []
        for b in reads:
            b.readers.append(tok)
        return tok

    def group(self, ename, pieces, reads_all, writes, inc=True):
        n = len(pieces)
        for i, (bf, rd) in enumerate(pieces):
            if n == 1:
                self.emit(ename, bf, reads=reads_all, writes=writes, inc=inc)
            elif i == 0:
                self.emit(ename, bf, reads=rd, writes=writes, inc=False, track=False)
            elif i == n - 1:
                self.emit(ename, bf, reads=reads_all, writes=writes, inc=inc)
            else:
                self.emit(ename, bf, reads=rd, writes=(), inc=False)

    def wait(self, ename, tok):
        if tok is None:
            return
        s, v = tok
        if self.mark[ename].get(s.name, 0) >= v:
            return
        self._note(ename, s, v)

        def thunk(eng, s=s, v=v):
            eng.wait_ge(s, v)
        self.q[ename].append(thunk)

    def barrier(self):
        toks = [(self.sem[e], self.cnt[e]) for e in ENGINES if self.cnt[e] > 0]
        toks += [(h, c) for (h, c) in self.dma_sems.values() if c > 0]
        for e in ENGINES:
            for t in toks:
                self.wait(e, t)

    def flush(self):
        nc = self.nc
        with nc.Block() as block:
            for ename in ENGINES:
                thunks = self.q[ename]
                if not thunks:
                    continue

                def section(eng, thunks=thunks):
                    for th in thunks:
                        th(eng)
                getattr(block, ename)(section)
        self.q = {e: [] for e in ENGINES}


def mask_slot(qt):
    return {0: 0, 1: 1, NQT - 2: 3, NQT - 1: 4}.get(qt, 2)


def s_pieces(qt):
    out = []
    j = 0
    while j < WW:
        slot0 = (qt + j // 128) % KRING
        run = 128
        while j + run < WW and (qt + (j + run) // 128) % KRING == slot0 + run // 128:
            run += 128
        if j < 512 < j + run:
            run = 512 - j
        out.append((j, j + run, slot0 * 128))
        j += run
    return out


def build_program():
    nc = bass.Bass("TRN2", target_bir_lowering=False)

    x_d = nc.dram_tensor("x", [HT, E], F32, kind="ExternalInput")
    wqt_d = nc.dram_tensor("wqt", [E, E], F32, kind="ExternalInput")
    wkt_d = nc.dram_tensor("wkt", [E, E], F32, kind="ExternalInput")
    wvt_d = nc.dram_tensor("wvt", [E, E], F32, kind="ExternalInput")
    w1_d = nc.dram_tensor("w1", [E, INTER], F32, kind="ExternalInput")
    w2_d = nc.dram_tensor("w2", [INTER, E], F32, kind="ExternalInput")
    bq_d = nc.dram_tensor("bq", [128, 8], F32, kind="ExternalInput")
    bk_d = nc.dram_tensor("bk", [128, 8], F32, kind="ExternalInput")
    bv_d = nc.dram_tensor("bv", [1, E], F32, kind="ExternalInput")
    b1_d = nc.dram_tensor("b1", [128, 32], F32, kind="ExternalInput")
    b2_d = nc.dram_tensor("b2", [128, 8], F32, kind="ExternalInput")
    masks_d = nc.dram_tensor("masks", [5, 128, WW], BF16, kind="ExternalInput")
    ident_d = nc.dram_tensor("ident", [128, 128], BF16, kind="ExternalInput")
    o_d = nc.dram_tensor("o", [TOK, E], F32, kind="ExternalOutput")
    d1_d = nc.dram_tensor("d1", [TOK, E], F32, kind="Internal")

    p = Prog(nc)

    ident_sb = nc.alloc_sbuf_tensor("ident_sb", [128, 128], BF16, side="left")
    bq_sb = nc.alloc_sbuf_tensor("bq_sb", [128, 8], F32, side="left")
    bk_sb = nc.alloc_sbuf_tensor("bk_sb", [128, 8], F32, side="left")
    bv_sb = nc.alloc_sbuf_tensor("bv_sb", [1, E], BF16, side="left")
    b1_sb = nc.alloc_sbuf_tensor("b1_sb", [128, 32], F32, side="left")
    b2_sb = nc.alloc_sbuf_tensor("b2_sb", [128, 8], F32, side="left")
    ones_sb = nc.alloc_sbuf_tensor("ones_sb", [1, 128], BF16, side="left")
    c18_sb = nc.alloc_sbuf_tensor("c18_sb", [128, 1], F32, side="left")
    cm05_sb = nc.alloc_sbuf_tensor("cm05_sb", [128, 1], F32, side="left")
    bqs_sb = nc.alloc_sbuf_tensor("bqs_sb", [128, 8], F32, side="left")
    st_sb = nc.alloc_sbuf_tensor("st_sb", [128, 24], F32, side="left")
    mv_sb = nc.alloc_sbuf_tensor("mv_sb", [128, 4], F32, side="left")
    aux_sb = nc.alloc_sbuf_tensor("aux_sb", [128, 8], F32, side="left")

    b_const = Buf("const")
    b_lnp = [Buf("ln0"), Buf("ln1")]
    ln_par = [0]

    p.dma("sync", "ci", ident_sb[:], ident_d[:, :], writes=[b_const])
    p.dma("sync", "ci", bq_sb[:], bq_d[:, :], writes=[b_const])
    p.dma("sync", "ci", bk_sb[:], bk_d[:, :], writes=[b_const])
    p.dma("sync", "ci", b1_sb[:], b1_d[:, :], writes=[b_const])
    p.dma("sync", "ci", b2_sb[:], b2_d[:, :], writes=[b_const])
    p.dma("gpsimd", "cg", bv_sb[:], bv_d[:, :], writes=[b_const])
    p.emit("vector", lambda e: e.memset(ones_sb[:], 1.0), reads=[b_const], writes=[b_const])
    p.emit("vector", lambda e: e.memset(c18_sb[:], 0.125), reads=[b_const], writes=[b_const])
    p.emit("vector", lambda e: e.memset(cm05_sb[:], -0.5), reads=[b_const], writes=[b_const])
    p.emit("vector", lambda e: e.tensor_scalar_mul(bqs_sb[:], bq_sb[:], 0.125),
           reads=[b_const], writes=[b_const])

    def ln_chain(src_ap, src_buf, dst_ap, dst_buf):
        par = ln_par[0]
        ln_par[0] ^= 1
        lb = b_lnp[par]
        st = st_sb[:, 12 * par:12 * par + 12]
        mv = mv_sb[:, 2 * par:2 * par + 2]
        vpe = aux_sb[:, 4 * par + 0:4 * par + 1]
        sd = aux_sb[:, 4 * par + 1:4 * par + 2]
        r = aux_sb[:, 4 * par + 2:4 * par + 3]
        nmr = aux_sb[:, 4 * par + 3:4 * par + 4]
        p.emit("vector", lambda e: e.bn_stats(st[:, 0:6], src_ap[:, 0:512]),
               reads=[src_buf], writes=[lb])
        p.emit("vector", lambda e: e.bn_stats(st[:, 6:12], src_ap[:, 512:1024]),
               reads=[src_buf, lb], writes=[lb])
        p.emit("vector", lambda e: e.bn_aggr(mv, st),
               reads=[lb], writes=[lb])
        p.emit("vector", lambda e: e.tensor_scalar_add(vpe, mv[:, 1:2], float(EPS)),
               reads=[lb], writes=[lb])
        p.emit("scalar", lambda e: e.activation(sd, vpe, ACT.Ln),
               reads=[lb], writes=[lb])
        p.emit("scalar", lambda e: e.activation(r, sd, ACT.Exp, scale=cm05_sb[:, 0:1]),
               reads=[lb, b_const], writes=[lb])
        p.emit("vector", lambda e: e.scalar_tensor_tensor(
            nmr, mv[:, 0:1], -1.0, r, op0=ALU.mult, op1=ALU.mult),
            reads=[lb], writes=[lb])
        p.emit("scalar", lambda e: e.activation(
            dst_ap, src_ap, ACT.Identity, bias=nmr, scale=r),
            reads=[lb, src_buf], writes=[dst_buf])

    stack1 = contextlib.ExitStack()

    def sb1(name, shape, dtype):
        return stack1.enter_context(nc.sbuf_tensor(name, shape, dtype, side="right"))

    def ps1(name, shape):
        return stack1.enter_context(nc.psum_tensor(name, shape, F32))

    wq_sb = sb1("wq_sb", [128, NCH * E], BF16)
    wk_sb = sb1("wk_sb", [128, NCH * E], BF16)
    wv_sb = sb1("wv_sb", [128, NCH * E], BF16)
    masks_sb = sb1("masks_sb", [128, 5 * WW], BF16)
    x_sb = sb1("x_sb", [128, XRING * E], F32)
    xn_sb = sb1("xn_sb", [128, 2 * E], BF16)
    xnT_sb = sb1("xnT_sb", [128, NCH * 512], BF16)
    qT_sb = sb1("qT_sb", [128, NCH * QRING * 128], BF16)
    kT_sb = sb1("kT_sb", [128, NCH * KRING * 128], BF16)
    v_sb = sb1("v_sb", [128, KRING * H * 65], BF16)
    pe_sb = [sb1(f"pe_sb{i}", [128, WW], BF16) for i in range(2)]
    ph_sb = [sb1(f"ph_sb{i}", [128, WW], BF16) for i in range(4)]
    pt_sb = [sb1(f"pt_sb{i}", [128, WW], BF16) for i in range(2)]
    cpair_sb = sb1("cpair_sb", [128, 256], BF16)
    negm_sb = sb1("negm_sb", [128, 2], F32)
    rl_sb = sb1("rl_sb", [128, H], F32)
    ctx_sb = sb1("ctx_sb", [128, 2 * E], F32)

    ps_pj0 = ps1("ps_pj0", [128, 512])
    ps_s = [ps1(f"ps_s{i}", [128, WW]) for i in range(2)]
    ps_ptA = ps1("ps_ptA", [128, 512])
    ps_m = [ps1(f"ps_m{i}", [128, 512]) for i in range(2)]

    b_w = Buf("w")
    b_masks = Buf("masks")
    b_x = [Buf(f"x{i}") for i in range(XRING)]
    b_xn = [Buf(f"xn{i}") for i in range(2)]
    b_xnT = Buf("xnT")
    b_q = [Buf(f"q{i}") for i in range(QRING)]
    b_k = [Buf(f"k{i}") for i in range(KRING)]
    b_v = [Buf(f"v{i}") for i in range(KRING)]
    b_pj = [Buf("pj0"), Buf("pj1")]
    b_s = [Buf("s0"), Buf("s1")]
    b_ptA = Buf("ptA")
    b_m = [Buf("m0"), Buf("m1")]
    b_pe = [Buf("pe0"), Buf("pe1")]
    b_ph = [Buf(f"ph{i}") for i in range(4)]
    b_pt = [Buf("pt0"), Buf("pt1")]
    b_cp = [Buf("cp0"), Buf("cp1")]
    b_negm = [Buf("negm0"), Buf("negm1")]
    b_rl = Buf("rl")
    b_ctx = [Buf("ctx0"), Buf("ctx1")]
    b_d1 = [Buf(f"d1_{i}") for i in range(NQT)]

    p.dma("gpsimd", "cg", wq_sb[:].rearrange("p (c e) -> p c e", c=NCH),
          wqt_d[:, :].rearrange("(c p) e -> p c e", p=128), writes=[b_w])
    p.dma("gpsimd", "cg", wk_sb[:].rearrange("p (c e) -> p c e", c=NCH),
          wkt_d[:, :].rearrange("(c p) e -> p c e", p=128), writes=[b_w])
    p.dma("gpsimd", "cg", wv_sb[:].rearrange("p (c e) -> p c e", c=NCH),
          wvt_d[:, :].rearrange("(c p) e -> p c e", p=128), writes=[b_w])
    p.dma("sync", "ci", masks_sb[:].rearrange("p (m w) -> p m w", m=5),
          masks_d[:, :, :].rearrange("m p w -> p m w"), writes=[b_masks])

    for slot in range(KRING):
        p.emit("vector", lambda e, slot=slot: e.memset(
            v_sb[:, slot * H * 65:(slot + 1) * H * 65]
            .rearrange("p (h d) -> p h d", d=65)[:, :, 64:65], 1.0),
            reads=[b_const], writes=[b_v[slot]])

    def next_pj():
        return (ps_pj0, b_pj[0])

    def tile_ln_t1(t):
        xs = x_sb[:, (t % XRING) * E:(t % XRING + 1) * E]
        xn = xn_sb[:, (t % 2) * E:(t % 2 + 1) * E]
        ln_chain(xs, b_x[t % XRING], xn, b_xn[t % 2])
        for hh in range(2):
            pst, pbb = next_pj()
            pieces = []
            for c in range(4):
                cc = hh * 4 + c
                pieces.append((lambda e, xn=xn, pst=pst, c=c, cc=cc: e.matmul(
                    pst[:, c * 128:(c + 1) * 128],
                    xn[:, cc * 128:(cc + 1) * 128], ident_sb[:],
                    start=True, stop=True), [b_xn[t % 2], b_const]))
            p.group("tensor", pieces, [b_xn[t % 2], b_const], [pbb])
            p.emit("scalar", lambda e, hh=hh, t=t, pst=pst: e.copy(
                xnT_sb[:].rearrange("p (c t) -> p c t", c=NCH)[:, hh * 4:(hh + 1) * 4, (t % 4) * 128:(t % 4 + 1) * 128],
                pst[:].rearrange("p (c t) -> p c t", c=4)),
                reads=[pbb], writes=[b_xnT])

    def group_proj_units(g):
        us = []
        t0 = 4 * g
        qts = [t - 2 for t in range(t0, t0 + 4) if 2 <= t < 2 + NQT]
        kc0 = (t0 % KRING) * 128
        def k_chunk(c):
            pst, pbb = next_pj()
            pieces = []
            for e8 in range(NCH):
                pieces.append((lambda e, c=c, e8=e8, pst=pst: e.matmul(
                    pst[:, 0:512],
                    wk_sb[:, e8 * E + c * 128:e8 * E + (c + 1) * 128],
                    xnT_sb[:, e8 * 512:(e8 + 1) * 512],
                    start=(e8 == 0), stop=(e8 == NCH - 1)),
                    [b_w, b_xnT]))
            kbufs = [b_k[(t0 + i) % KRING] for i in range(4)]
            p.group("tensor", pieces, [b_w, b_xnT], [pbb])

            def ev():
                p.emit("scalar", lambda e, c=c, kc0=kc0, pst=pst: e.activation(
                    kT_sb[:, c * (KRING * 128) + kc0:c * (KRING * 128) + kc0 + 512],
                    pst[:, 0:512], ACT.Identity, bias=bk_sb[:, c:c + 1]),
                    reads=[pbb, b_const], writes=kbufs)
            return ev
        def q_chunk(c):
            if not qts:
                return None
            if True:
                qn = len(qts) * 128
                xoff = ((qts[0] + 2) % 4) * 128
                qc0 = ((qts[0] + 2) % QRING) * 128
                pst, pbb = next_pj()
                pieces = []
                for e8 in range(NCH):
                    pieces.append((lambda e, c=c, e8=e8, xoff=xoff, qn=qn, pst=pst: e.matmul(
                        pst[:, 0:qn],
                        wq_sb[:, e8 * E + c * 128:e8 * E + (c + 1) * 128],
                        xnT_sb[:, e8 * 512 + xoff:e8 * 512 + xoff + qn],
                        start=(e8 == 0), stop=(e8 == NCH - 1)),
                        [b_w, b_xnT]))
                qbufs = [b_q[(qt + 2) % QRING] for qt in qts]
                p.group("tensor", pieces, [b_w, b_xnT], [pbb])

                def ev():
                    p.emit("scalar", lambda e, c=c, qc0=qc0, qn=qn, pst=pst: e.activation(
                        qT_sb[:, c * (QRING * 128) + qc0:c * (QRING * 128) + qc0 + qn],
                        pst[:, 0:qn], ACT.Identity, bias=bqs_sb[:, c:c + 1],
                        scale=c18_sb[:, 0:1]),
                        reads=[pbb, b_const], writes=qbufs)
                return ev
        def v_half(i, half):
            slot = (t0 + i) % KRING
            if True:
                pst, pbb = next_pj()
                pieces = [(lambda e, half=half, pst=pst: e.matmul(
                    pst[:, 0:512], ones_sb[:, 0:128],
                    bv_sb[:, half * 512:(half + 1) * 512],
                    start=True, stop=False), [b_const])]
                for e8 in range(NCH):
                    pieces.append((lambda e, i=i, half=half, e8=e8, pst=pst: e.matmul(
                        pst[:, 0:512],
                        xnT_sb[:, e8 * 512 + i * 128:e8 * 512 + (i + 1) * 128],
                        wv_sb[:, e8 * E + half * 512:e8 * E + (half + 1) * 512],
                        start=False, stop=(e8 == NCH - 1)),
                        [b_w, b_xnT, b_const]))
                p.group("tensor", pieces, [b_w, b_xnT, b_const], [pbb])

                def vev(e, slot=slot, half=half, pst=pst):
                    base = slot * H * 65 + half * 8 * 65
                    return e.copy(
                        v_sb[:, base:base + 8 * 65].rearrange(
                            "p (h d) -> p h d", d=65)[:, :, 0:64],
                        pst[:, 0:512].rearrange("p (h d) -> p h d", d=64))

                def ev():
                    p.emit("scalar", vev, reads=[pbb], writes=[b_v[slot]])
                return ev

        for c in range(NCH):
            us.append(lambda c=c: k_chunk(c))
            us.append(lambda c=c: q_chunk(c))
        for i in range(4):
            for half in range(2):
                us.append(lambda i=i, half=half: v_half(i, half))
        stag = {"pend": None}
        out = []

        def wrap(u):
            def run(u=u):
                if stag["pend"] is not None:
                    stag["pend"]()
                    stag["pend"] = None
                r = u()
                stag["pend"] = r if callable(r) else None
            return run

        def final():
            if stag["pend"] is not None:
                stag["pend"]()
                stag["pend"] = None
        return [wrap(u) for u in us] + [final]

    def attn_stage_A(qt, h):
        qslot = (qt + 2) % QRING
        wbufs = [b_k[(qt + i) % KRING] for i in range(5)]
        pcs = s_pieces(qt)
        j = h // 2
        lo = (h % 2) * 64
        pi = h % 2
        ps = ps_s[pi]
        pieces = []
        for (w0, w1, rc0) in pcs:
            pieces.append((lambda e, ps=ps, lo=lo, j=j, w0=w0, w1=w1, rc0=rc0, qslot=qslot: e.matmul(
                ps[:, w0:w1],
                qT_sb[lo:lo + 64, j * (QRING * 128) + qslot * 128:
                      j * (QRING * 128) + (qslot + 1) * 128],
                kT_sb[lo:lo + 64, j * (KRING * 128) + rc0:
                      j * (KRING * 128) + rc0 + (w1 - w0)],
                start=True, stop=True),
                [b_q[qslot]] + wbufs))
        p.group("tensor", pieces, [b_q[qslot]] + wbufs, [b_s[pi]])

    def attn_stage_Bmax(qt, h):
        pi = h % 2
        ps = ps_s[pi]
        nm = negm_sb[:, pi:pi + 1]
        p.emit("vector", lambda e, ps=ps, nm=nm: e.reduce_max(
            nm, ps[:, 0:WW], axis=AX, negate=True),
            reads=[b_s[pi]], writes=[b_negm[pi]])

    def attn_stage_Bexp(qt, h):
        ms = mask_slot(qt)
        mask = masks_sb[:, ms * WW:(ms + 1) * WW]
        pi = h % 2
        ps = ps_s[pi]
        nm = negm_sb[:, pi:pi + 1]
        pe = pe_sb[pi]
        ph = ph_sb[h % 4]
        p.emit("scalar", lambda e, ps=ps, pe=pe, nm=nm: e.activation(
            pe[:, 0:WW], ps[:, 0:WW], ACT.Exp, bias=nm),
            reads=[b_s[pi], b_negm[pi]], writes=[b_pe[pi]])
        p.emit("gpsimd", lambda e, pe=pe, ph=ph, mask=mask: e.tensor_tensor(
            ph[:, 0:WW], pe[:, 0:WW], mask[:, 0:WW], op=ALU.mult),
            reads=[b_pe[pi], b_masks], writes=[b_ph[h % 4]])

    def attn_stage_C(qt, h):
        pi = h % 2
        ph = ph_sb[h % 4]
        pt = pt_sb[pi]
        pieces = []
        for c in range(4):
            pieces.append((lambda e, ph=ph, c=c: e.matmul(
                ps_ptA[:, c * 128:(c + 1) * 128],
                ph[:, c * 128:(c + 1) * 128], ident_sb[:],
                start=True, stop=True),
                [b_ph[h % 4], b_const]))
        p.group("tensor", pieces, [b_ph[h % 4], b_const], [b_ptA])
        p.emit("tensor", lambda e, ph=ph, pi=pi: e.matmul(
            ps_m[pi][:, 0:128], ph[:, 512:640], ident_sb[:],
            start=True, stop=True),
            reads=[b_ph[h % 4], b_const], writes=[b_m[pi]])
        p.emit("scalar", lambda e, pt=pt: e.copy(pt[:, 0:512], ps_ptA[:, 0:512]),
               reads=[b_ptA], writes=[b_pt[pi]])
        p.emit("scalar", lambda e, pt=pt, pi=pi: e.copy(pt[:, 512:WW], ps_m[pi][:, 0:128]),
               reads=[b_m[pi]], writes=[b_pt[pi]])

    def attn_stage_D(qt, h):
        pi = h % 2
        pt = pt_sb[pi]
        vbufs = [b_v[(qt + i) % KRING] for i in range(5)]
        pieces = []
        for c in range(5):
            slot = (qt + c) % KRING
            pieces.append((lambda e, pt=pt, c=c, h=h, slot=slot, pi=pi: e.matmul(
                ps_m[pi][0:65, 128:256],
                v_sb[:, slot * H * 65 + h * 65:slot * H * 65 + (h + 1) * 65],
                pt[:, c * 128:(c + 1) * 128],
                start=(c == 0), stop=(c == 4)),
                [b_pt[pi]] + vbufs))
        p.group("tensor", pieces, [b_pt[pi]] + vbufs, [b_m[pi]])

    def attn_stage_Et(qt, h):
        pi = h % 2
        cp = cpair_sb[0:65, pi * 128:(pi + 1) * 128]
        p.emit("vector", lambda e, cp=cp, pi=pi: e.tensor_copy(cp, ps_m[pi][0:65, 128:256]),
               reads=[b_m[pi]], writes=[b_cp[pi]])
        p.emit("tensor", lambda e, cp=cp, pi=pi: e.matmul(
            ps_m[pi][:, 256:321], cp, ident_sb[0:65, 0:65], start=True, stop=True),
            reads=[b_cp[pi], b_const], writes=[b_m[pi]])

    def attn_stage_Ev(qt, h):
        pi = h % 2
        p.emit("vector", lambda e, h=h, pi=pi: e.reciprocal(
            rl_sb[:, h:h + 1], ps_m[pi][:, 320:321]),
            reads=[b_m[pi]], writes=[b_rl])
        p.emit("vector", lambda e, h=h, qt=qt, pi=pi: e.tensor_scalar(
            ctx_sb[:, (qt % 2) * E + h * 64:(qt % 2) * E + (h + 1) * 64],
            ps_m[pi][:, 256:320],
            rl_sb[:, h:h + 1], None, op0=ALU.mult),
            reads=[b_m[pi], b_rl], writes=[b_ctx[qt % 2]])

    def attn_finish(qt):
        cs = ctx_sb[:, (qt % 2) * E:(qt % 2 + 1) * E]
        p.dma("gpsimd", "d1w", d1_d[qt * 128:(qt + 1) * 128, :], cs,
              reads=[b_ctx[qt % 2]], writes=[b_d1[qt]])

    def x_dma(t):
        p.dma("sync", f"x{t % XRING}", x_sb[:, (t % XRING) * E:(t % XRING + 1) * E],
              x_d[t * 128:(t + 1) * 128, :], writes=[b_x[t % XRING]])

    def proj_units(g):
        us = [(lambda t=t: tile_ln_t1(t)) for t in range(4 * g, 4 * g + 4)]
        us += group_proj_units(g)
        return us

    cell_state = {"cells": [], "i": 0, "a": 0, "mx": 0, "em": 0}

    def _adv_a(limit):
        cs = cell_state["cells"]
        while cell_state["a"] <= limit and cell_state["a"] < len(cs):
            attn_stage_A(*cs[cell_state["a"]])
            cell_state["a"] += 1

    def _adv_mx(limit):
        cs = cell_state["cells"]
        while cell_state["mx"] <= limit and cell_state["mx"] < len(cs):
            _adv_a(cell_state["mx"])
            attn_stage_Bmax(*cs[cell_state["mx"]])
            cell_state["mx"] += 1

    def _adv_em(limit):
        cs = cell_state["cells"]
        while cell_state["em"] <= limit and cell_state["em"] < len(cs):
            _adv_mx(cell_state["em"])
            attn_stage_Bexp(*cs[cell_state["em"]])
            cell_state["em"] += 1

    def cell_unit():
        cs = cell_state["cells"]
        i = cell_state["i"]
        _adv_em(i + 2)
        if i >= 2:
            attn_stage_Ev(*cs[i - 2])
            if cs[i - 2][1] == H - 1:
                attn_finish(cs[i - 2][0])
        attn_stage_C(*cs[i])
        _adv_mx(i + 3)
        _adv_a(i + 4)
        if i >= 1:
            attn_stage_D(*cs[i - 1])
            attn_stage_Et(*cs[i - 1])
        cell_state["i"] = i + 1

    def cell_push_and_emit(new_cells, tu):
        cell_state["cells"].extend(new_cells)
        na, nb = len(new_cells), len(tu)
        ia = ib = 0
        while ia < na or ib < nb:
            if ia < na:
                cell_unit()
                ia += 1
            while ib < nb and (na == 0 or ib * na <= ia * nb):
                tu[ib]()
                ib += 1

    def cell_drain():
        cs = cell_state["cells"]
        n = len(cs)

        def ev(j):
            attn_stage_Ev(*cs[j])
            if cs[j][1] == H - 1:
                attn_finish(cs[j][0])
        ev(n - 2)
        attn_stage_D(*cs[n - 1])
        attn_stage_Et(*cs[n - 1])
        ev(n - 1)

    for t in range(0, 4):
        x_dma(t)
    for u in proj_units(0):
        u()
    for g in range(NG):
        new_cells = []
        for qt in range(max(0, 4 * g - 4), min(NQT, 4 * g)):
            for h in range(H):
                new_cells.append((qt, h))
        tu = []
        if g + 1 < NG:
            for t in range(4 * (g + 1), 4 * (g + 1) + 4):
                x_dma(t)
            tu = proj_units(g + 1)
        cell_push_and_emit(new_cells, tu)
    cell_drain()

    p.flush()
    stack1.close()

    p.barrier()

    w1r_sb = nc.alloc_sbuf_tensor("w1r_sb", [128, W1RING * NCH * 512], BF16, side="right")
    w2r_sb = nc.alloc_sbuf_tensor("w2r_sb", [128, W2RING * 32 * 128], BF16, side="right")
    x2_sb = nc.alloc_sbuf_tensor("x2_sb", [128, X2RING * E], F32, side="right")
    d1l_sb = nc.alloc_sbuf_tensor("d1l_sb", [128, D1RING * E], F32, side="right")
    xn2_sb = nc.alloc_sbuf_tensor("xn2_sb", [128, 2 * E], BF16, side="right")
    xn2T_sb = nc.alloc_sbuf_tensor("xn2T_sb", [128, NCH * ST_TOK], BF16, side="right")
    h_sb = nc.alloc_sbuf_tensor("h_sb", [128, 32 * ST_TOK], BF16, side="right")
    o2_sb = nc.alloc_sbuf_tensor("o2_sb", [128, NCH * ST_TOK], BF16, side="right")
    ffn_sb = nc.alloc_sbuf_tensor("ffn_sb", [128, 2 * E], F32, side="right")
    ob_sb = nc.alloc_sbuf_tensor("ob_sb", [128, 2 * E], F32, side="right")

    ps_t4a = nc.alloc_psum_tensor("ps_t4a", [128, 512], F32)
    ps_t4b = nc.alloc_psum_tensor("ps_t4b", [128, 512], F32)
    ps_h0 = nc.alloc_psum_tensor("ps_h0", [128, 1024], F32)
    ps_h1 = nc.alloc_psum_tensor("ps_h1", [128, 1024], F32)
    ps_o2 = nc.alloc_psum_tensor("ps_o2", [128, 1024], F32)

    b_w1 = [Buf(f"w1_{i}") for i in range(W1RING)]
    b_w2 = [Buf(f"w2_{i}") for i in range(W2RING)]
    b_x2 = [Buf(f"x2_{i}") for i in range(X2RING)]
    b_d1l = [Buf(f"d1l_{i}") for i in range(D1RING)]
    b_xn2 = [Buf("xn2_0"), Buf("xn2_1")]
    b_xn2T = [Buf("xn2T0"), Buf("xn2T1")]
    b_h = Buf("h")
    b_o2 = Buf("o2")
    b_ffn = [Buf("ffn0"), Buf("ffn1")]
    b_ob = [Buf("ob0"), Buf("ob1")]
    b_pt4 = [Buf("pt4a"), Buf("pt4b")]
    b_ph = [Buf(f"ph{i}") for i in range(4)]
    b_po2 = Buf("po2")

    def dma_w1(s, k4):
        slot = (s * 8 + k4) % W1RING
        ci0 = k4 * 4
        p.dma("gpsimd", f"w1_{slot}",
              w1r_sb[:, slot * NCH * 512:(slot + 1) * NCH * 512].rearrange("p (c i) -> p c i", c=NCH),
              w1_d[:, ci0 * 128:(ci0 + 4) * 128].rearrange("(c p) i -> p c i", p=128),
              writes=[b_w1[slot]])

    def dma_w2(s, eo):
        slot = (s * 8 + eo) % W2RING
        p.dma("gpsimd", f"w2_{slot}",
              w2r_sb[:, slot * 32 * 128:(slot + 1) * 32 * 128].rearrange("p (c e) -> p c e", c=32),
              w2_d[:, eo * 128:(eo + 1) * 128].rearrange("(c p) e -> p c e", p=128),
              writes=[b_w2[slot]])

    def tile_unit_a(s, tt):
        ti = s * STT + tt
        xsl = ti % X2RING
        dsl = ti % D1RING
        p.dma("sync", f"x2_{xsl}", x2_sb[:, xsl * E:(xsl + 1) * E],
              x_d[256 + ti * 128:256 + (ti + 1) * 128, :], writes=[b_x2[xsl]])
        p.dma("sync", f"d1l_{dsl}", d1l_sb[:, dsl * E:(dsl + 1) * E],
              d1_d[ti * 128:(ti + 1) * 128, :],
              reads=[b_d1[ti]], writes=[b_d1l[dsl]])
        dsv = d1l_sb[:, dsl * E:(dsl + 1) * E]
        ln_chain(dsv, b_d1l[dsl], dsv, b_d1l[dsl])

    def tile_unit(s, tt):
        ti = s * STT + tt
        xsl = ti % X2RING
        dsl = ti % D1RING
        dsv = d1l_sb[:, dsl * E:(dsl + 1) * E]
        p.emit("vector", lambda e, xsl=xsl, dsv=dsv: e.tensor_tensor(
            x2_sb[:, xsl * E:(xsl + 1) * E], x2_sb[:, xsl * E:(xsl + 1) * E],
            dsv, op=ALU.add),
            reads=[b_x2[xsl], b_d1l[dsl]], writes=[b_x2[xsl]])
        xn2 = xn2_sb[:, (tt % 2) * E:(tt % 2 + 1) * E]
        ln_chain(x2_sb[:, xsl * E:(xsl + 1) * E], b_x2[xsl], xn2, b_xn2[tt % 2])
        for hh in range(2):
            pst = ps_t4a if hh == 0 else ps_t4b
            pbb = b_pt4[hh]
            pieces = []
            for c in range(4):
                cc = hh * 4 + c
                pieces.append((lambda e, xn2=xn2, pst=pst, c=c, cc=cc: e.matmul(
                    pst[:, c * 128:(c + 1) * 128],
                    xn2[:, cc * 128:(cc + 1) * 128], ident_sb[:],
                    start=True, stop=True), [b_xn2[tt % 2], b_const]))
            p.group("tensor", pieces, [b_xn2[tt % 2], b_const], [pbb])
            p.emit("scalar", lambda e, hh=hh, tt=tt, pst=pst: e.copy(
                xn2T_sb[:].rearrange("p (c t) -> p c t", c=NCH)[:, hh * 4:(hh + 1) * 4, tt * 128:(tt + 1) * 128],
                pst[:].rearrange("p (c t) -> p c t", c=4)),
                reads=[pbb], writes=[b_xn2T[tt // 4]])

    def w1_unit(s, ci, half):
        if True:
            slot = (s * 8 + ci // 4) % W1RING
            wbase = slot * NCH * 512 + (ci % 4) * 128
            psh = ps_h0 if ci % 2 == 0 else ps_h1
            pbh = b_ph[ci % 2]
            pieces = []
            for e8 in range(NCH):
                pieces.append((lambda e, e8=e8, half=half, psh=psh, wbase=wbase: e.matmul(
                    psh[:, half * 512:(half + 1) * 512],
                    w1r_sb[:, wbase + e8 * 512:wbase + e8 * 512 + 128],
                    xn2T_sb[:, e8 * ST_TOK + half * 512:e8 * ST_TOK + (half + 1) * 512],
                    start=(e8 == 0), stop=(e8 == NCH - 1)),
                    [b_w1[slot], b_xn2T[half]]))
            p.group("tensor", pieces, [b_w1[slot], b_xn2T[half]], [pbh],
                    inc=(half == 1))
            if half == 1:
                p.emit("vector", lambda e, ci=ci, psh=psh: e.tensor_scalar(
                    h_sb[:, ci * ST_TOK:(ci + 1) * ST_TOK], psh[:],
                    b1_sb[:, ci:ci + 1], 0.0, op0=ALU.add, op1=ALU.max),
                    reads=[pbh, b_const], writes=[b_h])
                if ci % 4 == 3:
                    k4n = ci // 4 + W1RING
                    if k4n < 8:
                        dma_w1(s, k4n)
                    elif s + 1 < NST:
                        dma_w1(s + 1, k4n - 8)
                    if ci // 4 < W2RING:
                        dma_w2(s, ci // 4)
    def w2_unit(s, eo):
        if True:
            slot = (s * 8 + eo) % W2RING
            pieces = []
            for ic in range(32):
                for half in range(2):
                    pieces.append((lambda e, ic=ic, half=half, slot=slot: e.matmul(
                        ps_o2[:, half * 512:(half + 1) * 512],
                        w2r_sb[:, slot * 32 * 128 + ic * 128:slot * 32 * 128 + (ic + 1) * 128],
                        h_sb[:, ic * ST_TOK + half * 512:ic * ST_TOK + (half + 1) * 512],
                        start=(ic == 0), stop=(ic == 31)),
                        [b_w2[slot], b_h]))
            p.group("tensor", pieces, [b_w2[slot], b_h], [b_po2])
            p.emit("scalar", lambda e, eo=eo: e.activation(
                o2_sb[:, eo * ST_TOK:(eo + 1) * ST_TOK], ps_o2[:], ACT.Identity,
                bias=b2_sb[:, eo:eo + 1]),
                reads=[b_po2, b_const], writes=[b_o2])
            if eo + W2RING < 8:
                dma_w2(s, eo + W2RING)
    def t5_unit(s, tt):
        if True:
            ti = s * STT + tt
            xsl = ti % X2RING
            pieces = []
            for eo in range(NCH):
                pieces.append((lambda e, eo=eo, tt=tt: e.matmul(
                    ps_o2[:, eo * 128:(eo + 1) * 128],
                    o2_sb[:, eo * ST_TOK + tt * 128:eo * ST_TOK + (tt + 1) * 128],
                    ident_sb[:], start=True, stop=True),
                    [b_o2, b_const]))
            p.group("tensor", pieces, [b_o2, b_const], [b_po2])
            fs = ffn_sb[:, (tt % 2) * E:(tt % 2 + 1) * E]
            p.emit("scalar", lambda e, fs=fs: e.copy(fs, ps_o2[:]),
                   reads=[b_po2], writes=[b_ffn[tt % 2]])
            ob = ob_sb[:, (tt % 2) * E:(tt % 2 + 1) * E]
            ln_chain(fs, b_ffn[tt % 2], ob, b_ob[tt % 2])
            p.emit("vector", lambda e, ob=ob, xsl=xsl: e.tensor_tensor(
                ob, ob, x2_sb[:, xsl * E:(xsl + 1) * E], op=ALU.add),
                reads=[b_ob[tt % 2], b_x2[xsl]], writes=[b_ob[tt % 2]])
            p.dma("sync", "out", o_d[ti * 128:(ti + 1) * 128, :], ob,
                  reads=[b_ob[tt % 2]])

    def interleave(ua, ub):
        na, nb = len(ua), len(ub)
        ia = ib = 0
        while ia < na or ib < nb:
            if ia < na:
                ua[ia]()
                ia += 1
            while ib < nb and (na == 0 or ib * na <= ia * nb):
                ub[ib]()
                ib += 1

    def tiles_and_t5_units(s, skip=0):
        seq = []
        if s > 0:
            for i in range(3):
                seq.append(lambda s=s, i=i: t5_unit(s - 1, i))
        if skip == 0:
            seq.append(lambda s=s: tile_unit_a(s, 0))
        for tt in range(STT):
            if tt + 1 < STT and tt + 1 >= skip:
                seq.append(lambda s=s, tt=tt: tile_unit_a(s, tt + 1))
            if tt >= skip:
                seq.append(lambda s=s, tt=tt: tile_unit(s, tt))
            if s > 0 and tt + 3 < STT:
                seq.append(lambda s=s, tt=tt: t5_unit(s - 1, tt + 3))
        return seq

    for k4 in range(W1RING):
        dma_w1(0, k4)
    for s in range(NST):
        for u in tiles_and_t5_units(s, 2 if s > 0 else 0):
            u()
        for ci in range(32):
            w1_unit(s, ci, 0)
            w1_unit(s, ci, 1)
        hoist = []
        if s + 1 < NST:
            hoist = [lambda s=s: tile_unit_a(s + 1, 0),
                     lambda s=s: tile_unit_a(s + 1, 1),
                     lambda s=s: tile_unit(s + 1, 0),
                     lambda s=s: tile_unit(s + 1, 1)]
        interleave([lambda s=s, eo=eo: w2_unit(s, eo) for eo in range(8)],
                   hoist)
    for tt in range(STT):
        t5_unit(NST - 1, tt)

    sem, cntv = p.dma_sems["out"]
    p.wait("sync", (sem, cntv))
    p.flush()
    return nc


_NC_CACHE = {}


def _get_nc():
    if "nc" not in _NC_CACHE:
        _NC_CACHE["nc"] = build_program()
    return _NC_CACHE["nc"]


def _make_masks(h):
    i = np.arange(128)[:, None]
    out = np.zeros((5, 128, WW), np.float32)
    for si, qt in enumerate((0, 1, 15, NQT - 2, NQT - 1)):
        j = np.arange(WW)[None, :]
        band = np.abs(j - 256 - i) <= 256
        g = h * 4096 + qt * 128 - 256 + j
        valid = (g >= 0) & (g < SEQ)
        out[si] = (band & valid).astype(np.float32)
    return out.astype(ml_dtypes.bfloat16)


def kernel(L0_attn_hidden_states, L0_attn_Wq, L0_attn_bq, L0_attn_Wk, L0_attn_bk,
           L0_attn_Wv, L0_attn_bv, L0_ffn_W1, L0_ffn_b1, L0_ffn_W2, L0_ffn_b2):
    x = np.ascontiguousarray(np.asarray(L0_attn_hidden_states, dtype=np.float32))
    wq = np.asarray(L0_attn_Wq, np.float32)
    wk = np.asarray(L0_attn_Wk, np.float32)
    wv = np.asarray(L0_attn_Wv, np.float32)
    w1 = np.ascontiguousarray(np.asarray(L0_ffn_W1, np.float32))
    w2 = np.ascontiguousarray(np.asarray(L0_ffn_W2, np.float32))
    bq = np.asarray(L0_attn_bq, np.float32)
    bk = np.asarray(L0_attn_bk, np.float32)
    bv = np.asarray(L0_attn_bv, np.float32)
    b1 = np.asarray(L0_ffn_b1, np.float32)
    b2 = np.asarray(L0_ffn_b2, np.float32)

    wqt = np.ascontiguousarray(wq.T)
    wkt = np.ascontiguousarray(wk.T)
    wvt = np.ascontiguousarray(wv.T)
    bq_l = np.ascontiguousarray(bq.reshape(8, 128).T)
    bk_l = np.ascontiguousarray(bk.reshape(8, 128).T)
    bv_l = np.ascontiguousarray(bv.reshape(1, E))
    b1_l = np.ascontiguousarray(b1.reshape(32, 128).T)
    b2_l = np.ascontiguousarray(b2.reshape(8, 128).T)
    ident = np.eye(128, dtype=ml_dtypes.bfloat16)
    masks_h = [_make_masks(0), _make_masks(1)]

    in_maps = []
    for c in range(8):
        b, h = c // 2, c % 2
        xs = np.zeros((HT, E), np.float32)
        if h == 0:
            xs[256:] = x[b, 0:HT - 256]
        else:
            xs[:HT - 256] = x[b, SEQ - (HT - 256):SEQ]
        in_maps.append({
            "x": xs, "wqt": wqt, "wkt": wkt, "wvt": wvt, "w1": w1, "w2": w2,
            "bq": bq_l, "bk": bk_l, "bv": bv_l, "b1": b1_l, "b2": b2_l,
            "masks": masks_h[h], "ident": ident,
        })

    nc = _get_nc()
    res = run_bass_kernel_spmd(nc, in_maps, core_ids=list(range(8)))
    out = np.empty((4, SEQ, E), np.float32)
    for c in range(8):
        b, h = c // 2, c % 2
        out[b, h * 4096:(h + 1) * 4096] = np.asarray(res.results[c]["o"], np.float32)
    return out


# version 23
# speedup vs baseline: 1.1251x; 1.0246x over previous
import contextlib
import numpy as np
import ml_dtypes
import concourse.bass as bass
import concourse.mybir as mybir
from concourse.bass_utils import run_bass_kernel_spmd

F32 = mybir.dt.float32
BF16 = mybir.dt.bfloat16
AX = mybir.AxisListType.X
ALU = mybir.AluOpType
ACT = mybir.ActivationFunctionType

ENGINES = ("tensor", "vector", "scalar", "sync", "gpsimd")

E = 1024
H = 16
D = 64
NCH = 8
SEQ = 8192
TOK = 4096
HT = TOK + 512
NKT = HT // 128
NQT = TOK // 128
NG = NKT // 4
WW = 640
EPS = 1e-5
INTER = 4096
ST_TOK = 1024
NST = TOK // ST_TOK
STT = ST_TOK // 128

KRING = 12
QRING = 12
XRING = 6
X2RING = 12
D1RING = 2
W1RING = 3
W2RING = 2


class Buf:
    __slots__ = ("writer", "readers", "name")

    def __init__(self, name=""):
        self.writer = None
        self.readers = []
        self.name = name


class Prog:
    def __init__(self, nc):
        self.nc = nc
        self.sem = {}
        self.cnt = {}
        self.q = {e: [] for e in ENGINES}
        for e in ENGINES:
            self.sem[e] = nc.alloc_semaphore(f"sem_{e}")
            self.cnt[e] = 0
        self.mark = {e: {} for e in ENGINES}
        self.dma_sems = {}

    def dma_sem(self, name):
        if name not in self.dma_sems:
            self.dma_sems[name] = [self.nc.alloc_semaphore(f"dsem_{name}"), 0]
        return self.dma_sems[name]

    def _needed(self, ename, reads, writes):
        toks = []
        for b in reads:
            if b.writer is not None:
                toks.append(b.writer)
        for b in writes:
            if b.writer is not None:
                toks.append(b.writer)
            toks.extend(b.readers)
        merged = {}
        for (s, v) in toks:
            if merged.get(s.name, (None, -1))[1] < v:
                merged[s.name] = (s, v)
        wm = self.mark[ename]
        return [(s, v) for nm, (s, v) in merged.items() if wm.get(nm, 0) < v]

    def _note(self, ename, s, v):
        wm = self.mark[ename]
        if wm.get(s.name, 0) < v:
            wm[s.name] = v

    def emit(self, ename, build, reads=(), writes=(), inc=True, track=True):
        needed = self._needed(ename, reads, writes)
        for (s, v) in needed:
            self._note(ename, s, v)
        tok = None
        if inc:
            self.cnt[ename] += 1
            tok = (self.sem[ename], self.cnt[ename])

        def thunk(eng, build=build, needed=needed, tok=tok):
            for (s, v) in needed[:-1]:
                eng.wait_ge(s, v)
            inst = build(eng)
            if needed:
                inst._wait_ge(*needed[-1])
            if tok is not None:
                inst.then_inc(tok[0], 1)
        self.q[ename].append(thunk)
        if tok is not None and track:
            for b in writes:
                b.writer = tok
                b.readers = []
            for b in reads:
                b.readers.append(tok)
        return tok

    def dma(self, qname, semname, out_ap, in_ap, reads=(), writes=()):
        needed = self._needed(qname, reads, writes)
        ent = self.dma_sem(semname)
        if ent[1] > 0 and self.mark[qname].get(ent[0].name, 0) < ent[1]:
            needed.append((ent[0], ent[1]))
        for (s, v) in needed:
            self._note(qname, s, v)
        ent[1] += 16
        tok = (ent[0], ent[1])

        def thunk(eng, needed=needed, tok=tok, out_ap=out_ap, in_ap=in_ap):
            for (s, v) in needed[:-1]:
                eng.wait_ge(s, v)
            inst = eng.dma_start(out_ap, in_ap)
            if needed:
                inst._wait_ge(*needed[-1])
            inst.then_inc(tok[0], 16)
        self.q[qname].append(thunk)
        for b in writes:
            b.writer = tok
            b.readers = []
        for b in reads:
            b.readers.append(tok)
        return tok

    def group(self, ename, pieces, reads_all, writes, inc=True):
        n = len(pieces)
        for i, (bf, rd) in enumerate(pieces):
            if n == 1:
                self.emit(ename, bf, reads=reads_all, writes=writes, inc=inc)
            elif i == 0:
                self.emit(ename, bf, reads=rd, writes=writes, inc=False, track=False)
            elif i == n - 1:
                self.emit(ename, bf, reads=reads_all, writes=writes, inc=inc)
            else:
                self.emit(ename, bf, reads=rd, writes=(), inc=False)

    def wait(self, ename, tok):
        if tok is None:
            return
        s, v = tok
        if self.mark[ename].get(s.name, 0) >= v:
            return
        self._note(ename, s, v)

        def thunk(eng, s=s, v=v):
            eng.wait_ge(s, v)
        self.q[ename].append(thunk)

    def barrier(self):
        toks = [(self.sem[e], self.cnt[e]) for e in ENGINES if self.cnt[e] > 0]
        toks += [(h, c) for (h, c) in self.dma_sems.values() if c > 0]
        for e in ENGINES:
            for t in toks:
                self.wait(e, t)

    def flush(self):
        nc = self.nc
        with nc.Block() as block:
            for ename in ENGINES:
                thunks = self.q[ename]
                if not thunks:
                    continue

                def section(eng, thunks=thunks):
                    for th in thunks:
                        th(eng)
                getattr(block, ename)(section)
        self.q = {e: [] for e in ENGINES}


def mask_slot(qt):
    return {0: 0, 1: 1, NQT - 2: 3, NQT - 1: 4}.get(qt, 2)


def s_pieces(qt):
    out = []
    j = 0
    while j < WW:
        slot0 = (qt + j // 128) % KRING
        run = 128
        while j + run < WW and (qt + (j + run) // 128) % KRING == slot0 + run // 128:
            run += 128
        if j < 512 < j + run:
            run = 512 - j
        out.append((j, j + run, slot0 * 128))
        j += run
    return out


def build_program():
    nc = bass.Bass("TRN2", target_bir_lowering=False)

    x_d = nc.dram_tensor("x", [HT, E], F32, kind="ExternalInput")
    wqt_d = nc.dram_tensor("wqt", [E, E], F32, kind="ExternalInput")
    wkt_d = nc.dram_tensor("wkt", [E, E], F32, kind="ExternalInput")
    wvt_d = nc.dram_tensor("wvt", [E, E], F32, kind="ExternalInput")
    w1_d = nc.dram_tensor("w1", [E, INTER], F32, kind="ExternalInput")
    w2_d = nc.dram_tensor("w2", [INTER, E], F32, kind="ExternalInput")
    bq_d = nc.dram_tensor("bq", [128, 8], F32, kind="ExternalInput")
    bk_d = nc.dram_tensor("bk", [128, 8], F32, kind="ExternalInput")
    bv_d = nc.dram_tensor("bv", [1, E], F32, kind="ExternalInput")
    b1_d = nc.dram_tensor("b1", [128, 32], F32, kind="ExternalInput")
    b2_d = nc.dram_tensor("b2", [128, 8], F32, kind="ExternalInput")
    masks_d = nc.dram_tensor("masks", [5, 128, WW], BF16, kind="ExternalInput")
    ident_d = nc.dram_tensor("ident", [128, 128], BF16, kind="ExternalInput")
    o_d = nc.dram_tensor("o", [TOK, E], F32, kind="ExternalOutput")
    d1_d = nc.dram_tensor("d1", [TOK, E], F32, kind="Internal")

    p = Prog(nc)

    ident_sb = nc.alloc_sbuf_tensor("ident_sb", [128, 128], BF16, side="left")
    bq_sb = nc.alloc_sbuf_tensor("bq_sb", [128, 8], F32, side="left")
    bk_sb = nc.alloc_sbuf_tensor("bk_sb", [128, 8], F32, side="left")
    bv_sb = nc.alloc_sbuf_tensor("bv_sb", [1, E], BF16, side="left")
    b1_sb = nc.alloc_sbuf_tensor("b1_sb", [128, 32], F32, side="left")
    b2_sb = nc.alloc_sbuf_tensor("b2_sb", [128, 8], F32, side="left")
    ones_sb = nc.alloc_sbuf_tensor("ones_sb", [1, 128], BF16, side="left")
    c18_sb = nc.alloc_sbuf_tensor("c18_sb", [128, 1], F32, side="left")
    cm05_sb = nc.alloc_sbuf_tensor("cm05_sb", [128, 1], F32, side="left")
    bqs_sb = nc.alloc_sbuf_tensor("bqs_sb", [128, 8], F32, side="left")
    st_sb = nc.alloc_sbuf_tensor("st_sb", [128, 24], F32, side="left")
    mv_sb = nc.alloc_sbuf_tensor("mv_sb", [128, 4], F32, side="left")
    aux_sb = nc.alloc_sbuf_tensor("aux_sb", [128, 8], F32, side="left")

    b_const = Buf("const")
    b_lnp = [Buf("ln0"), Buf("ln1")]
    ln_par = [0]

    p.dma("sync", "ci", ident_sb[:], ident_d[:, :], writes=[b_const])
    p.dma("sync", "ci", bq_sb[:], bq_d[:, :], writes=[b_const])
    p.dma("sync", "ci", bk_sb[:], bk_d[:, :], writes=[b_const])
    p.dma("sync", "ci", b1_sb[:], b1_d[:, :], writes=[b_const])
    p.dma("sync", "ci", b2_sb[:], b2_d[:, :], writes=[b_const])
    p.dma("gpsimd", "cg", bv_sb[:], bv_d[:, :], writes=[b_const])
    p.emit("vector", lambda e: e.memset(ones_sb[:], 1.0), reads=[b_const], writes=[b_const])
    p.emit("vector", lambda e: e.memset(c18_sb[:], 0.125), reads=[b_const], writes=[b_const])
    p.emit("vector", lambda e: e.memset(cm05_sb[:], -0.5), reads=[b_const], writes=[b_const])
    p.emit("vector", lambda e: e.tensor_scalar_mul(bqs_sb[:], bq_sb[:], 0.125),
           reads=[b_const], writes=[b_const])

    def ln_chain(src_ap, src_buf, dst_ap, dst_buf):
        par = ln_par[0]
        ln_par[0] ^= 1
        lb = b_lnp[par]
        st = st_sb[:, 12 * par:12 * par + 12]
        mv = mv_sb[:, 2 * par:2 * par + 2]
        vpe = aux_sb[:, 4 * par + 0:4 * par + 1]
        sd = aux_sb[:, 4 * par + 1:4 * par + 2]
        r = aux_sb[:, 4 * par + 2:4 * par + 3]
        nmr = aux_sb[:, 4 * par + 3:4 * par + 4]
        p.emit("vector", lambda e: e.bn_stats(st[:, 0:6], src_ap[:, 0:512]),
               reads=[src_buf], writes=[lb])
        p.emit("vector", lambda e: e.bn_stats(st[:, 6:12], src_ap[:, 512:1024]),
               reads=[src_buf, lb], writes=[lb])
        p.emit("vector", lambda e: e.bn_aggr(mv, st),
               reads=[lb], writes=[lb])
        p.emit("vector", lambda e: e.tensor_scalar_add(vpe, mv[:, 1:2], float(EPS)),
               reads=[lb], writes=[lb])
        p.emit("scalar", lambda e: e.activation(sd, vpe, ACT.Ln),
               reads=[lb], writes=[lb])
        p.emit("scalar", lambda e: e.activation(r, sd, ACT.Exp, scale=cm05_sb[:, 0:1]),
               reads=[lb, b_const], writes=[lb])
        p.emit("vector", lambda e: e.scalar_tensor_tensor(
            nmr, mv[:, 0:1], -1.0, r, op0=ALU.mult, op1=ALU.mult),
            reads=[lb], writes=[lb])
        p.emit("scalar", lambda e: e.activation(
            dst_ap, src_ap, ACT.Identity, bias=nmr, scale=r),
            reads=[lb, src_buf], writes=[dst_buf])

    stack1 = contextlib.ExitStack()

    def sb1(name, shape, dtype):
        return stack1.enter_context(nc.sbuf_tensor(name, shape, dtype, side="right"))

    def ps1(name, shape):
        return stack1.enter_context(nc.psum_tensor(name, shape, F32))

    wq_sb = sb1("wq_sb", [128, NCH * E], BF16)
    wk_sb = sb1("wk_sb", [128, NCH * E], BF16)
    wv_sb = sb1("wv_sb", [128, NCH * E], BF16)
    masks_sb = sb1("masks_sb", [128, 5 * WW], BF16)
    x_sb = sb1("x_sb", [128, XRING * E], F32)
    xn_sb = sb1("xn_sb", [128, 2 * E], BF16)
    xnT_sb = sb1("xnT_sb", [128, NCH * 512], BF16)
    qT_sb = sb1("qT_sb", [128, NCH * QRING * 128], BF16)
    kT_sb = sb1("kT_sb", [128, NCH * KRING * 128], BF16)
    v_sb = sb1("v_sb", [128, KRING * H * 65], BF16)
    pe_sb = [sb1(f"pe_sb{i}", [128, WW], BF16) for i in range(2)]
    ph_sb = [sb1(f"ph_sb{i}", [128, WW], BF16) for i in range(4)]
    pt_sb = [sb1(f"pt_sb{i}", [128, WW], BF16) for i in range(2)]
    cpair_sb = sb1("cpair_sb", [128, 256], BF16)
    negm_sb = sb1("negm_sb", [128, 2], F32)
    rl_sb = sb1("rl_sb", [128, H], F32)
    ctx_sb = sb1("ctx_sb", [128, 2 * E], F32)

    ps_pj0 = ps1("ps_pj0", [128, 512])
    ps_s = [ps1(f"ps_s{i}", [128, WW]) for i in range(2)]
    ps_ptA = ps1("ps_ptA", [128, 512])
    ps_m = [ps1(f"ps_m{i}", [128, 512]) for i in range(2)]

    b_w = Buf("w")
    b_masks = Buf("masks")
    b_x = [Buf(f"x{i}") for i in range(XRING)]
    b_xn = [Buf(f"xn{i}") for i in range(2)]
    b_xnT = Buf("xnT")
    b_q = [Buf(f"q{i}") for i in range(QRING)]
    b_k = [Buf(f"k{i}") for i in range(KRING)]
    b_v = [Buf(f"v{i}") for i in range(KRING)]
    b_pj = [Buf("pj0"), Buf("pj1")]
    b_s = [Buf("s0"), Buf("s1")]
    b_ptA = Buf("ptA")
    b_m = [Buf("m0"), Buf("m1")]
    b_pe = [Buf("pe0"), Buf("pe1")]
    b_ph = [Buf(f"ph{i}") for i in range(4)]
    b_pt = [Buf("pt0"), Buf("pt1")]
    b_cp = [Buf("cp0"), Buf("cp1")]
    b_negm = [Buf("negm0"), Buf("negm1")]
    b_rl = Buf("rl")
    b_ctx = [Buf("ctx0"), Buf("ctx1")]
    b_d1 = [Buf(f"d1_{i}") for i in range(NQT)]

    p.dma("gpsimd", "cg", wq_sb[:].rearrange("p (c e) -> p c e", c=NCH),
          wqt_d[:, :].rearrange("(c p) e -> p c e", p=128), writes=[b_w])
    p.dma("gpsimd", "cg", wk_sb[:].rearrange("p (c e) -> p c e", c=NCH),
          wkt_d[:, :].rearrange("(c p) e -> p c e", p=128), writes=[b_w])
    p.dma("gpsimd", "cg", wv_sb[:].rearrange("p (c e) -> p c e", c=NCH),
          wvt_d[:, :].rearrange("(c p) e -> p c e", p=128), writes=[b_w])
    p.dma("sync", "ci", masks_sb[:].rearrange("p (m w) -> p m w", m=5),
          masks_d[:, :, :].rearrange("m p w -> p m w"), writes=[b_masks])

    for slot in range(KRING):
        p.emit("vector", lambda e, slot=slot: e.memset(
            v_sb[:, slot * H * 65:(slot + 1) * H * 65]
            .rearrange("p (h d) -> p h d", d=65)[:, :, 64:65], 1.0),
            reads=[b_const], writes=[b_v[slot]])

    def next_pj():
        return (ps_pj0, b_pj[0])

    def tile_ln_t1(t):
        xs = x_sb[:, (t % XRING) * E:(t % XRING + 1) * E]
        xn = xn_sb[:, (t % 2) * E:(t % 2 + 1) * E]
        ln_chain(xs, b_x[t % XRING], xn, b_xn[t % 2])
        for hh in range(2):
            pst, pbb = next_pj()
            pieces = []
            for c in range(4):
                cc = hh * 4 + c
                pieces.append((lambda e, xn=xn, pst=pst, c=c, cc=cc: e.matmul(
                    pst[:, c * 128:(c + 1) * 128],
                    xn[:, cc * 128:(cc + 1) * 128], ident_sb[:],
                    start=True, stop=True), [b_xn[t % 2], b_const]))
            p.group("tensor", pieces, [b_xn[t % 2], b_const], [pbb])
            p.emit("scalar", lambda e, hh=hh, t=t, pst=pst: e.copy(
                xnT_sb[:].rearrange("p (c t) -> p c t", c=NCH)[:, hh * 4:(hh + 1) * 4, (t % 4) * 128:(t % 4 + 1) * 128],
                pst[:].rearrange("p (c t) -> p c t", c=4)),
                reads=[pbb], writes=[b_xnT])

    def group_proj_units(g):
        us = []
        t0 = 4 * g
        qts = [t - 2 for t in range(t0, t0 + 4) if 2 <= t < 2 + NQT]
        kc0 = (t0 % KRING) * 128
        def k_chunk(c):
            pst, pbb = next_pj()
            pieces = []
            for e8 in range(NCH):
                pieces.append((lambda e, c=c, e8=e8, pst=pst: e.matmul(
                    pst[:, 0:512],
                    wk_sb[:, e8 * E + c * 128:e8 * E + (c + 1) * 128],
                    xnT_sb[:, e8 * 512:(e8 + 1) * 512],
                    start=(e8 == 0), stop=(e8 == NCH - 1)),
                    [b_w, b_xnT]))
            kbufs = [b_k[(t0 + i) % KRING] for i in range(4)]
            p.group("tensor", pieces, [b_w, b_xnT], [pbb])

            def ev():
                p.emit("scalar", lambda e, c=c, kc0=kc0, pst=pst: e.activation(
                    kT_sb[:, c * (KRING * 128) + kc0:c * (KRING * 128) + kc0 + 512],
                    pst[:, 0:512], ACT.Identity, bias=bk_sb[:, c:c + 1]),
                    reads=[pbb, b_const], writes=kbufs)
            return ev
        def q_chunk(c):
            if not qts:
                return None
            if True:
                qn = len(qts) * 128
                xoff = ((qts[0] + 2) % 4) * 128
                qc0 = ((qts[0] + 2) % QRING) * 128
                pst, pbb = next_pj()
                pieces = []
                for e8 in range(NCH):
                    pieces.append((lambda e, c=c, e8=e8, xoff=xoff, qn=qn, pst=pst: e.matmul(
                        pst[:, 0:qn],
                        wq_sb[:, e8 * E + c * 128:e8 * E + (c + 1) * 128],
                        xnT_sb[:, e8 * 512 + xoff:e8 * 512 + xoff + qn],
                        start=(e8 == 0), stop=(e8 == NCH - 1)),
                        [b_w, b_xnT]))
                qbufs = [b_q[(qt + 2) % QRING] for qt in qts]
                p.group("tensor", pieces, [b_w, b_xnT], [pbb])

                def ev():
                    p.emit("scalar", lambda e, c=c, qc0=qc0, qn=qn, pst=pst: e.activation(
                        qT_sb[:, c * (QRING * 128) + qc0:c * (QRING * 128) + qc0 + qn],
                        pst[:, 0:qn], ACT.Identity, bias=bqs_sb[:, c:c + 1],
                        scale=c18_sb[:, 0:1]),
                        reads=[pbb, b_const], writes=qbufs)
                return ev
        def v_half(i, half):
            slot = (t0 + i) % KRING
            if True:
                pst, pbb = next_pj()
                pieces = [(lambda e, half=half, pst=pst: e.matmul(
                    pst[:, 0:512], ones_sb[:, 0:128],
                    bv_sb[:, half * 512:(half + 1) * 512],
                    start=True, stop=False), [b_const])]
                for e8 in range(NCH):
                    pieces.append((lambda e, i=i, half=half, e8=e8, pst=pst: e.matmul(
                        pst[:, 0:512],
                        xnT_sb[:, e8 * 512 + i * 128:e8 * 512 + (i + 1) * 128],
                        wv_sb[:, e8 * E + half * 512:e8 * E + (half + 1) * 512],
                        start=False, stop=(e8 == NCH - 1)),
                        [b_w, b_xnT, b_const]))
                p.group("tensor", pieces, [b_w, b_xnT, b_const], [pbb])

                def vev(e, slot=slot, half=half, pst=pst):
                    base = slot * H * 65 + half * 8 * 65
                    return e.copy(
                        v_sb[:, base:base + 8 * 65].rearrange(
                            "p (h d) -> p h d", d=65)[:, :, 0:64],
                        pst[:, 0:512].rearrange("p (h d) -> p h d", d=64))

                def ev():
                    p.emit("scalar", vev, reads=[pbb], writes=[b_v[slot]])
                return ev

        for c in range(NCH):
            us.append(lambda c=c: k_chunk(c))
            us.append(lambda c=c: q_chunk(c))
        for i in range(4):
            for half in range(2):
                us.append(lambda i=i, half=half: v_half(i, half))
        stag = {"pend": None}
        out = []

        def wrap(u):
            def run(u=u):
                if stag["pend"] is not None:
                    stag["pend"]()
                    stag["pend"] = None
                r = u()
                stag["pend"] = r if callable(r) else None
            return run

        def final():
            if stag["pend"] is not None:
                stag["pend"]()
                stag["pend"] = None
        return [wrap(u) for u in us] + [final]

    def attn_stage_A(qt, h):
        qslot = (qt + 2) % QRING
        wbufs = [b_k[(qt + i) % KRING] for i in range(5)]
        pcs = s_pieces(qt)
        j = h // 2
        lo = (h % 2) * 64
        pi = h % 2
        ps = ps_s[pi]
        pieces = []
        for (w0, w1, rc0) in pcs:
            pieces.append((lambda e, ps=ps, lo=lo, j=j, w0=w0, w1=w1, rc0=rc0, qslot=qslot: e.matmul(
                ps[:, w0:w1],
                qT_sb[lo:lo + 64, j * (QRING * 128) + qslot * 128:
                      j * (QRING * 128) + (qslot + 1) * 128],
                kT_sb[lo:lo + 64, j * (KRING * 128) + rc0:
                      j * (KRING * 128) + rc0 + (w1 - w0)],
                start=True, stop=True),
                [b_q[qslot]] + wbufs))
        p.group("tensor", pieces, [b_q[qslot]] + wbufs, [b_s[pi]])

    def attn_stage_Bmax(qt, h):
        pi = h % 2
        ps = ps_s[pi]
        nm = negm_sb[:, pi:pi + 1]
        p.emit("vector", lambda e, ps=ps, nm=nm: e.reduce_max(
            nm, ps[:, 0:WW], axis=AX, negate=True),
            reads=[b_s[pi]], writes=[b_negm[pi]])

    def attn_stage_Bexp(qt, h):
        ms = mask_slot(qt)
        mask = masks_sb[:, ms * WW:(ms + 1) * WW]
        pi = h % 2
        ps = ps_s[pi]
        nm = negm_sb[:, pi:pi + 1]
        pe = pe_sb[pi]
        ph = ph_sb[h % 4]
        p.emit("scalar", lambda e, ps=ps, pe=pe, nm=nm: e.activation(
            pe[:, 0:WW], ps[:, 0:WW], ACT.Exp, bias=nm),
            reads=[b_s[pi], b_negm[pi]], writes=[b_pe[pi]])
        p.emit("gpsimd", lambda e, pe=pe, ph=ph, mask=mask: e.tensor_tensor(
            ph[:, 0:WW], pe[:, 0:WW], mask[:, 0:WW], op=ALU.mult),
            reads=[b_pe[pi], b_masks], writes=[b_ph[h % 4]])

    def attn_stage_C(qt, h):
        pi = h % 2
        ph = ph_sb[h % 4]
        pt = pt_sb[pi]
        pieces = []
        for c in range(4):
            pieces.append((lambda e, ph=ph, c=c: e.matmul(
                ps_ptA[:, c * 128:(c + 1) * 128],
                ph[:, c * 128:(c + 1) * 128], ident_sb[:],
                start=True, stop=True),
                [b_ph[h % 4], b_const]))
        p.group("tensor", pieces, [b_ph[h % 4], b_const], [b_ptA])
        p.emit("tensor", lambda e, ph=ph, pi=pi: e.matmul(
            ps_m[pi][:, 0:128], ph[:, 512:640], ident_sb[:],
            start=True, stop=True),
            reads=[b_ph[h % 4], b_const], writes=[b_m[pi]])
        p.emit("scalar", lambda e, pt=pt: e.copy(pt[:, 0:512], ps_ptA[:, 0:512]),
               reads=[b_ptA], writes=[b_pt[pi]])
        p.emit("scalar", lambda e, pt=pt, pi=pi: e.copy(pt[:, 512:WW], ps_m[pi][:, 0:128]),
               reads=[b_m[pi]], writes=[b_pt[pi]])

    def attn_stage_D(qt, h):
        pi = h % 2
        pt = pt_sb[pi]
        vbufs = [b_v[(qt + i) % KRING] for i in range(5)]
        pieces = []
        for c in range(5):
            slot = (qt + c) % KRING
            pieces.append((lambda e, pt=pt, c=c, h=h, slot=slot, pi=pi: e.matmul(
                ps_m[pi][0:65, 128:256],
                v_sb[:, slot * H * 65 + h * 65:slot * H * 65 + (h + 1) * 65],
                pt[:, c * 128:(c + 1) * 128],
                start=(c == 0), stop=(c == 4)),
                [b_pt[pi]] + vbufs))
        p.group("tensor", pieces, [b_pt[pi]] + vbufs, [b_m[pi]])

    def attn_stage_Et(qt, h):
        pi = h % 2
        cp = cpair_sb[0:65, pi * 128:(pi + 1) * 128]
        p.emit("vector", lambda e, cp=cp, pi=pi: e.tensor_copy(cp, ps_m[pi][0:65, 128:256]),
               reads=[b_m[pi]], writes=[b_cp[pi]])
        p.emit("tensor", lambda e, cp=cp, pi=pi: e.matmul(
            ps_m[pi][:, 256:321], cp, ident_sb[0:65, 0:65], start=True, stop=True),
            reads=[b_cp[pi], b_const], writes=[b_m[pi]])

    def attn_stage_Ev(qt, h):
        pi = h % 2
        p.emit("vector", lambda e, h=h, pi=pi: e.reciprocal(
            rl_sb[:, h:h + 1], ps_m[pi][:, 320:321]),
            reads=[b_m[pi]], writes=[b_rl])
        p.emit("vector", lambda e, h=h, qt=qt, pi=pi: e.tensor_scalar(
            ctx_sb[:, (qt % 2) * E + h * 64:(qt % 2) * E + (h + 1) * 64],
            ps_m[pi][:, 256:320],
            rl_sb[:, h:h + 1], None, op0=ALU.mult),
            reads=[b_m[pi], b_rl], writes=[b_ctx[qt % 2]])

    def attn_finish(qt):
        cs = ctx_sb[:, (qt % 2) * E:(qt % 2 + 1) * E]
        p.dma("gpsimd", "d1w", d1_d[qt * 128:(qt + 1) * 128, :], cs,
              reads=[b_ctx[qt % 2]], writes=[b_d1[qt]])

    def x_dma(t):
        p.dma("sync", f"x{t % XRING}", x_sb[:, (t % XRING) * E:(t % XRING + 1) * E],
              x_d[t * 128:(t + 1) * 128, :], writes=[b_x[t % XRING]])

    def proj_units(g):
        us = [(lambda t=t: tile_ln_t1(t)) for t in range(4 * g, 4 * g + 4)]
        us += group_proj_units(g)
        return us

    cell_state = {"cells": [], "i": 0, "a": 0, "mx": 0, "em": 0}

    def _adv_a(limit):
        cs = cell_state["cells"]
        while cell_state["a"] <= limit and cell_state["a"] < len(cs):
            attn_stage_A(*cs[cell_state["a"]])
            cell_state["a"] += 1

    def _adv_mx(limit):
        cs = cell_state["cells"]
        while cell_state["mx"] <= limit and cell_state["mx"] < len(cs):
            _adv_a(cell_state["mx"])
            attn_stage_Bmax(*cs[cell_state["mx"]])
            cell_state["mx"] += 1

    def _adv_em(limit):
        cs = cell_state["cells"]
        while cell_state["em"] <= limit and cell_state["em"] < len(cs):
            _adv_mx(cell_state["em"])
            attn_stage_Bexp(*cs[cell_state["em"]])
            cell_state["em"] += 1

    def cell_unit():
        cs = cell_state["cells"]
        i = cell_state["i"]
        _adv_em(i + 2)
        if i >= 2:
            attn_stage_Ev(*cs[i - 2])
            if cs[i - 2][1] == H - 1:
                attn_finish(cs[i - 2][0])
        attn_stage_C(*cs[i])
        _adv_mx(i + 3)
        _adv_a(i + 4)
        if i >= 1:
            attn_stage_D(*cs[i - 1])
            attn_stage_Et(*cs[i - 1])
        cell_state["i"] = i + 1

    def cell_push_and_emit(new_cells, tu):
        cell_state["cells"].extend(new_cells)
        na, nb = len(new_cells), len(tu)
        ia = ib = 0
        while ia < na or ib < nb:
            if ia < na:
                cell_unit()
                ia += 1
            while ib < nb and (na == 0 or ib * na <= ia * nb):
                tu[ib]()
                ib += 1

    def cell_drain():
        cs = cell_state["cells"]
        n = len(cs)

        def ev(j):
            attn_stage_Ev(*cs[j])
            if cs[j][1] == H - 1:
                attn_finish(cs[j][0])
        ev(n - 2)
        attn_stage_D(*cs[n - 1])
        attn_stage_Et(*cs[n - 1])
        ev(n - 1)

    for t in range(0, 4):
        x_dma(t)
    for u in proj_units(0):
        u()
    for g in range(NG):
        new_cells = []
        for qt in range(max(0, 4 * g - 4), min(NQT, 4 * g)):
            for h in range(H):
                new_cells.append((qt, h))
        tu = []
        if g + 1 < NG:
            for t in range(4 * (g + 1), 4 * (g + 1) + 4):
                x_dma(t)
            tu = proj_units(g + 1)
        cell_push_and_emit(new_cells, tu)
    cell_drain()

    p.flush()
    stack1.close()

    p.barrier()

    w1r_sb = nc.alloc_sbuf_tensor("w1r_sb", [128, W1RING * NCH * 512], BF16, side="right")
    w2r_sb = nc.alloc_sbuf_tensor("w2r_sb", [128, W2RING * 32 * 128], BF16, side="right")
    x2_sb = nc.alloc_sbuf_tensor("x2_sb", [128, X2RING * E], F32, side="right")
    d1l_sb = nc.alloc_sbuf_tensor("d1l_sb", [128, D1RING * E], F32, side="right")
    xn2_sb = nc.alloc_sbuf_tensor("xn2_sb", [128, 2 * E], BF16, side="right")
    xn2T_sb = nc.alloc_sbuf_tensor("xn2T_sb", [128, NCH * ST_TOK], BF16, side="right")
    h_sb = nc.alloc_sbuf_tensor("h_sb", [128, 32 * ST_TOK], BF16, side="right")
    o2_sb = nc.alloc_sbuf_tensor("o2_sb", [128, NCH * ST_TOK], BF16, side="right")
    ffn_sb = nc.alloc_sbuf_tensor("ffn_sb", [128, 2 * E], F32, side="right")

    ps_t4a = nc.alloc_psum_tensor("ps_t4a", [128, 512], F32)
    ps_t4b = nc.alloc_psum_tensor("ps_t4b", [128, 512], F32)
    ps_h0 = nc.alloc_psum_tensor("ps_h0", [128, 1024], F32)
    ps_h1 = nc.alloc_psum_tensor("ps_h1", [128, 1024], F32)
    ps_o2 = nc.alloc_psum_tensor("ps_o2", [128, 1024], F32)

    b_w1 = [Buf(f"w1_{i}") for i in range(W1RING)]
    b_w2 = [Buf(f"w2_{i}") for i in range(W2RING)]
    b_x2 = [Buf(f"x2_{i}") for i in range(X2RING)]
    b_d1l = [Buf(f"d1l_{i}") for i in range(D1RING)]
    b_xn2 = [Buf("xn2_0"), Buf("xn2_1")]
    b_xn2T = [Buf("xn2T0"), Buf("xn2T1")]
    b_h = Buf("h")
    b_o2 = Buf("o2")
    b_ffn = [Buf("ffn0"), Buf("ffn1")]
    b_ob = [Buf("ob0"), Buf("ob1")]
    b_pt4 = [Buf("pt4a"), Buf("pt4b")]
    b_ph = [Buf(f"ph{i}") for i in range(4)]
    b_po2 = Buf("po2")

    def dma_w1(s, k4):
        slot = (s * 8 + k4) % W1RING
        ci0 = k4 * 4
        p.dma("gpsimd", f"w1_{slot}",
              w1r_sb[:, slot * NCH * 512:(slot + 1) * NCH * 512].rearrange("p (c i) -> p c i", c=NCH),
              w1_d[:, ci0 * 128:(ci0 + 4) * 128].rearrange("(c p) i -> p c i", p=128),
              writes=[b_w1[slot]])

    def dma_w2(s, eo):
        slot = (s * 8 + eo) % W2RING
        p.dma("gpsimd", f"w2_{slot}",
              w2r_sb[:, slot * 32 * 128:(slot + 1) * 32 * 128].rearrange("p (c e) -> p c e", c=32),
              w2_d[:, eo * 128:(eo + 1) * 128].rearrange("(c p) e -> p c e", p=128),
              writes=[b_w2[slot]])

    def tile_unit_a(s, tt):
        ti = s * STT + tt
        xsl = ti % X2RING
        dsl = ti % D1RING
        p.dma("sync", f"x2_{xsl}", x2_sb[:, xsl * E:(xsl + 1) * E],
              x_d[256 + ti * 128:256 + (ti + 1) * 128, :], writes=[b_x2[xsl]])
        p.dma("sync", f"d1l_{dsl}", d1l_sb[:, dsl * E:(dsl + 1) * E],
              d1_d[ti * 128:(ti + 1) * 128, :],
              reads=[b_d1[ti]], writes=[b_d1l[dsl]])
        dsv = d1l_sb[:, dsl * E:(dsl + 1) * E]
        ln_chain(dsv, b_d1l[dsl], dsv, b_d1l[dsl])

    def tile_unit(s, tt):
        ti = s * STT + tt
        xsl = ti % X2RING
        dsl = ti % D1RING
        dsv = d1l_sb[:, dsl * E:(dsl + 1) * E]
        p.emit("vector", lambda e, xsl=xsl, dsv=dsv: e.tensor_tensor(
            x2_sb[:, xsl * E:(xsl + 1) * E], x2_sb[:, xsl * E:(xsl + 1) * E],
            dsv, op=ALU.add),
            reads=[b_x2[xsl], b_d1l[dsl]], writes=[b_x2[xsl]])
        xn2 = xn2_sb[:, (tt % 2) * E:(tt % 2 + 1) * E]
        ln_chain(x2_sb[:, xsl * E:(xsl + 1) * E], b_x2[xsl], xn2, b_xn2[tt % 2])
        for hh in range(2):
            pst = ps_t4a if hh == 0 else ps_t4b
            pbb = b_pt4[hh]
            pieces = []
            for c in range(4):
                cc = hh * 4 + c
                pieces.append((lambda e, xn2=xn2, pst=pst, c=c, cc=cc: e.matmul(
                    pst[:, c * 128:(c + 1) * 128],
                    xn2[:, cc * 128:(cc + 1) * 128], ident_sb[:],
                    start=True, stop=True), [b_xn2[tt % 2], b_const]))
            p.group("tensor", pieces, [b_xn2[tt % 2], b_const], [pbb])
            p.emit("scalar", lambda e, hh=hh, tt=tt, pst=pst: e.copy(
                xn2T_sb[:].rearrange("p (c t) -> p c t", c=NCH)[:, hh * 4:(hh + 1) * 4, tt * 128:(tt + 1) * 128],
                pst[:].rearrange("p (c t) -> p c t", c=4)),
                reads=[pbb], writes=[b_xn2T[tt // 4]])

    def w1_unit(s, ci, half):
        if True:
            slot = (s * 8 + ci // 4) % W1RING
            wbase = slot * NCH * 512 + (ci % 4) * 128
            psh = ps_h0 if ci % 2 == 0 else ps_h1
            pbh = b_ph[ci % 2]
            pieces = []
            for e8 in range(NCH):
                pieces.append((lambda e, e8=e8, half=half, psh=psh, wbase=wbase: e.matmul(
                    psh[:, half * 512:(half + 1) * 512],
                    w1r_sb[:, wbase + e8 * 512:wbase + e8 * 512 + 128],
                    xn2T_sb[:, e8 * ST_TOK + half * 512:e8 * ST_TOK + (half + 1) * 512],
                    start=(e8 == 0), stop=(e8 == NCH - 1)),
                    [b_w1[slot], b_xn2T[half]]))
            p.group("tensor", pieces, [b_w1[slot], b_xn2T[half]], [pbh],
                    inc=(half == 1))
            if half == 1:
                p.emit("vector", lambda e, ci=ci, psh=psh: e.tensor_scalar(
                    h_sb[:, ci * ST_TOK:(ci + 1) * ST_TOK], psh[:],
                    b1_sb[:, ci:ci + 1], 0.0, op0=ALU.add, op1=ALU.max),
                    reads=[pbh, b_const], writes=[b_h])
                if ci % 4 == 3:
                    k4n = ci // 4 + W1RING
                    if k4n < 8:
                        dma_w1(s, k4n)
                    elif s + 1 < NST:
                        dma_w1(s + 1, k4n - 8)
                    if ci // 4 < W2RING:
                        dma_w2(s, ci // 4)
    def w2_unit(s, eo):
        if True:
            slot = (s * 8 + eo) % W2RING
            pieces = []
            for ic in range(32):
                for half in range(2):
                    pieces.append((lambda e, ic=ic, half=half, slot=slot: e.matmul(
                        ps_o2[:, half * 512:(half + 1) * 512],
                        w2r_sb[:, slot * 32 * 128 + ic * 128:slot * 32 * 128 + (ic + 1) * 128],
                        h_sb[:, ic * ST_TOK + half * 512:ic * ST_TOK + (half + 1) * 512],
                        start=(ic == 0), stop=(ic == 31)),
                        [b_w2[slot], b_h]))
            p.group("tensor", pieces, [b_w2[slot], b_h], [b_po2])
            p.emit("scalar", lambda e, eo=eo: e.activation(
                o2_sb[:, eo * ST_TOK:(eo + 1) * ST_TOK], ps_o2[:], ACT.Identity,
                bias=b2_sb[:, eo:eo + 1]),
                reads=[b_po2, b_const], writes=[b_o2])
            if eo + W2RING < 8:
                dma_w2(s, eo + W2RING)
    def t5_unit(s, tt):
        if True:
            ti = s * STT + tt
            xsl = ti % X2RING
            pieces = []
            for eo in range(NCH):
                pieces.append((lambda e, eo=eo, tt=tt: e.matmul(
                    ps_o2[:, eo * 128:(eo + 1) * 128],
                    o2_sb[:, eo * ST_TOK + tt * 128:eo * ST_TOK + (tt + 1) * 128],
                    ident_sb[:], start=True, stop=True),
                    [b_o2, b_const]))
            p.group("tensor", pieces, [b_o2, b_const], [b_po2])
            fs = ffn_sb[:, (tt % 2) * E:(tt % 2 + 1) * E]
            p.emit("scalar", lambda e, fs=fs: e.copy(fs, ps_o2[:]),
                   reads=[b_po2], writes=[b_ffn[tt % 2]])
            ln_chain(fs, b_ffn[tt % 2], fs, b_ffn[tt % 2])
            p.emit("vector", lambda e, fs=fs, xsl=xsl: e.tensor_tensor(
                fs, fs, x2_sb[:, xsl * E:(xsl + 1) * E], op=ALU.add),
                reads=[b_ffn[tt % 2], b_x2[xsl]], writes=[b_ffn[tt % 2]])
            p.dma("sync", "out", o_d[ti * 128:(ti + 1) * 128, :], fs,
                  reads=[b_ffn[tt % 2]])

    def interleave(ua, ub):
        na, nb = len(ua), len(ub)
        ia = ib = 0
        while ia < na or ib < nb:
            if ia < na:
                ua[ia]()
                ia += 1
            while ib < nb and (na == 0 or ib * na <= ia * nb):
                ub[ib]()
                ib += 1

    def tiles_and_t5_units(s, skip=0):
        seq = []
        if s > 0:
            for i in range(3):
                seq.append(lambda s=s, i=i: t5_unit(s - 1, i))
        if skip == 0:
            seq.append(lambda s=s: tile_unit_a(s, 0))
        for tt in range(STT):
            if tt + 1 < STT and tt + 1 >= skip:
                seq.append(lambda s=s, tt=tt: tile_unit_a(s, tt + 1))
            if tt >= skip:
                seq.append(lambda s=s, tt=tt: tile_unit(s, tt))
            if s > 0 and tt + 3 < STT:
                seq.append(lambda s=s, tt=tt: t5_unit(s - 1, tt + 3))
        return seq

    for k4 in range(W1RING):
        dma_w1(0, k4)
    for s in range(NST):
        for u in tiles_and_t5_units(s, 4 if s > 0 else 0):
            u()
        for ci in range(32):
            w1_unit(s, ci, 0)
            w1_unit(s, ci, 1)
        hoist = []
        if s + 1 < NST:
            hoist = [lambda s=s: tile_unit_a(s + 1, 0),
                     lambda s=s: tile_unit_a(s + 1, 1),
                     lambda s=s: tile_unit(s + 1, 0),
                     lambda s=s: tile_unit_a(s + 1, 2),
                     lambda s=s: tile_unit(s + 1, 1),
                     lambda s=s: tile_unit_a(s + 1, 3),
                     lambda s=s: tile_unit(s + 1, 2),
                     lambda s=s: tile_unit(s + 1, 3)]
        interleave([lambda s=s, eo=eo: w2_unit(s, eo) for eo in range(8)],
                   hoist)
    for tt in range(STT):
        t5_unit(NST - 1, tt)

    sem, cntv = p.dma_sems["out"]
    p.wait("sync", (sem, cntv))
    p.flush()
    return nc


_NC_CACHE = {}


def _get_nc():
    if "nc" not in _NC_CACHE:
        _NC_CACHE["nc"] = build_program()
    return _NC_CACHE["nc"]


def _make_masks(h):
    i = np.arange(128)[:, None]
    out = np.zeros((5, 128, WW), np.float32)
    for si, qt in enumerate((0, 1, 15, NQT - 2, NQT - 1)):
        j = np.arange(WW)[None, :]
        band = np.abs(j - 256 - i) <= 256
        g = h * 4096 + qt * 128 - 256 + j
        valid = (g >= 0) & (g < SEQ)
        out[si] = (band & valid).astype(np.float32)
    return out.astype(ml_dtypes.bfloat16)


def kernel(L0_attn_hidden_states, L0_attn_Wq, L0_attn_bq, L0_attn_Wk, L0_attn_bk,
           L0_attn_Wv, L0_attn_bv, L0_ffn_W1, L0_ffn_b1, L0_ffn_W2, L0_ffn_b2):
    x = np.ascontiguousarray(np.asarray(L0_attn_hidden_states, dtype=np.float32))
    wq = np.asarray(L0_attn_Wq, np.float32)
    wk = np.asarray(L0_attn_Wk, np.float32)
    wv = np.asarray(L0_attn_Wv, np.float32)
    w1 = np.ascontiguousarray(np.asarray(L0_ffn_W1, np.float32))
    w2 = np.ascontiguousarray(np.asarray(L0_ffn_W2, np.float32))
    bq = np.asarray(L0_attn_bq, np.float32)
    bk = np.asarray(L0_attn_bk, np.float32)
    bv = np.asarray(L0_attn_bv, np.float32)
    b1 = np.asarray(L0_ffn_b1, np.float32)
    b2 = np.asarray(L0_ffn_b2, np.float32)

    wqt = np.ascontiguousarray(wq.T)
    wkt = np.ascontiguousarray(wk.T)
    wvt = np.ascontiguousarray(wv.T)
    bq_l = np.ascontiguousarray(bq.reshape(8, 128).T)
    bk_l = np.ascontiguousarray(bk.reshape(8, 128).T)
    bv_l = np.ascontiguousarray(bv.reshape(1, E))
    b1_l = np.ascontiguousarray(b1.reshape(32, 128).T)
    b2_l = np.ascontiguousarray(b2.reshape(8, 128).T)
    ident = np.eye(128, dtype=ml_dtypes.bfloat16)
    masks_h = [_make_masks(0), _make_masks(1)]

    in_maps = []
    for c in range(8):
        b, h = c // 2, c % 2
        xs = np.zeros((HT, E), np.float32)
        if h == 0:
            xs[256:] = x[b, 0:HT - 256]
        else:
            xs[:HT - 256] = x[b, SEQ - (HT - 256):SEQ]
        in_maps.append({
            "x": xs, "wqt": wqt, "wkt": wkt, "wvt": wvt, "w1": w1, "w2": w2,
            "bq": bq_l, "bk": bk_l, "bv": bv_l, "b1": b1_l, "b2": b2_l,
            "masks": masks_h[h], "ident": ident,
        })

    nc = _get_nc()
    res = run_bass_kernel_spmd(nc, in_maps, core_ids=list(range(8)))
    out = np.empty((4, SEQ, E), np.float32)
    for c in range(8):
        b, h = c // 2, c % 2
        out[b, h * 4096:(h + 1) * 4096] = np.asarray(res.results[c]["o"], np.float32)
    return out


# version 24
# speedup vs baseline: 1.1599x; 1.0310x over previous
import contextlib
import numpy as np
import ml_dtypes
import concourse.bass as bass
import concourse.mybir as mybir
from concourse.bass_utils import run_bass_kernel_spmd

F32 = mybir.dt.float32
BF16 = mybir.dt.bfloat16
AX = mybir.AxisListType.X
ALU = mybir.AluOpType
ACT = mybir.ActivationFunctionType

ENGINES = ("tensor", "vector", "scalar", "sync", "gpsimd")

E = 1024
H = 16
D = 64
NCH = 8
SEQ = 8192
TOK = 4096
HT = TOK + 512
NKT = HT // 128
NQT = TOK // 128
NG = NKT // 4
WW = 640
EPS = 1e-5
INTER = 4096
ST_TOK = 1024
NST = TOK // ST_TOK
STT = ST_TOK // 128

KRING = 12
QRING = 12
XRING = 6
X2RING = 14
D1RING = 2
W1RING = 2
W2RING = 2


class Buf:
    __slots__ = ("writer", "readers", "name")

    def __init__(self, name=""):
        self.writer = None
        self.readers = []
        self.name = name


class Prog:
    def __init__(self, nc):
        self.nc = nc
        self.sem = {}
        self.cnt = {}
        self.q = {e: [] for e in ENGINES}
        for e in ENGINES:
            self.sem[e] = nc.alloc_semaphore(f"sem_{e}")
            self.cnt[e] = 0
        self.mark = {e: {} for e in ENGINES}
        self.dma_sems = {}

    def dma_sem(self, name):
        if name not in self.dma_sems:
            self.dma_sems[name] = [self.nc.alloc_semaphore(f"dsem_{name}"), 0]
        return self.dma_sems[name]

    def _needed(self, ename, reads, writes):
        toks = []
        for b in reads:
            if b.writer is not None:
                toks.append(b.writer)
        for b in writes:
            if b.writer is not None:
                toks.append(b.writer)
            toks.extend(b.readers)
        merged = {}
        for (s, v) in toks:
            if merged.get(s.name, (None, -1))[1] < v:
                merged[s.name] = (s, v)
        wm = self.mark[ename]
        return [(s, v) for nm, (s, v) in merged.items() if wm.get(nm, 0) < v]

    def _note(self, ename, s, v):
        wm = self.mark[ename]
        if wm.get(s.name, 0) < v:
            wm[s.name] = v

    def emit(self, ename, build, reads=(), writes=(), inc=True, track=True):
        needed = self._needed(ename, reads, writes)
        for (s, v) in needed:
            self._note(ename, s, v)
        tok = None
        if inc:
            self.cnt[ename] += 1
            tok = (self.sem[ename], self.cnt[ename])

        def thunk(eng, build=build, needed=needed, tok=tok):
            for (s, v) in needed[:-1]:
                eng.wait_ge(s, v)
            inst = build(eng)
            if needed:
                inst._wait_ge(*needed[-1])
            if tok is not None:
                inst.then_inc(tok[0], 1)
        self.q[ename].append(thunk)
        if tok is not None and track:
            for b in writes:
                b.writer = tok
                b.readers = []
            for b in reads:
                b.readers.append(tok)
        return tok

    def dma(self, qname, semname, out_ap, in_ap, reads=(), writes=()):
        needed = self._needed(qname, reads, writes)
        ent = self.dma_sem(semname)
        if ent[1] > 0 and self.mark[qname].get(ent[0].name, 0) < ent[1]:
            needed.append((ent[0], ent[1]))
        for (s, v) in needed:
            self._note(qname, s, v)
        ent[1] += 16
        tok = (ent[0], ent[1])

        def thunk(eng, needed=needed, tok=tok, out_ap=out_ap, in_ap=in_ap):
            for (s, v) in needed[:-1]:
                eng.wait_ge(s, v)
            inst = eng.dma_start(out_ap, in_ap)
            if needed:
                inst._wait_ge(*needed[-1])
            inst.then_inc(tok[0], 16)
        self.q[qname].append(thunk)
        for b in writes:
            b.writer = tok
            b.readers = []
        for b in reads:
            b.readers.append(tok)
        return tok

    def group(self, ename, pieces, reads_all, writes, inc=True):
        n = len(pieces)
        for i, (bf, rd) in enumerate(pieces):
            if n == 1:
                self.emit(ename, bf, reads=reads_all, writes=writes, inc=inc)
            elif i == 0:
                self.emit(ename, bf, reads=rd, writes=writes, inc=False, track=False)
            elif i == n - 1:
                self.emit(ename, bf, reads=reads_all, writes=writes, inc=inc)
            else:
                self.emit(ename, bf, reads=rd, writes=(), inc=False)

    def wait(self, ename, tok):
        if tok is None:
            return
        s, v = tok
        if self.mark[ename].get(s.name, 0) >= v:
            return
        self._note(ename, s, v)

        def thunk(eng, s=s, v=v):
            eng.wait_ge(s, v)
        self.q[ename].append(thunk)

    def barrier(self):
        toks = [(self.sem[e], self.cnt[e]) for e in ENGINES if self.cnt[e] > 0]
        toks += [(h, c) for (h, c) in self.dma_sems.values() if c > 0]
        for e in ENGINES:
            for t in toks:
                self.wait(e, t)

    def flush(self):
        nc = self.nc
        with nc.Block() as block:
            for ename in ENGINES:
                thunks = self.q[ename]
                if not thunks:
                    continue

                def section(eng, thunks=thunks):
                    for th in thunks:
                        th(eng)
                getattr(block, ename)(section)
        self.q = {e: [] for e in ENGINES}


def mask_slot(qt):
    return {0: 0, 1: 1, NQT - 2: 3, NQT - 1: 4}.get(qt, 2)


def s_pieces(qt):
    out = []
    j = 0
    while j < WW:
        slot0 = (qt + j // 128) % KRING
        run = 128
        while j + run < WW and (qt + (j + run) // 128) % KRING == slot0 + run // 128:
            run += 128
        if j < 512 < j + run:
            run = 512 - j
        out.append((j, j + run, slot0 * 128))
        j += run
    return out


def build_program():
    nc = bass.Bass("TRN2", target_bir_lowering=False)

    x_d = nc.dram_tensor("x", [HT, E], F32, kind="ExternalInput")
    wqt_d = nc.dram_tensor("wqt", [E, E], F32, kind="ExternalInput")
    wkt_d = nc.dram_tensor("wkt", [E, E], F32, kind="ExternalInput")
    wvt_d = nc.dram_tensor("wvt", [E, E], F32, kind="ExternalInput")
    w1_d = nc.dram_tensor("w1", [E, INTER], F32, kind="ExternalInput")
    w2_d = nc.dram_tensor("w2", [INTER, E], F32, kind="ExternalInput")
    bq_d = nc.dram_tensor("bq", [128, 8], F32, kind="ExternalInput")
    bk_d = nc.dram_tensor("bk", [128, 8], F32, kind="ExternalInput")
    bv_d = nc.dram_tensor("bv", [1, E], F32, kind="ExternalInput")
    b1_d = nc.dram_tensor("b1", [128, 32], F32, kind="ExternalInput")
    b2_d = nc.dram_tensor("b2", [128, 8], F32, kind="ExternalInput")
    masks_d = nc.dram_tensor("masks", [5, 128, WW], BF16, kind="ExternalInput")
    ident_d = nc.dram_tensor("ident", [128, 128], BF16, kind="ExternalInput")
    o_d = nc.dram_tensor("o", [TOK, E], F32, kind="ExternalOutput")
    d1_d = nc.dram_tensor("d1", [TOK, E], F32, kind="Internal")

    p = Prog(nc)

    ident_sb = nc.alloc_sbuf_tensor("ident_sb", [128, 128], BF16, side="left")
    bq_sb = nc.alloc_sbuf_tensor("bq_sb", [128, 8], F32, side="left")
    bk_sb = nc.alloc_sbuf_tensor("bk_sb", [128, 8], F32, side="left")
    bv_sb = nc.alloc_sbuf_tensor("bv_sb", [1, E], BF16, side="left")
    b1_sb = nc.alloc_sbuf_tensor("b1_sb", [128, 32], F32, side="left")
    b2_sb = nc.alloc_sbuf_tensor("b2_sb", [128, 8], F32, side="left")
    ones_sb = nc.alloc_sbuf_tensor("ones_sb", [1, 128], BF16, side="left")
    c18_sb = nc.alloc_sbuf_tensor("c18_sb", [128, 1], F32, side="left")
    cm05_sb = nc.alloc_sbuf_tensor("cm05_sb", [128, 1], F32, side="left")
    bqs_sb = nc.alloc_sbuf_tensor("bqs_sb", [128, 8], F32, side="left")
    st_sb = nc.alloc_sbuf_tensor("st_sb", [128, 24], F32, side="left")
    mv_sb = nc.alloc_sbuf_tensor("mv_sb", [128, 4], F32, side="left")
    aux_sb = nc.alloc_sbuf_tensor("aux_sb", [128, 8], F32, side="left")

    b_const = Buf("const")
    b_lnp = [Buf("ln0"), Buf("ln1")]
    ln_par = [0]

    p.dma("sync", "ci", ident_sb[:], ident_d[:, :], writes=[b_const])
    p.dma("sync", "ci", bq_sb[:], bq_d[:, :], writes=[b_const])
    p.dma("sync", "ci", bk_sb[:], bk_d[:, :], writes=[b_const])
    p.dma("sync", "ci", b1_sb[:], b1_d[:, :], writes=[b_const])
    p.dma("sync", "ci", b2_sb[:], b2_d[:, :], writes=[b_const])
    p.dma("gpsimd", "cg", bv_sb[:], bv_d[:, :], writes=[b_const])
    p.emit("vector", lambda e: e.memset(ones_sb[:], 1.0), reads=[b_const], writes=[b_const])
    p.emit("vector", lambda e: e.memset(c18_sb[:], 0.125), reads=[b_const], writes=[b_const])
    p.emit("vector", lambda e: e.memset(cm05_sb[:], -0.5), reads=[b_const], writes=[b_const])
    p.emit("vector", lambda e: e.tensor_scalar_mul(bqs_sb[:], bq_sb[:], 0.125),
           reads=[b_const], writes=[b_const])

    def ln_chain(src_ap, src_buf, dst_ap, dst_buf):
        par = ln_par[0]
        ln_par[0] ^= 1
        lb = b_lnp[par]
        st = st_sb[:, 12 * par:12 * par + 12]
        mv = mv_sb[:, 2 * par:2 * par + 2]
        vpe = aux_sb[:, 4 * par + 0:4 * par + 1]
        sd = aux_sb[:, 4 * par + 1:4 * par + 2]
        r = aux_sb[:, 4 * par + 2:4 * par + 3]
        nmr = aux_sb[:, 4 * par + 3:4 * par + 4]
        p.emit("vector", lambda e: e.bn_stats(st[:, 0:6], src_ap[:, 0:512]),
               reads=[src_buf], writes=[lb])
        p.emit("vector", lambda e: e.bn_stats(st[:, 6:12], src_ap[:, 512:1024]),
               reads=[src_buf, lb], writes=[lb])
        p.emit("vector", lambda e: e.bn_aggr(mv, st),
               reads=[lb], writes=[lb])
        p.emit("vector", lambda e: e.tensor_scalar_add(vpe, mv[:, 1:2], float(EPS)),
               reads=[lb], writes=[lb])
        p.emit("scalar", lambda e: e.activation(sd, vpe, ACT.Ln),
               reads=[lb], writes=[lb])
        p.emit("scalar", lambda e: e.activation(r, sd, ACT.Exp, scale=cm05_sb[:, 0:1]),
               reads=[lb, b_const], writes=[lb])
        p.emit("vector", lambda e: e.scalar_tensor_tensor(
            nmr, mv[:, 0:1], -1.0, r, op0=ALU.mult, op1=ALU.mult),
            reads=[lb], writes=[lb])
        p.emit("scalar", lambda e: e.activation(
            dst_ap, src_ap, ACT.Identity, bias=nmr, scale=r),
            reads=[lb, src_buf], writes=[dst_buf])

    stack1 = contextlib.ExitStack()

    def sb1(name, shape, dtype):
        return stack1.enter_context(nc.sbuf_tensor(name, shape, dtype, side="right"))

    def ps1(name, shape):
        return stack1.enter_context(nc.psum_tensor(name, shape, F32))

    wq_sb = sb1("wq_sb", [128, NCH * E], BF16)
    wk_sb = sb1("wk_sb", [128, NCH * E], BF16)
    wv_sb = sb1("wv_sb", [128, NCH * E], BF16)
    masks_sb = sb1("masks_sb", [128, 5 * WW], BF16)
    x_sb = sb1("x_sb", [128, XRING * E], F32)
    xn_sb = sb1("xn_sb", [128, 2 * E], BF16)
    xnT_sb = sb1("xnT_sb", [128, NCH * 512], BF16)
    qT_sb = sb1("qT_sb", [128, NCH * QRING * 128], BF16)
    kT_sb = sb1("kT_sb", [128, NCH * KRING * 128], BF16)
    v_sb = sb1("v_sb", [128, KRING * H * 65], BF16)
    pe_sb = [sb1(f"pe_sb{i}", [128, WW], BF16) for i in range(2)]
    ph_sb = [sb1(f"ph_sb{i}", [128, WW], BF16) for i in range(4)]
    pt_sb = [sb1(f"pt_sb{i}", [128, WW], BF16) for i in range(2)]
    cpair_sb = sb1("cpair_sb", [128, 256], BF16)
    negm_sb = sb1("negm_sb", [128, 2], F32)
    rl_sb = sb1("rl_sb", [128, H], F32)
    ctx_sb = sb1("ctx_sb", [128, 2 * E], F32)

    ps_pj0 = ps1("ps_pj0", [128, 512])
    ps_s = [ps1(f"ps_s{i}", [128, WW]) for i in range(2)]
    ps_ptA = ps1("ps_ptA", [128, 512])
    ps_m = [ps1(f"ps_m{i}", [128, 512]) for i in range(2)]

    b_w = Buf("w")
    b_masks = Buf("masks")
    b_x = [Buf(f"x{i}") for i in range(XRING)]
    b_xn = [Buf(f"xn{i}") for i in range(2)]
    b_xnT = Buf("xnT")
    b_q = [Buf(f"q{i}") for i in range(QRING)]
    b_k = [Buf(f"k{i}") for i in range(KRING)]
    b_v = [Buf(f"v{i}") for i in range(KRING)]
    b_pj = [Buf("pj0"), Buf("pj1")]
    b_s = [Buf("s0"), Buf("s1")]
    b_ptA = Buf("ptA")
    b_m = [Buf("m0"), Buf("m1")]
    b_pe = [Buf("pe0"), Buf("pe1")]
    b_ph = [Buf(f"ph{i}") for i in range(4)]
    b_pt = [Buf("pt0"), Buf("pt1")]
    b_cp = [Buf("cp0"), Buf("cp1")]
    b_negm = [Buf("negm0"), Buf("negm1")]
    b_rl = Buf("rl")
    b_ctx = [Buf("ctx0"), Buf("ctx1")]
    b_d1 = [Buf(f"d1_{i}") for i in range(NQT)]

    p.dma("gpsimd", "cg", wq_sb[:].rearrange("p (c e) -> p c e", c=NCH),
          wqt_d[:, :].rearrange("(c p) e -> p c e", p=128), writes=[b_w])
    p.dma("gpsimd", "cg", wk_sb[:].rearrange("p (c e) -> p c e", c=NCH),
          wkt_d[:, :].rearrange("(c p) e -> p c e", p=128), writes=[b_w])
    p.dma("gpsimd", "cg", wv_sb[:].rearrange("p (c e) -> p c e", c=NCH),
          wvt_d[:, :].rearrange("(c p) e -> p c e", p=128), writes=[b_w])
    p.dma("sync", "ci", masks_sb[:].rearrange("p (m w) -> p m w", m=5),
          masks_d[:, :, :].rearrange("m p w -> p m w"), writes=[b_masks])

    for slot in range(KRING):
        p.emit("vector", lambda e, slot=slot: e.memset(
            v_sb[:, slot * H * 65:(slot + 1) * H * 65]
            .rearrange("p (h d) -> p h d", d=65)[:, :, 64:65], 1.0),
            reads=[b_const], writes=[b_v[slot]])

    def next_pj():
        return (ps_pj0, b_pj[0])

    def tile_ln_t1(t):
        xs = x_sb[:, (t % XRING) * E:(t % XRING + 1) * E]
        xn = xn_sb[:, (t % 2) * E:(t % 2 + 1) * E]
        ln_chain(xs, b_x[t % XRING], xn, b_xn[t % 2])
        for hh in range(2):
            pst, pbb = next_pj()
            pieces = []
            for c in range(4):
                cc = hh * 4 + c
                pieces.append((lambda e, xn=xn, pst=pst, c=c, cc=cc: e.matmul(
                    pst[:, c * 128:(c + 1) * 128],
                    xn[:, cc * 128:(cc + 1) * 128], ident_sb[:],
                    start=True, stop=True), [b_xn[t % 2], b_const]))
            p.group("tensor", pieces, [b_xn[t % 2], b_const], [pbb])
            p.emit("scalar", lambda e, hh=hh, t=t, pst=pst: e.copy(
                xnT_sb[:].rearrange("p (c t) -> p c t", c=NCH)[:, hh * 4:(hh + 1) * 4, (t % 4) * 128:(t % 4 + 1) * 128],
                pst[:].rearrange("p (c t) -> p c t", c=4)),
                reads=[pbb], writes=[b_xnT])

    def group_proj_units(g):
        us = []
        t0 = 4 * g
        qts = [t - 2 for t in range(t0, t0 + 4) if 2 <= t < 2 + NQT]
        kc0 = (t0 % KRING) * 128
        def k_chunk(c):
            pst, pbb = next_pj()
            pieces = []
            for e8 in range(NCH):
                pieces.append((lambda e, c=c, e8=e8, pst=pst: e.matmul(
                    pst[:, 0:512],
                    wk_sb[:, e8 * E + c * 128:e8 * E + (c + 1) * 128],
                    xnT_sb[:, e8 * 512:(e8 + 1) * 512],
                    start=(e8 == 0), stop=(e8 == NCH - 1)),
                    [b_w, b_xnT]))
            kbufs = [b_k[(t0 + i) % KRING] for i in range(4)]
            p.group("tensor", pieces, [b_w, b_xnT], [pbb])

            def ev():
                p.emit("scalar", lambda e, c=c, kc0=kc0, pst=pst: e.activation(
                    kT_sb[:, c * (KRING * 128) + kc0:c * (KRING * 128) + kc0 + 512],
                    pst[:, 0:512], ACT.Identity, bias=bk_sb[:, c:c + 1]),
                    reads=[pbb, b_const], writes=kbufs)
            return ev
        def q_chunk(c):
            if not qts:
                return None
            if True:
                qn = len(qts) * 128
                xoff = ((qts[0] + 2) % 4) * 128
                qc0 = ((qts[0] + 2) % QRING) * 128
                pst, pbb = next_pj()
                pieces = []
                for e8 in range(NCH):
                    pieces.append((lambda e, c=c, e8=e8, xoff=xoff, qn=qn, pst=pst: e.matmul(
                        pst[:, 0:qn],
                        wq_sb[:, e8 * E + c * 128:e8 * E + (c + 1) * 128],
                        xnT_sb[:, e8 * 512 + xoff:e8 * 512 + xoff + qn],
                        start=(e8 == 0), stop=(e8 == NCH - 1)),
                        [b_w, b_xnT]))
                qbufs = [b_q[(qt + 2) % QRING] for qt in qts]
                p.group("tensor", pieces, [b_w, b_xnT], [pbb])

                def ev():
                    p.emit("scalar", lambda e, c=c, qc0=qc0, qn=qn, pst=pst: e.activation(
                        qT_sb[:, c * (QRING * 128) + qc0:c * (QRING * 128) + qc0 + qn],
                        pst[:, 0:qn], ACT.Identity, bias=bqs_sb[:, c:c + 1],
                        scale=c18_sb[:, 0:1]),
                        reads=[pbb, b_const], writes=qbufs)
                return ev
        def v_half(i, half):
            slot = (t0 + i) % KRING
            if True:
                pst, pbb = next_pj()
                pieces = [(lambda e, half=half, pst=pst: e.matmul(
                    pst[:, 0:512], ones_sb[:, 0:128],
                    bv_sb[:, half * 512:(half + 1) * 512],
                    start=True, stop=False), [b_const])]
                for e8 in range(NCH):
                    pieces.append((lambda e, i=i, half=half, e8=e8, pst=pst: e.matmul(
                        pst[:, 0:512],
                        xnT_sb[:, e8 * 512 + i * 128:e8 * 512 + (i + 1) * 128],
                        wv_sb[:, e8 * E + half * 512:e8 * E + (half + 1) * 512],
                        start=False, stop=(e8 == NCH - 1)),
                        [b_w, b_xnT, b_const]))
                p.group("tensor", pieces, [b_w, b_xnT, b_const], [pbb])

                def vev(e, slot=slot, half=half, pst=pst):
                    base = slot * H * 65 + half * 8 * 65
                    return e.copy(
                        v_sb[:, base:base + 8 * 65].rearrange(
                            "p (h d) -> p h d", d=65)[:, :, 0:64],
                        pst[:, 0:512].rearrange("p (h d) -> p h d", d=64))

                def ev():
                    p.emit("scalar", vev, reads=[pbb], writes=[b_v[slot]])
                return ev

        for c in range(NCH):
            us.append(lambda c=c: k_chunk(c))
            us.append(lambda c=c: q_chunk(c))
        for i in range(4):
            for half in range(2):
                us.append(lambda i=i, half=half: v_half(i, half))
        stag = {"pend": None}
        out = []

        def wrap(u):
            def run(u=u):
                if stag["pend"] is not None:
                    stag["pend"]()
                    stag["pend"] = None
                r = u()
                stag["pend"] = r if callable(r) else None
            return run

        def final():
            if stag["pend"] is not None:
                stag["pend"]()
                stag["pend"] = None
        return [wrap(u) for u in us] + [final]

    def attn_stage_A(qt, h):
        qslot = (qt + 2) % QRING
        wbufs = [b_k[(qt + i) % KRING] for i in range(5)]
        pcs = s_pieces(qt)
        j = h // 2
        lo = (h % 2) * 64
        pi = h % 2
        ps = ps_s[pi]
        pieces = []
        for (w0, w1, rc0) in pcs:
            pieces.append((lambda e, ps=ps, lo=lo, j=j, w0=w0, w1=w1, rc0=rc0, qslot=qslot: e.matmul(
                ps[:, w0:w1],
                qT_sb[lo:lo + 64, j * (QRING * 128) + qslot * 128:
                      j * (QRING * 128) + (qslot + 1) * 128],
                kT_sb[lo:lo + 64, j * (KRING * 128) + rc0:
                      j * (KRING * 128) + rc0 + (w1 - w0)],
                start=True, stop=True),
                [b_q[qslot]] + wbufs))
        p.group("tensor", pieces, [b_q[qslot]] + wbufs, [b_s[pi]])

    def attn_stage_Bmax(qt, h):
        pi = h % 2
        ps = ps_s[pi]
        nm = negm_sb[:, pi:pi + 1]
        p.emit("vector", lambda e, ps=ps, nm=nm: e.reduce_max(
            nm, ps[:, 0:WW], axis=AX, negate=True),
            reads=[b_s[pi]], writes=[b_negm[pi]])

    def attn_stage_Bexp(qt, h):
        ms = mask_slot(qt)
        mask = masks_sb[:, ms * WW:(ms + 1) * WW]
        pi = h % 2
        ps = ps_s[pi]
        nm = negm_sb[:, pi:pi + 1]
        pe = pe_sb[pi]
        ph = ph_sb[h % 4]
        p.emit("scalar", lambda e, ps=ps, pe=pe, nm=nm: e.activation(
            pe[:, 0:WW], ps[:, 0:WW], ACT.Exp, bias=nm),
            reads=[b_s[pi], b_negm[pi]], writes=[b_pe[pi]])
        p.emit("gpsimd", lambda e, pe=pe, ph=ph, mask=mask: e.tensor_tensor(
            ph[:, 0:WW], pe[:, 0:WW], mask[:, 0:WW], op=ALU.mult),
            reads=[b_pe[pi], b_masks], writes=[b_ph[h % 4]])

    def attn_stage_C(qt, h):
        pi = h % 2
        ph = ph_sb[h % 4]
        pt = pt_sb[pi]
        pieces = []
        for c in range(4):
            pieces.append((lambda e, ph=ph, c=c: e.matmul(
                ps_ptA[:, c * 128:(c + 1) * 128],
                ph[:, c * 128:(c + 1) * 128], ident_sb[:],
                start=True, stop=True),
                [b_ph[h % 4], b_const]))
        p.group("tensor", pieces, [b_ph[h % 4], b_const], [b_ptA])
        p.emit("tensor", lambda e, ph=ph, pi=pi: e.matmul(
            ps_m[pi][:, 0:128], ph[:, 512:640], ident_sb[:],
            start=True, stop=True),
            reads=[b_ph[h % 4], b_const], writes=[b_m[pi]])
        p.emit("scalar", lambda e, pt=pt: e.copy(pt[:, 0:512], ps_ptA[:, 0:512]),
               reads=[b_ptA], writes=[b_pt[pi]])
        p.emit("scalar", lambda e, pt=pt, pi=pi: e.copy(pt[:, 512:WW], ps_m[pi][:, 0:128]),
               reads=[b_m[pi]], writes=[b_pt[pi]])

    def attn_stage_D(qt, h):
        pi = h % 2
        pt = pt_sb[pi]
        vbufs = [b_v[(qt + i) % KRING] for i in range(5)]
        pieces = []
        for c in range(5):
            slot = (qt + c) % KRING
            pieces.append((lambda e, pt=pt, c=c, h=h, slot=slot, pi=pi: e.matmul(
                ps_m[pi][0:65, 128:256],
                v_sb[:, slot * H * 65 + h * 65:slot * H * 65 + (h + 1) * 65],
                pt[:, c * 128:(c + 1) * 128],
                start=(c == 0), stop=(c == 4)),
                [b_pt[pi]] + vbufs))
        p.group("tensor", pieces, [b_pt[pi]] + vbufs, [b_m[pi]])

    def attn_stage_Et(qt, h):
        pi = h % 2
        cp = cpair_sb[0:65, pi * 128:(pi + 1) * 128]
        p.emit("vector", lambda e, cp=cp, pi=pi: e.tensor_copy(cp, ps_m[pi][0:65, 128:256]),
               reads=[b_m[pi]], writes=[b_cp[pi]])
        p.emit("tensor", lambda e, cp=cp, pi=pi: e.matmul(
            ps_m[pi][:, 256:321], cp, ident_sb[0:65, 0:65], start=True, stop=True),
            reads=[b_cp[pi], b_const], writes=[b_m[pi]])

    def attn_stage_Ev(qt, h):
        pi = h % 2
        p.emit("vector", lambda e, h=h, pi=pi: e.reciprocal(
            rl_sb[:, h:h + 1], ps_m[pi][:, 320:321]),
            reads=[b_m[pi]], writes=[b_rl])
        p.emit("vector", lambda e, h=h, qt=qt, pi=pi: e.tensor_scalar(
            ctx_sb[:, (qt % 2) * E + h * 64:(qt % 2) * E + (h + 1) * 64],
            ps_m[pi][:, 256:320],
            rl_sb[:, h:h + 1], None, op0=ALU.mult),
            reads=[b_m[pi], b_rl], writes=[b_ctx[qt % 2]])

    def attn_finish(qt):
        cs = ctx_sb[:, (qt % 2) * E:(qt % 2 + 1) * E]
        p.dma("gpsimd", "d1w", d1_d[qt * 128:(qt + 1) * 128, :], cs,
              reads=[b_ctx[qt % 2]], writes=[b_d1[qt]])

    def x_dma(t):
        p.dma("sync", f"x{t % XRING}", x_sb[:, (t % XRING) * E:(t % XRING + 1) * E],
              x_d[t * 128:(t + 1) * 128, :], writes=[b_x[t % XRING]])

    def proj_units(g):
        us = [(lambda t=t: tile_ln_t1(t)) for t in range(4 * g, 4 * g + 4)]
        us += group_proj_units(g)
        return us

    cell_state = {"cells": [], "i": 0, "a": 0, "mx": 0, "em": 0}

    def _adv_a(limit):
        cs = cell_state["cells"]
        while cell_state["a"] <= limit and cell_state["a"] < len(cs):
            attn_stage_A(*cs[cell_state["a"]])
            cell_state["a"] += 1

    def _adv_mx(limit):
        cs = cell_state["cells"]
        while cell_state["mx"] <= limit and cell_state["mx"] < len(cs):
            _adv_a(cell_state["mx"])
            attn_stage_Bmax(*cs[cell_state["mx"]])
            cell_state["mx"] += 1

    def _adv_em(limit):
        cs = cell_state["cells"]
        while cell_state["em"] <= limit and cell_state["em"] < len(cs):
            _adv_mx(cell_state["em"])
            attn_stage_Bexp(*cs[cell_state["em"]])
            cell_state["em"] += 1

    def cell_unit():
        cs = cell_state["cells"]
        i = cell_state["i"]
        _adv_em(i + 2)
        if i >= 2:
            attn_stage_Ev(*cs[i - 2])
            if cs[i - 2][1] == H - 1:
                attn_finish(cs[i - 2][0])
        attn_stage_C(*cs[i])
        _adv_mx(i + 3)
        _adv_a(i + 4)
        if i >= 1:
            attn_stage_D(*cs[i - 1])
            attn_stage_Et(*cs[i - 1])
        cell_state["i"] = i + 1

    def cell_push_and_emit(new_cells, tu):
        cell_state["cells"].extend(new_cells)
        na, nb = len(new_cells), len(tu)
        ia = ib = 0
        while ia < na or ib < nb:
            if ia < na:
                cell_unit()
                ia += 1
            while ib < nb and (na == 0 or ib * na <= ia * nb):
                tu[ib]()
                ib += 1

    def cell_drain():
        cs = cell_state["cells"]
        n = len(cs)

        def ev(j):
            attn_stage_Ev(*cs[j])
            if cs[j][1] == H - 1:
                attn_finish(cs[j][0])
        ev(n - 2)
        attn_stage_D(*cs[n - 1])
        attn_stage_Et(*cs[n - 1])
        ev(n - 1)

    for t in range(0, 4):
        x_dma(t)
    for u in proj_units(0):
        u()
    for g in range(NG):
        new_cells = []
        for qt in range(max(0, 4 * g - 4), min(NQT, 4 * g)):
            for h in range(H):
                new_cells.append((qt, h))
        tu = []
        if g + 1 < NG:
            for t in range(4 * (g + 1), 4 * (g + 1) + 4):
                x_dma(t)
            tu = proj_units(g + 1)
        cell_push_and_emit(new_cells, tu)
    cell_drain()

    p.flush()
    stack1.close()

    p.barrier()

    w1r_sb = nc.alloc_sbuf_tensor("w1r_sb", [128, W1RING * NCH * 512], BF16, side="right")
    w2r_sb = nc.alloc_sbuf_tensor("w2r_sb", [128, W2RING * 32 * 128], BF16, side="right")
    x2_sb = nc.alloc_sbuf_tensor("x2_sb", [128, X2RING * E], F32, side="right")
    d1l_sb = nc.alloc_sbuf_tensor("d1l_sb", [128, D1RING * E], F32, side="right")
    xn2_sb = nc.alloc_sbuf_tensor("xn2_sb", [128, 2 * E], BF16, side="right")
    xn2T_sb = nc.alloc_sbuf_tensor("xn2T_sb", [128, NCH * ST_TOK], BF16, side="right")
    h_sb = nc.alloc_sbuf_tensor("h_sb", [128, 32 * ST_TOK], BF16, side="right")
    o2_sb = nc.alloc_sbuf_tensor("o2_sb", [128, NCH * ST_TOK], BF16, side="right")
    ffn_sb = nc.alloc_sbuf_tensor("ffn_sb", [128, 2 * E], F32, side="right")

    ps_t4a = nc.alloc_psum_tensor("ps_t4a", [128, 512], F32)
    ps_t4b = nc.alloc_psum_tensor("ps_t4b", [128, 512], F32)
    ps_h0 = nc.alloc_psum_tensor("ps_h0", [128, 1024], F32)
    ps_h1 = nc.alloc_psum_tensor("ps_h1", [128, 1024], F32)
    ps_o2 = nc.alloc_psum_tensor("ps_o2", [128, 1024], F32)

    b_w1 = [Buf(f"w1_{i}") for i in range(W1RING)]
    b_w2 = [Buf(f"w2_{i}") for i in range(W2RING)]
    b_x2 = [Buf(f"x2_{i}") for i in range(X2RING)]
    b_d1l = [Buf(f"d1l_{i}") for i in range(D1RING)]
    b_xn2 = [Buf("xn2_0"), Buf("xn2_1")]
    b_xn2T = [Buf("xn2T0"), Buf("xn2T1")]
    b_h = Buf("h")
    b_o2 = Buf("o2")
    b_ffn = [Buf("ffn0"), Buf("ffn1")]
    b_ob = [Buf("ob0"), Buf("ob1")]
    b_pt4 = [Buf("pt4a"), Buf("pt4b")]
    b_ph = [Buf(f"ph{i}") for i in range(4)]
    b_po2 = Buf("po2")

    def dma_w1(s, k4):
        slot = (s * 8 + k4) % W1RING
        ci0 = k4 * 4
        p.dma("gpsimd", f"w1_{slot}",
              w1r_sb[:, slot * NCH * 512:(slot + 1) * NCH * 512].rearrange("p (c i) -> p c i", c=NCH),
              w1_d[:, ci0 * 128:(ci0 + 4) * 128].rearrange("(c p) i -> p c i", p=128),
              writes=[b_w1[slot]])

    def dma_w2(s, eo):
        slot = (s * 8 + eo) % W2RING
        p.dma("gpsimd", f"w2_{slot}",
              w2r_sb[:, slot * 32 * 128:(slot + 1) * 32 * 128].rearrange("p (c e) -> p c e", c=32),
              w2_d[:, eo * 128:(eo + 1) * 128].rearrange("(c p) e -> p c e", p=128),
              writes=[b_w2[slot]])

    def tile_unit_a(s, tt):
        ti = s * STT + tt
        xsl = ti % X2RING
        dsl = ti % D1RING
        p.dma("sync", f"x2_{xsl}", x2_sb[:, xsl * E:(xsl + 1) * E],
              x_d[256 + ti * 128:256 + (ti + 1) * 128, :], writes=[b_x2[xsl]])
        p.dma("sync", f"d1l_{dsl}", d1l_sb[:, dsl * E:(dsl + 1) * E],
              d1_d[ti * 128:(ti + 1) * 128, :],
              reads=[b_d1[ti]], writes=[b_d1l[dsl]])
        dsv = d1l_sb[:, dsl * E:(dsl + 1) * E]
        ln_chain(dsv, b_d1l[dsl], dsv, b_d1l[dsl])

    def tile_unit(s, tt):
        ti = s * STT + tt
        xsl = ti % X2RING
        dsl = ti % D1RING
        dsv = d1l_sb[:, dsl * E:(dsl + 1) * E]
        p.emit("vector", lambda e, xsl=xsl, dsv=dsv: e.tensor_tensor(
            x2_sb[:, xsl * E:(xsl + 1) * E], x2_sb[:, xsl * E:(xsl + 1) * E],
            dsv, op=ALU.add),
            reads=[b_x2[xsl], b_d1l[dsl]], writes=[b_x2[xsl]])
        xn2 = xn2_sb[:, (tt % 2) * E:(tt % 2 + 1) * E]
        ln_chain(x2_sb[:, xsl * E:(xsl + 1) * E], b_x2[xsl], xn2, b_xn2[tt % 2])
        for hh in range(2):
            pst = ps_t4a if hh == 0 else ps_t4b
            pbb = b_pt4[hh]
            pieces = []
            for c in range(4):
                cc = hh * 4 + c
                pieces.append((lambda e, xn2=xn2, pst=pst, c=c, cc=cc: e.matmul(
                    pst[:, c * 128:(c + 1) * 128],
                    xn2[:, cc * 128:(cc + 1) * 128], ident_sb[:],
                    start=True, stop=True), [b_xn2[tt % 2], b_const]))
            p.group("tensor", pieces, [b_xn2[tt % 2], b_const], [pbb])
            p.emit("scalar", lambda e, hh=hh, tt=tt, pst=pst: e.copy(
                xn2T_sb[:].rearrange("p (c t) -> p c t", c=NCH)[:, hh * 4:(hh + 1) * 4, tt * 128:(tt + 1) * 128],
                pst[:].rearrange("p (c t) -> p c t", c=4)),
                reads=[pbb], writes=[b_xn2T[tt // 4]])

    def w1_unit(s, ci, half):
        if True:
            slot = (s * 8 + ci // 4) % W1RING
            wbase = slot * NCH * 512 + (ci % 4) * 128
            psh = ps_h0 if ci % 2 == 0 else ps_h1
            pbh = b_ph[ci % 2]
            pieces = []
            for e8 in range(NCH):
                pieces.append((lambda e, e8=e8, half=half, psh=psh, wbase=wbase: e.matmul(
                    psh[:, half * 512:(half + 1) * 512],
                    w1r_sb[:, wbase + e8 * 512:wbase + e8 * 512 + 128],
                    xn2T_sb[:, e8 * ST_TOK + half * 512:e8 * ST_TOK + (half + 1) * 512],
                    start=(e8 == 0), stop=(e8 == NCH - 1)),
                    [b_w1[slot], b_xn2T[half]]))
            p.group("tensor", pieces, [b_w1[slot], b_xn2T[half]], [pbh],
                    inc=(half == 1))
            if half == 1:
                p.emit("vector", lambda e, ci=ci, psh=psh: e.tensor_scalar(
                    h_sb[:, ci * ST_TOK:(ci + 1) * ST_TOK], psh[:],
                    b1_sb[:, ci:ci + 1], 0.0, op0=ALU.add, op1=ALU.max),
                    reads=[pbh, b_const], writes=[b_h])
                if ci % 4 == 3:
                    k4n = ci // 4 + W1RING
                    if k4n < 8:
                        dma_w1(s, k4n)
                    elif s + 1 < NST:
                        dma_w1(s + 1, k4n - 8)
                    if ci // 4 < W2RING:
                        dma_w2(s, ci // 4)
    def w2_unit(s, eo):
        if True:
            slot = (s * 8 + eo) % W2RING
            pieces = []
            for ic in range(32):
                for half in range(2):
                    pieces.append((lambda e, ic=ic, half=half, slot=slot: e.matmul(
                        ps_o2[:, half * 512:(half + 1) * 512],
                        w2r_sb[:, slot * 32 * 128 + ic * 128:slot * 32 * 128 + (ic + 1) * 128],
                        h_sb[:, ic * ST_TOK + half * 512:ic * ST_TOK + (half + 1) * 512],
                        start=(ic == 0), stop=(ic == 31)),
                        [b_w2[slot], b_h]))
            p.group("tensor", pieces, [b_w2[slot], b_h], [b_po2])
            p.emit("scalar", lambda e, eo=eo: e.activation(
                o2_sb[:, eo * ST_TOK:(eo + 1) * ST_TOK], ps_o2[:], ACT.Identity,
                bias=b2_sb[:, eo:eo + 1]),
                reads=[b_po2, b_const], writes=[b_o2])
            if eo + W2RING < 8:
                dma_w2(s, eo + W2RING)
    def t5_unit(s, tt):
        if True:
            ti = s * STT + tt
            xsl = ti % X2RING
            pieces = []
            for eo in range(NCH):
                pieces.append((lambda e, eo=eo, tt=tt: e.matmul(
                    ps_o2[:, eo * 128:(eo + 1) * 128],
                    o2_sb[:, eo * ST_TOK + tt * 128:eo * ST_TOK + (tt + 1) * 128],
                    ident_sb[:], start=True, stop=True),
                    [b_o2, b_const]))
            p.group("tensor", pieces, [b_o2, b_const], [b_po2])
            fs = ffn_sb[:, (tt % 2) * E:(tt % 2 + 1) * E]
            p.emit("scalar", lambda e, fs=fs: e.copy(fs, ps_o2[:]),
                   reads=[b_po2], writes=[b_ffn[tt % 2]])
            ln_chain(fs, b_ffn[tt % 2], fs, b_ffn[tt % 2])
            p.emit("vector", lambda e, fs=fs, xsl=xsl: e.tensor_tensor(
                fs, fs, x2_sb[:, xsl * E:(xsl + 1) * E], op=ALU.add),
                reads=[b_ffn[tt % 2], b_x2[xsl]], writes=[b_ffn[tt % 2]])
            p.dma("sync", "out", o_d[ti * 128:(ti + 1) * 128, :], fs,
                  reads=[b_ffn[tt % 2]])

    def interleave(ua, ub):
        na, nb = len(ua), len(ub)
        ia = ib = 0
        while ia < na or ib < nb:
            if ia < na:
                ua[ia]()
                ia += 1
            while ib < nb and (na == 0 or ib * na <= ia * nb):
                ub[ib]()
                ib += 1

    def tiles_and_t5_units(s, skip=0):
        seq = []
        if s > 0:
            for i in range(3):
                seq.append(lambda s=s, i=i: t5_unit(s - 1, i))
        if skip == 0:
            seq.append(lambda s=s: tile_unit_a(s, 0))
        for tt in range(STT):
            if tt + 1 < STT and tt + 1 >= skip:
                seq.append(lambda s=s, tt=tt: tile_unit_a(s, tt + 1))
            if tt >= skip:
                seq.append(lambda s=s, tt=tt: tile_unit(s, tt))
            if s > 0 and tt + 3 < STT:
                seq.append(lambda s=s, tt=tt: t5_unit(s - 1, tt + 3))
        return seq

    for k4 in range(W1RING):
        dma_w1(0, k4)
    for s in range(NST):
        for u in tiles_and_t5_units(s, 6 if s > 0 else 0):
            u()
        for ci in range(32):
            w1_unit(s, ci, 0)
            w1_unit(s, ci, 1)
        hoist = []
        if s + 1 < NST:
            hoist = [lambda s=s: tile_unit_a(s + 1, 0),
                     lambda s=s: tile_unit_a(s + 1, 1),
                     lambda s=s: tile_unit(s + 1, 0),
                     lambda s=s: tile_unit_a(s + 1, 2),
                     lambda s=s: tile_unit(s + 1, 1),
                     lambda s=s: tile_unit_a(s + 1, 3),
                     lambda s=s: tile_unit(s + 1, 2),
                     lambda s=s: tile_unit_a(s + 1, 4),
                     lambda s=s: tile_unit(s + 1, 3),
                     lambda s=s: tile_unit_a(s + 1, 5),
                     lambda s=s: tile_unit(s + 1, 4),
                     lambda s=s: tile_unit(s + 1, 5)]
        interleave([lambda s=s, eo=eo: w2_unit(s, eo) for eo in range(8)],
                   hoist)
    for tt in range(STT):
        t5_unit(NST - 1, tt)

    sem, cntv = p.dma_sems["out"]
    p.wait("sync", (sem, cntv))
    p.flush()
    return nc


_NC_CACHE = {}


def _get_nc():
    if "nc" not in _NC_CACHE:
        _NC_CACHE["nc"] = build_program()
    return _NC_CACHE["nc"]


def _make_masks(h):
    i = np.arange(128)[:, None]
    out = np.zeros((5, 128, WW), np.float32)
    for si, qt in enumerate((0, 1, 15, NQT - 2, NQT - 1)):
        j = np.arange(WW)[None, :]
        band = np.abs(j - 256 - i) <= 256
        g = h * 4096 + qt * 128 - 256 + j
        valid = (g >= 0) & (g < SEQ)
        out[si] = (band & valid).astype(np.float32)
    return out.astype(ml_dtypes.bfloat16)


def kernel(L0_attn_hidden_states, L0_attn_Wq, L0_attn_bq, L0_attn_Wk, L0_attn_bk,
           L0_attn_Wv, L0_attn_bv, L0_ffn_W1, L0_ffn_b1, L0_ffn_W2, L0_ffn_b2):
    x = np.ascontiguousarray(np.asarray(L0_attn_hidden_states, dtype=np.float32))
    wq = np.asarray(L0_attn_Wq, np.float32)
    wk = np.asarray(L0_attn_Wk, np.float32)
    wv = np.asarray(L0_attn_Wv, np.float32)
    w1 = np.ascontiguousarray(np.asarray(L0_ffn_W1, np.float32))
    w2 = np.ascontiguousarray(np.asarray(L0_ffn_W2, np.float32))
    bq = np.asarray(L0_attn_bq, np.float32)
    bk = np.asarray(L0_attn_bk, np.float32)
    bv = np.asarray(L0_attn_bv, np.float32)
    b1 = np.asarray(L0_ffn_b1, np.float32)
    b2 = np.asarray(L0_ffn_b2, np.float32)

    wqt = np.ascontiguousarray(wq.T)
    wkt = np.ascontiguousarray(wk.T)
    wvt = np.ascontiguousarray(wv.T)
    bq_l = np.ascontiguousarray(bq.reshape(8, 128).T)
    bk_l = np.ascontiguousarray(bk.reshape(8, 128).T)
    bv_l = np.ascontiguousarray(bv.reshape(1, E))
    b1_l = np.ascontiguousarray(b1.reshape(32, 128).T)
    b2_l = np.ascontiguousarray(b2.reshape(8, 128).T)
    ident = np.eye(128, dtype=ml_dtypes.bfloat16)
    masks_h = [_make_masks(0), _make_masks(1)]

    in_maps = []
    for c in range(8):
        b, h = c // 2, c % 2
        xs = np.zeros((HT, E), np.float32)
        if h == 0:
            xs[256:] = x[b, 0:HT - 256]
        else:
            xs[:HT - 256] = x[b, SEQ - (HT - 256):SEQ]
        in_maps.append({
            "x": xs, "wqt": wqt, "wkt": wkt, "wvt": wvt, "w1": w1, "w2": w2,
            "bq": bq_l, "bk": bk_l, "bv": bv_l, "b1": b1_l, "b2": b2_l,
            "masks": masks_h[h], "ident": ident,
        })

    nc = _get_nc()
    res = run_bass_kernel_spmd(nc, in_maps, core_ids=list(range(8)))
    out = np.empty((4, SEQ, E), np.float32)
    for c in range(8):
        b, h = c // 2, c % 2
        out[b, h * 4096:(h + 1) * 4096] = np.asarray(res.results[c]["o"], np.float32)
    return out


# version 25 (final)
# speedup vs baseline: 1.1653x; 1.0046x over previous
import contextlib
import numpy as np
import ml_dtypes
import concourse.bass as bass
import concourse.mybir as mybir
from concourse.bass_utils import run_bass_kernel_spmd

F32 = mybir.dt.float32
BF16 = mybir.dt.bfloat16
AX = mybir.AxisListType.X
ALU = mybir.AluOpType
ACT = mybir.ActivationFunctionType

ENGINES = ("tensor", "vector", "scalar", "sync", "gpsimd")

E = 1024
H = 16
D = 64
NCH = 8
SEQ = 8192
TOK = 4096
HT = TOK + 512
NKT = HT // 128
NQT = TOK // 128
NG = NKT // 4
WW = 640
EPS = 1e-5
INTER = 4096
ST_TOK = 1024
NST = TOK // ST_TOK
STT = ST_TOK // 128

KRING = 12
QRING = 12
XRING = 6
X2RING = 14
D1RING = 2
W1RING = 2
W2RING = 2


class Buf:
    __slots__ = ("writer", "readers", "name")

    def __init__(self, name=""):
        self.writer = None
        self.readers = []
        self.name = name


class Prog:
    def __init__(self, nc):
        self.nc = nc
        self.sem = {}
        self.cnt = {}
        self.q = {e: [] for e in ENGINES}
        for e in ENGINES:
            self.sem[e] = nc.alloc_semaphore(f"sem_{e}")
            self.cnt[e] = 0
        self.mark = {e: {} for e in ENGINES}
        self.dma_sems = {}

    def dma_sem(self, name):
        if name not in self.dma_sems:
            self.dma_sems[name] = [self.nc.alloc_semaphore(f"dsem_{name}"), 0]
        return self.dma_sems[name]

    def _needed(self, ename, reads, writes):
        toks = []
        for b in reads:
            if b.writer is not None:
                toks.append(b.writer)
        for b in writes:
            if b.writer is not None:
                toks.append(b.writer)
            toks.extend(b.readers)
        merged = {}
        for (s, v) in toks:
            if merged.get(s.name, (None, -1))[1] < v:
                merged[s.name] = (s, v)
        wm = self.mark[ename]
        return [(s, v) for nm, (s, v) in merged.items() if wm.get(nm, 0) < v]

    def _note(self, ename, s, v):
        wm = self.mark[ename]
        if wm.get(s.name, 0) < v:
            wm[s.name] = v

    def emit(self, ename, build, reads=(), writes=(), inc=True, track=True):
        needed = self._needed(ename, reads, writes)
        for (s, v) in needed:
            self._note(ename, s, v)
        tok = None
        if inc:
            self.cnt[ename] += 1
            tok = (self.sem[ename], self.cnt[ename])

        def thunk(eng, build=build, needed=needed, tok=tok):
            for (s, v) in needed[:-1]:
                eng.wait_ge(s, v)
            inst = build(eng)
            if needed:
                inst._wait_ge(*needed[-1])
            if tok is not None:
                inst.then_inc(tok[0], 1)
        self.q[ename].append(thunk)
        if tok is not None and track:
            for b in writes:
                b.writer = tok
                b.readers = []
            for b in reads:
                b.readers.append(tok)
        return tok

    def dma(self, qname, semname, out_ap, in_ap, reads=(), writes=()):
        needed = self._needed(qname, reads, writes)
        ent = self.dma_sem(semname)
        if ent[1] > 0 and self.mark[qname].get(ent[0].name, 0) < ent[1]:
            needed.append((ent[0], ent[1]))
        for (s, v) in needed:
            self._note(qname, s, v)
        ent[1] += 16
        tok = (ent[0], ent[1])

        def thunk(eng, needed=needed, tok=tok, out_ap=out_ap, in_ap=in_ap):
            for (s, v) in needed[:-1]:
                eng.wait_ge(s, v)
            inst = eng.dma_start(out_ap, in_ap)
            if needed:
                inst._wait_ge(*needed[-1])
            inst.then_inc(tok[0], 16)
        self.q[qname].append(thunk)
        for b in writes:
            b.writer = tok
            b.readers = []
        for b in reads:
            b.readers.append(tok)
        return tok

    def group(self, ename, pieces, reads_all, writes, inc=True):
        n = len(pieces)
        for i, (bf, rd) in enumerate(pieces):
            if n == 1:
                self.emit(ename, bf, reads=reads_all, writes=writes, inc=inc)
            elif i == 0:
                self.emit(ename, bf, reads=rd, writes=writes, inc=False, track=False)
            elif i == n - 1:
                self.emit(ename, bf, reads=reads_all, writes=writes, inc=inc)
            else:
                self.emit(ename, bf, reads=rd, writes=(), inc=False)

    def wait(self, ename, tok):
        if tok is None:
            return
        s, v = tok
        if self.mark[ename].get(s.name, 0) >= v:
            return
        self._note(ename, s, v)

        def thunk(eng, s=s, v=v):
            eng.wait_ge(s, v)
        self.q[ename].append(thunk)

    def barrier(self):
        toks = [(self.sem[e], self.cnt[e]) for e in ENGINES if self.cnt[e] > 0]
        toks += [(h, c) for (h, c) in self.dma_sems.values() if c > 0]
        for e in ENGINES:
            for t in toks:
                self.wait(e, t)

    def flush(self):
        nc = self.nc
        with nc.Block() as block:
            for ename in ENGINES:
                thunks = self.q[ename]
                if not thunks:
                    continue

                def section(eng, thunks=thunks):
                    for th in thunks:
                        th(eng)
                getattr(block, ename)(section)
        self.q = {e: [] for e in ENGINES}


def mask_slot(qt):
    return {0: 0, 1: 1, NQT - 2: 3, NQT - 1: 4}.get(qt, 2)


def s_pieces(qt):
    out = []
    j = 0
    while j < WW:
        slot0 = (qt + j // 128) % KRING
        run = 128
        while j + run < WW and (qt + (j + run) // 128) % KRING == slot0 + run // 128:
            run += 128
        if j < 512 < j + run:
            run = 512 - j
        out.append((j, j + run, slot0 * 128))
        j += run
    return out


def build_program():
    nc = bass.Bass("TRN2", target_bir_lowering=False)

    x_d = nc.dram_tensor("x", [HT, E], F32, kind="ExternalInput")
    wqt_d = nc.dram_tensor("wqt", [E, E], F32, kind="ExternalInput")
    wkt_d = nc.dram_tensor("wkt", [E, E], F32, kind="ExternalInput")
    wvt_d = nc.dram_tensor("wvt", [E, E], F32, kind="ExternalInput")
    w1_d = nc.dram_tensor("w1", [E, INTER], F32, kind="ExternalInput")
    w2_d = nc.dram_tensor("w2", [INTER, E], F32, kind="ExternalInput")
    bq_d = nc.dram_tensor("bq", [128, 8], F32, kind="ExternalInput")
    bk_d = nc.dram_tensor("bk", [128, 8], F32, kind="ExternalInput")
    bv_d = nc.dram_tensor("bv", [1, E], F32, kind="ExternalInput")
    b1_d = nc.dram_tensor("b1", [128, 32], F32, kind="ExternalInput")
    b2_d = nc.dram_tensor("b2", [128, 8], F32, kind="ExternalInput")
    masks_d = nc.dram_tensor("masks", [5, 128, WW], BF16, kind="ExternalInput")
    ident_d = nc.dram_tensor("ident", [128, 128], BF16, kind="ExternalInput")
    o_d = nc.dram_tensor("o", [TOK, E], F32, kind="ExternalOutput")
    d1_d = nc.dram_tensor("d1", [TOK, E], F32, kind="Internal")

    p = Prog(nc)

    ident_sb = nc.alloc_sbuf_tensor("ident_sb", [128, 128], BF16, side="left")
    bq_sb = nc.alloc_sbuf_tensor("bq_sb", [128, 8], F32, side="left")
    bk_sb = nc.alloc_sbuf_tensor("bk_sb", [128, 8], F32, side="left")
    bv_sb = nc.alloc_sbuf_tensor("bv_sb", [1, E], BF16, side="left")
    b1_sb = nc.alloc_sbuf_tensor("b1_sb", [128, 32], F32, side="left")
    b2_sb = nc.alloc_sbuf_tensor("b2_sb", [128, 8], F32, side="left")
    ones_sb = nc.alloc_sbuf_tensor("ones_sb", [1, 128], BF16, side="left")
    c18_sb = nc.alloc_sbuf_tensor("c18_sb", [128, 1], F32, side="left")
    cm05_sb = nc.alloc_sbuf_tensor("cm05_sb", [128, 1], F32, side="left")
    bqs_sb = nc.alloc_sbuf_tensor("bqs_sb", [128, 8], F32, side="left")
    st_sb = nc.alloc_sbuf_tensor("st_sb", [128, 24], F32, side="left")
    mv_sb = nc.alloc_sbuf_tensor("mv_sb", [128, 4], F32, side="left")
    aux_sb = nc.alloc_sbuf_tensor("aux_sb", [128, 8], F32, side="left")

    b_const = Buf("const")
    b_lnp = [Buf("ln0"), Buf("ln1")]
    ln_par = [0]

    p.dma("sync", "ci", ident_sb[:], ident_d[:, :], writes=[b_const])
    p.dma("sync", "ci", bq_sb[:], bq_d[:, :], writes=[b_const])
    p.dma("sync", "ci", bk_sb[:], bk_d[:, :], writes=[b_const])
    p.dma("sync", "ci", b1_sb[:], b1_d[:, :], writes=[b_const])
    p.dma("sync", "ci", b2_sb[:], b2_d[:, :], writes=[b_const])
    p.dma("gpsimd", "cg", bv_sb[:], bv_d[:, :], writes=[b_const])
    p.emit("vector", lambda e: e.memset(ones_sb[:], 1.0), reads=[b_const], writes=[b_const])
    p.emit("vector", lambda e: e.memset(c18_sb[:], 0.125), reads=[b_const], writes=[b_const])
    p.emit("vector", lambda e: e.memset(cm05_sb[:], -0.5), reads=[b_const], writes=[b_const])
    p.emit("vector", lambda e: e.tensor_scalar_mul(bqs_sb[:], bq_sb[:], 0.125),
           reads=[b_const], writes=[b_const])

    def ln_chain(src_ap, src_buf, dst_ap, dst_buf):
        par = ln_par[0]
        ln_par[0] ^= 1
        lb = b_lnp[par]
        st = st_sb[:, 12 * par:12 * par + 12]
        mv = mv_sb[:, 2 * par:2 * par + 2]
        vpe = aux_sb[:, 4 * par + 0:4 * par + 1]
        sd = aux_sb[:, 4 * par + 1:4 * par + 2]
        r = aux_sb[:, 4 * par + 2:4 * par + 3]
        nmr = aux_sb[:, 4 * par + 3:4 * par + 4]
        p.emit("vector", lambda e: e.bn_stats(st[:, 0:6], src_ap[:, 0:512]),
               reads=[src_buf], writes=[lb])
        p.emit("vector", lambda e: e.bn_stats(st[:, 6:12], src_ap[:, 512:1024]),
               reads=[src_buf, lb], writes=[lb])
        p.emit("vector", lambda e: e.bn_aggr(mv, st),
               reads=[lb], writes=[lb])
        p.emit("vector", lambda e: e.tensor_scalar_add(vpe, mv[:, 1:2], float(EPS)),
               reads=[lb], writes=[lb])
        p.emit("scalar", lambda e: e.activation(sd, vpe, ACT.Ln),
               reads=[lb], writes=[lb])
        p.emit("scalar", lambda e: e.activation(r, sd, ACT.Exp, scale=cm05_sb[:, 0:1]),
               reads=[lb, b_const], writes=[lb])
        p.emit("vector", lambda e: e.scalar_tensor_tensor(
            nmr, mv[:, 0:1], -1.0, r, op0=ALU.mult, op1=ALU.mult),
            reads=[lb], writes=[lb])
        p.emit("scalar", lambda e: e.activation(
            dst_ap, src_ap, ACT.Identity, bias=nmr, scale=r),
            reads=[lb, src_buf], writes=[dst_buf])

    stack1 = contextlib.ExitStack()

    def sb1(name, shape, dtype):
        return stack1.enter_context(nc.sbuf_tensor(name, shape, dtype, side="right"))

    def ps1(name, shape):
        return stack1.enter_context(nc.psum_tensor(name, shape, F32))

    wq_sb = sb1("wq_sb", [128, NCH * E], BF16)
    wk_sb = sb1("wk_sb", [128, NCH * E], BF16)
    wv_sb = sb1("wv_sb", [128, NCH * E], BF16)
    masks_sb = sb1("masks_sb", [128, 5 * WW], BF16)
    x_sb = sb1("x_sb", [128, XRING * E], F32)
    xn_sb = sb1("xn_sb", [128, 2 * E], BF16)
    xnT_sb = sb1("xnT_sb", [128, NCH * 512], BF16)
    qT_sb = sb1("qT_sb", [128, NCH * QRING * 128], BF16)
    kT_sb = sb1("kT_sb", [128, NCH * KRING * 128], BF16)
    v_sb = sb1("v_sb", [128, KRING * H * 65], BF16)
    pe_sb = [sb1(f"pe_sb{i}", [128, WW], BF16) for i in range(2)]
    ph_sb = [sb1(f"ph_sb{i}", [128, WW], BF16) for i in range(4)]
    pt_sb = [sb1(f"pt_sb{i}", [128, WW], BF16) for i in range(2)]
    cpair_sb = sb1("cpair_sb", [128, 256], BF16)
    negm_sb = sb1("negm_sb", [128, 2], F32)
    rl_sb = sb1("rl_sb", [128, H], F32)
    ctx_sb = sb1("ctx_sb", [128, 2 * E], F32)

    ps_pj0 = ps1("ps_pj0", [128, 512])
    ps_s = [ps1(f"ps_s{i}", [128, WW]) for i in range(2)]
    ps_ptA = ps1("ps_ptA", [128, 512])
    ps_m = [ps1(f"ps_m{i}", [128, 512]) for i in range(2)]

    b_w = Buf("w")
    b_masks = Buf("masks")
    b_x = [Buf(f"x{i}") for i in range(XRING)]
    b_xn = [Buf(f"xn{i}") for i in range(2)]
    b_xnT = Buf("xnT")
    b_q = [Buf(f"q{i}") for i in range(QRING)]
    b_k = [Buf(f"k{i}") for i in range(KRING)]
    b_v = [Buf(f"v{i}") for i in range(KRING)]
    b_pj = [Buf("pj0"), Buf("pj1")]
    b_s = [Buf("s0"), Buf("s1")]
    b_ptA = Buf("ptA")
    b_m = [Buf("m0"), Buf("m1")]
    b_pe = [Buf("pe0"), Buf("pe1")]
    b_ph = [Buf(f"ph{i}") for i in range(4)]
    b_pt = [Buf("pt0"), Buf("pt1")]
    b_cp = [Buf("cp0"), Buf("cp1")]
    b_negm = [Buf("negm0"), Buf("negm1")]
    b_rl = Buf("rl")
    b_ctx = [Buf("ctx0"), Buf("ctx1")]
    b_d1 = [Buf(f"d1_{i}") for i in range(NQT)]

    p.dma("gpsimd", "cg", wq_sb[:].rearrange("p (c e) -> p c e", c=NCH),
          wqt_d[:, :].rearrange("(c p) e -> p c e", p=128), writes=[b_w])
    p.dma("gpsimd", "cg", wk_sb[:].rearrange("p (c e) -> p c e", c=NCH),
          wkt_d[:, :].rearrange("(c p) e -> p c e", p=128), writes=[b_w])
    p.dma("gpsimd", "cg", wv_sb[:].rearrange("p (c e) -> p c e", c=NCH),
          wvt_d[:, :].rearrange("(c p) e -> p c e", p=128), writes=[b_w])
    p.dma("sync", "ci", masks_sb[:].rearrange("p (m w) -> p m w", m=5),
          masks_d[:, :, :].rearrange("m p w -> p m w"), writes=[b_masks])

    for slot in range(KRING):
        p.emit("vector", lambda e, slot=slot: e.memset(
            v_sb[:, slot * H * 65:(slot + 1) * H * 65]
            .rearrange("p (h d) -> p h d", d=65)[:, :, 64:65], 1.0),
            reads=[b_const], writes=[b_v[slot]])

    def next_pj():
        return (ps_pj0, b_pj[0])

    def tile_ln_t1(t):
        xs = x_sb[:, (t % XRING) * E:(t % XRING + 1) * E]
        xn = xn_sb[:, (t % 2) * E:(t % 2 + 1) * E]
        ln_chain(xs, b_x[t % XRING], xn, b_xn[t % 2])
        for hh in range(2):
            pst, pbb = next_pj()
            pieces = []
            for c in range(4):
                cc = hh * 4 + c
                pieces.append((lambda e, xn=xn, pst=pst, c=c, cc=cc: e.matmul(
                    pst[:, c * 128:(c + 1) * 128],
                    xn[:, cc * 128:(cc + 1) * 128], ident_sb[:],
                    start=True, stop=True), [b_xn[t % 2], b_const]))
            p.group("tensor", pieces, [b_xn[t % 2], b_const], [pbb])
            p.emit("scalar", lambda e, hh=hh, t=t, pst=pst: e.copy(
                xnT_sb[:].rearrange("p (c t) -> p c t", c=NCH)[:, hh * 4:(hh + 1) * 4, (t % 4) * 128:(t % 4 + 1) * 128],
                pst[:].rearrange("p (c t) -> p c t", c=4)),
                reads=[pbb], writes=[b_xnT])

    def group_proj_units(g):
        us = []
        t0 = 4 * g
        qts = [t - 2 for t in range(t0, t0 + 4) if 2 <= t < 2 + NQT]
        kc0 = (t0 % KRING) * 128
        def k_chunk(c):
            pst, pbb = next_pj()
            pieces = []
            for e8 in range(NCH):
                pieces.append((lambda e, c=c, e8=e8, pst=pst: e.matmul(
                    pst[:, 0:512],
                    wk_sb[:, e8 * E + c * 128:e8 * E + (c + 1) * 128],
                    xnT_sb[:, e8 * 512:(e8 + 1) * 512],
                    start=(e8 == 0), stop=(e8 == NCH - 1)),
                    [b_w, b_xnT]))
            kbufs = [b_k[(t0 + i) % KRING] for i in range(4)]
            p.group("tensor", pieces, [b_w, b_xnT], [pbb])

            def ev():
                p.emit("scalar", lambda e, c=c, kc0=kc0, pst=pst: e.activation(
                    kT_sb[:, c * (KRING * 128) + kc0:c * (KRING * 128) + kc0 + 512],
                    pst[:, 0:512], ACT.Identity, bias=bk_sb[:, c:c + 1]),
                    reads=[pbb, b_const], writes=kbufs)
            return ev
        def q_chunk(c):
            if not qts:
                return None
            if True:
                qn = len(qts) * 128
                xoff = ((qts[0] + 2) % 4) * 128
                qc0 = ((qts[0] + 2) % QRING) * 128
                pst, pbb = next_pj()
                pieces = []
                for e8 in range(NCH):
                    pieces.append((lambda e, c=c, e8=e8, xoff=xoff, qn=qn, pst=pst: e.matmul(
                        pst[:, 0:qn],
                        wq_sb[:, e8 * E + c * 128:e8 * E + (c + 1) * 128],
                        xnT_sb[:, e8 * 512 + xoff:e8 * 512 + xoff + qn],
                        start=(e8 == 0), stop=(e8 == NCH - 1)),
                        [b_w, b_xnT]))
                qbufs = [b_q[(qt + 2) % QRING] for qt in qts]
                p.group("tensor", pieces, [b_w, b_xnT], [pbb])

                def ev():
                    p.emit("scalar", lambda e, c=c, qc0=qc0, qn=qn, pst=pst: e.activation(
                        qT_sb[:, c * (QRING * 128) + qc0:c * (QRING * 128) + qc0 + qn],
                        pst[:, 0:qn], ACT.Identity, bias=bqs_sb[:, c:c + 1],
                        scale=c18_sb[:, 0:1]),
                        reads=[pbb, b_const], writes=qbufs)
                return ev
        def v_half(i, half):
            slot = (t0 + i) % KRING
            if True:
                pst, pbb = next_pj()
                pieces = [(lambda e, half=half, pst=pst: e.matmul(
                    pst[:, 0:512], ones_sb[:, 0:128],
                    bv_sb[:, half * 512:(half + 1) * 512],
                    start=True, stop=False), [b_const])]
                for e8 in range(NCH):
                    pieces.append((lambda e, i=i, half=half, e8=e8, pst=pst: e.matmul(
                        pst[:, 0:512],
                        xnT_sb[:, e8 * 512 + i * 128:e8 * 512 + (i + 1) * 128],
                        wv_sb[:, e8 * E + half * 512:e8 * E + (half + 1) * 512],
                        start=False, stop=(e8 == NCH - 1)),
                        [b_w, b_xnT, b_const]))
                p.group("tensor", pieces, [b_w, b_xnT, b_const], [pbb])

                def vev(e, slot=slot, half=half, pst=pst):
                    base = slot * H * 65 + half * 8 * 65
                    return e.copy(
                        v_sb[:, base:base + 8 * 65].rearrange(
                            "p (h d) -> p h d", d=65)[:, :, 0:64],
                        pst[:, 0:512].rearrange("p (h d) -> p h d", d=64))

                def ev():
                    p.emit("scalar", vev, reads=[pbb], writes=[b_v[slot]])
                return ev

        for c in range(NCH):
            us.append(lambda c=c: k_chunk(c))
            us.append(lambda c=c: q_chunk(c))
        for i in range(4):
            for half in range(2):
                us.append(lambda i=i, half=half: v_half(i, half))
        stag = {"pend": None}
        out = []

        def wrap(u):
            def run(u=u):
                if stag["pend"] is not None:
                    stag["pend"]()
                    stag["pend"] = None
                r = u()
                stag["pend"] = r if callable(r) else None
            return run

        def final():
            if stag["pend"] is not None:
                stag["pend"]()
                stag["pend"] = None
        return [wrap(u) for u in us] + [final]

    def attn_stage_A(qt, h):
        qslot = (qt + 2) % QRING
        wbufs = [b_k[(qt + i) % KRING] for i in range(5)]
        pcs = s_pieces(qt)
        j = h // 2
        lo = (h % 2) * 64
        pi = h % 2
        ps = ps_s[pi]
        pieces = []
        for (w0, w1, rc0) in pcs:
            pieces.append((lambda e, ps=ps, lo=lo, j=j, w0=w0, w1=w1, rc0=rc0, qslot=qslot: e.matmul(
                ps[:, w0:w1],
                qT_sb[lo:lo + 64, j * (QRING * 128) + qslot * 128:
                      j * (QRING * 128) + (qslot + 1) * 128],
                kT_sb[lo:lo + 64, j * (KRING * 128) + rc0:
                      j * (KRING * 128) + rc0 + (w1 - w0)],
                start=True, stop=True),
                [b_q[qslot]] + wbufs))
        p.group("tensor", pieces, [b_q[qslot]] + wbufs, [b_s[pi]])

    def attn_stage_Bmax(qt, h):
        pi = h % 2
        ps = ps_s[pi]
        nm = negm_sb[:, pi:pi + 1]
        p.emit("vector", lambda e, ps=ps, nm=nm: e.reduce_max(
            nm, ps[:, 0:WW], axis=AX, negate=True),
            reads=[b_s[pi]], writes=[b_negm[pi]])

    def attn_stage_Bexp(qt, h):
        ms = mask_slot(qt)
        mask = masks_sb[:, ms * WW:(ms + 1) * WW]
        pi = h % 2
        ps = ps_s[pi]
        nm = negm_sb[:, pi:pi + 1]
        pe = pe_sb[pi]
        ph = ph_sb[h % 4]
        p.emit("scalar", lambda e, ps=ps, pe=pe, nm=nm: e.activation(
            pe[:, 0:WW], ps[:, 0:WW], ACT.Exp, bias=nm),
            reads=[b_s[pi], b_negm[pi]], writes=[b_pe[pi]])
        p.emit("gpsimd", lambda e, pe=pe, ph=ph, mask=mask: e.tensor_tensor(
            ph[:, 0:WW], pe[:, 0:WW], mask[:, 0:WW], op=ALU.mult),
            reads=[b_pe[pi], b_masks], writes=[b_ph[h % 4]])

    def attn_stage_C(qt, h):
        pi = h % 2
        ph = ph_sb[h % 4]
        pt = pt_sb[pi]
        pieces = []
        for c in range(4):
            pieces.append((lambda e, ph=ph, c=c: e.matmul(
                ps_ptA[:, c * 128:(c + 1) * 128],
                ph[:, c * 128:(c + 1) * 128], ident_sb[:],
                start=True, stop=True),
                [b_ph[h % 4], b_const]))
        p.group("tensor", pieces, [b_ph[h % 4], b_const], [b_ptA])
        p.emit("tensor", lambda e, ph=ph, pi=pi: e.matmul(
            ps_m[pi][:, 0:128], ph[:, 512:640], ident_sb[:],
            start=True, stop=True),
            reads=[b_ph[h % 4], b_const], writes=[b_m[pi]])
        p.emit("scalar", lambda e, pt=pt: e.copy(pt[:, 0:512], ps_ptA[:, 0:512]),
               reads=[b_ptA], writes=[b_pt[pi]])
        p.emit("scalar", lambda e, pt=pt, pi=pi: e.copy(pt[:, 512:WW], ps_m[pi][:, 0:128]),
               reads=[b_m[pi]], writes=[b_pt[pi]])

    def attn_stage_D(qt, h):
        pi = h % 2
        pt = pt_sb[pi]
        vbufs = [b_v[(qt + i) % KRING] for i in range(5)]
        pieces = []
        for c in range(5):
            slot = (qt + c) % KRING
            pieces.append((lambda e, pt=pt, c=c, h=h, slot=slot, pi=pi: e.matmul(
                ps_m[pi][0:65, 128:256],
                v_sb[:, slot * H * 65 + h * 65:slot * H * 65 + (h + 1) * 65],
                pt[:, c * 128:(c + 1) * 128],
                start=(c == 0), stop=(c == 4)),
                [b_pt[pi]] + vbufs))
        p.group("tensor", pieces, [b_pt[pi]] + vbufs, [b_m[pi]])

    def attn_stage_Et(qt, h):
        pi = h % 2
        cp = cpair_sb[0:65, pi * 128:(pi + 1) * 128]
        p.emit("vector", lambda e, cp=cp, pi=pi: e.tensor_copy(cp, ps_m[pi][0:65, 128:256]),
               reads=[b_m[pi]], writes=[b_cp[pi]])
        p.emit("tensor", lambda e, cp=cp, pi=pi: e.matmul(
            ps_m[pi][:, 256:321], cp, ident_sb[0:65, 0:65], start=True, stop=True),
            reads=[b_cp[pi], b_const], writes=[b_m[pi]])

    def attn_stage_Ev(qt, h):
        pi = h % 2
        p.emit("vector", lambda e, h=h, pi=pi: e.reciprocal(
            rl_sb[:, h:h + 1], ps_m[pi][:, 320:321]),
            reads=[b_m[pi]], writes=[b_rl])
        p.emit("vector", lambda e, h=h, qt=qt, pi=pi: e.tensor_scalar(
            ctx_sb[:, (qt % 2) * E + h * 64:(qt % 2) * E + (h + 1) * 64],
            ps_m[pi][:, 256:320],
            rl_sb[:, h:h + 1], None, op0=ALU.mult),
            reads=[b_m[pi], b_rl], writes=[b_ctx[qt % 2]])

    def attn_finish(qt):
        cs = ctx_sb[:, (qt % 2) * E:(qt % 2 + 1) * E]
        p.dma("gpsimd", "d1w", d1_d[qt * 128:(qt + 1) * 128, :], cs,
              reads=[b_ctx[qt % 2]], writes=[b_d1[qt]])

    def x_dma(t):
        p.dma("sync", f"x{t % XRING}", x_sb[:, (t % XRING) * E:(t % XRING + 1) * E],
              x_d[t * 128:(t + 1) * 128, :], writes=[b_x[t % XRING]])

    def proj_units(g):
        us = [(lambda t=t: tile_ln_t1(t)) for t in range(4 * g, 4 * g + 4)]
        us += group_proj_units(g)
        return us

    cell_state = {"cells": [], "i": 0, "a": 0, "mx": 0, "em": 0}

    def _adv_a(limit):
        cs = cell_state["cells"]
        while cell_state["a"] <= limit and cell_state["a"] < len(cs):
            attn_stage_A(*cs[cell_state["a"]])
            cell_state["a"] += 1

    def _adv_mx(limit):
        cs = cell_state["cells"]
        while cell_state["mx"] <= limit and cell_state["mx"] < len(cs):
            _adv_a(cell_state["mx"])
            attn_stage_Bmax(*cs[cell_state["mx"]])
            cell_state["mx"] += 1

    def _adv_em(limit):
        cs = cell_state["cells"]
        while cell_state["em"] <= limit and cell_state["em"] < len(cs):
            _adv_mx(cell_state["em"])
            attn_stage_Bexp(*cs[cell_state["em"]])
            cell_state["em"] += 1

    def cell_unit():
        cs = cell_state["cells"]
        i = cell_state["i"]
        _adv_em(i + 2)
        if i >= 2:
            attn_stage_Ev(*cs[i - 2])
            if cs[i - 2][1] == H - 1:
                attn_finish(cs[i - 2][0])
        attn_stage_C(*cs[i])
        _adv_mx(i + 3)
        _adv_a(i + 4)
        if i >= 1:
            attn_stage_D(*cs[i - 1])
            attn_stage_Et(*cs[i - 1])
        cell_state["i"] = i + 1

    def cell_push_and_emit(new_cells, tu):
        cell_state["cells"].extend(new_cells)
        na, nb = len(new_cells), len(tu)
        ia = ib = 0
        while ia < na or ib < nb:
            if ia < na:
                cell_unit()
                ia += 1
            while ib < nb and (na == 0 or ib * na <= ia * nb):
                tu[ib]()
                ib += 1

    def cell_drain():
        cs = cell_state["cells"]
        n = len(cs)

        def ev(j):
            attn_stage_Ev(*cs[j])
            if cs[j][1] == H - 1:
                attn_finish(cs[j][0])
        ev(n - 2)
        attn_stage_D(*cs[n - 1])
        attn_stage_Et(*cs[n - 1])
        ev(n - 1)

    for t in range(0, 4):
        x_dma(t)
    for u in proj_units(0):
        u()
    for g in range(NG):
        new_cells = []
        for qt in range(max(0, 4 * g - 4), min(NQT, 4 * g)):
            for h in range(H):
                new_cells.append((qt, h))
        tu = []
        if g + 1 < NG:
            for t in range(4 * (g + 1), 4 * (g + 1) + 4):
                x_dma(t)
            tu = proj_units(g + 1)
        cell_push_and_emit(new_cells, tu)
    cell_drain()

    p.flush()
    stack1.close()

    p.barrier()

    w1r_sb = nc.alloc_sbuf_tensor("w1r_sb", [128, W1RING * NCH * 512], BF16, side="right")
    w2r_sb = nc.alloc_sbuf_tensor("w2r_sb", [128, W2RING * 32 * 128], BF16, side="right")
    x2_sb = nc.alloc_sbuf_tensor("x2_sb", [128, X2RING * E], F32, side="right")
    d1l_sb = nc.alloc_sbuf_tensor("d1l_sb", [128, D1RING * E], F32, side="right")
    xn2_sb = nc.alloc_sbuf_tensor("xn2_sb", [128, 2 * E], BF16, side="right")
    xn2T_sb = nc.alloc_sbuf_tensor("xn2T_sb", [128, NCH * ST_TOK], BF16, side="right")
    h_sb = nc.alloc_sbuf_tensor("h_sb", [128, 32 * ST_TOK], BF16, side="right")
    o2_sb = nc.alloc_sbuf_tensor("o2_sb", [128, NCH * ST_TOK], BF16, side="right")
    ffn_sb = nc.alloc_sbuf_tensor("ffn_sb", [128, 2 * E], F32, side="right")

    ps_t4a = nc.alloc_psum_tensor("ps_t4a", [128, 512], F32)
    ps_t4b = nc.alloc_psum_tensor("ps_t4b", [128, 512], F32)
    ps_h0 = nc.alloc_psum_tensor("ps_h0", [128, 1024], F32)
    ps_h1 = nc.alloc_psum_tensor("ps_h1", [128, 1024], F32)
    ps_o2 = nc.alloc_psum_tensor("ps_o2", [128, 1024], F32)

    b_w1 = [Buf(f"w1_{i}") for i in range(W1RING)]
    b_w2 = [Buf(f"w2_{i}") for i in range(W2RING)]
    b_x2 = [Buf(f"x2_{i}") for i in range(X2RING)]
    b_d1l = [Buf(f"d1l_{i}") for i in range(D1RING)]
    b_xn2 = [Buf("xn2_0"), Buf("xn2_1")]
    b_xn2T = [Buf("xn2T0"), Buf("xn2T1")]
    b_h = Buf("h")
    b_o2 = Buf("o2")
    b_ffn = [Buf("ffn0"), Buf("ffn1")]
    b_ob = [Buf("ob0"), Buf("ob1")]
    b_pt4 = [Buf("pt4a"), Buf("pt4b")]
    b_ph = [Buf(f"ph{i}") for i in range(4)]
    b_po2 = Buf("po2")

    def dma_w1(s, k4):
        slot = (s * 8 + k4) % W1RING
        ci0 = k4 * 4
        p.dma("gpsimd", f"w1_{slot}",
              w1r_sb[:, slot * NCH * 512:(slot + 1) * NCH * 512].rearrange("p (c i) -> p c i", c=NCH),
              w1_d[:, ci0 * 128:(ci0 + 4) * 128].rearrange("(c p) i -> p c i", p=128),
              writes=[b_w1[slot]])

    def dma_w2(s, eo):
        slot = (s * 8 + eo) % W2RING
        p.dma("gpsimd", f"w2_{slot}",
              w2r_sb[:, slot * 32 * 128:(slot + 1) * 32 * 128].rearrange("p (c e) -> p c e", c=32),
              w2_d[:, eo * 128:(eo + 1) * 128].rearrange("(c p) e -> p c e", p=128),
              writes=[b_w2[slot]])

    def tile_unit_a(s, tt):
        ti = s * STT + tt
        xsl = ti % X2RING
        dsl = ti % D1RING
        p.dma("sync", f"x2_{xsl}", x2_sb[:, xsl * E:(xsl + 1) * E],
              x_d[256 + ti * 128:256 + (ti + 1) * 128, :], writes=[b_x2[xsl]])
        p.dma("sync", f"d1l_{dsl}", d1l_sb[:, dsl * E:(dsl + 1) * E],
              d1_d[ti * 128:(ti + 1) * 128, :],
              reads=[b_d1[ti]], writes=[b_d1l[dsl]])
        dsv = d1l_sb[:, dsl * E:(dsl + 1) * E]
        ln_chain(dsv, b_d1l[dsl], dsv, b_d1l[dsl])

    def tile_unit(s, tt):
        ti = s * STT + tt
        xsl = ti % X2RING
        dsl = ti % D1RING
        dsv = d1l_sb[:, dsl * E:(dsl + 1) * E]
        p.emit("vector", lambda e, xsl=xsl, dsv=dsv: e.tensor_tensor(
            x2_sb[:, xsl * E:(xsl + 1) * E], x2_sb[:, xsl * E:(xsl + 1) * E],
            dsv, op=ALU.add),
            reads=[b_x2[xsl], b_d1l[dsl]], writes=[b_x2[xsl]])
        xn2 = xn2_sb[:, (tt % 2) * E:(tt % 2 + 1) * E]
        ln_chain(x2_sb[:, xsl * E:(xsl + 1) * E], b_x2[xsl], xn2, b_xn2[tt % 2])
        for hh in range(2):
            pst = ps_t4a if hh == 0 else ps_t4b
            pbb = b_pt4[hh]
            pieces = []
            for c in range(4):
                cc = hh * 4 + c
                pieces.append((lambda e, xn2=xn2, pst=pst, c=c, cc=cc: e.matmul(
                    pst[:, c * 128:(c + 1) * 128],
                    xn2[:, cc * 128:(cc + 1) * 128], ident_sb[:],
                    start=True, stop=True), [b_xn2[tt % 2], b_const]))
            p.group("tensor", pieces, [b_xn2[tt % 2], b_const], [pbb])
            p.emit("scalar", lambda e, hh=hh, tt=tt, pst=pst: e.copy(
                xn2T_sb[:].rearrange("p (c t) -> p c t", c=NCH)[:, hh * 4:(hh + 1) * 4, tt * 128:(tt + 1) * 128],
                pst[:].rearrange("p (c t) -> p c t", c=4)),
                reads=[pbb], writes=[b_xn2T[tt // 4]])

    def w1_unit(s, ci, half):
        if True:
            slot = (s * 8 + ci // 4) % W1RING
            wbase = slot * NCH * 512 + (ci % 4) * 128
            psh = ps_h0 if ci % 2 == 0 else ps_h1
            pbh = b_ph[ci % 2]
            pieces = []
            for e8 in range(NCH):
                pieces.append((lambda e, e8=e8, half=half, psh=psh, wbase=wbase: e.matmul(
                    psh[:, half * 512:(half + 1) * 512],
                    w1r_sb[:, wbase + e8 * 512:wbase + e8 * 512 + 128],
                    xn2T_sb[:, e8 * ST_TOK + half * 512:e8 * ST_TOK + (half + 1) * 512],
                    start=(e8 == 0), stop=(e8 == NCH - 1)),
                    [b_w1[slot], b_xn2T[half]]))
            p.group("tensor", pieces, [b_w1[slot], b_xn2T[half]], [pbh],
                    inc=(half == 1))
            if half == 1:
                p.emit("vector", lambda e, ci=ci, psh=psh: e.tensor_scalar(
                    h_sb[:, ci * ST_TOK:(ci + 1) * ST_TOK], psh[:],
                    b1_sb[:, ci:ci + 1], 0.0, op0=ALU.add, op1=ALU.max),
                    reads=[pbh, b_const], writes=[b_h])
                if ci % 4 == 3:
                    k4n = ci // 4 + W1RING
                    if k4n < 8:
                        dma_w1(s, k4n)
                    elif s + 1 < NST:
                        dma_w1(s + 1, k4n - 8)
                    if ci // 4 < W2RING:
                        dma_w2(s, ci // 4)
    def w2_unit(s, eo):
        if True:
            slot = (s * 8 + eo) % W2RING
            pieces = []
            for ic in range(32):
                for half in range(2):
                    pieces.append((lambda e, ic=ic, half=half, slot=slot: e.matmul(
                        ps_o2[:, half * 512:(half + 1) * 512],
                        w2r_sb[:, slot * 32 * 128 + ic * 128:slot * 32 * 128 + (ic + 1) * 128],
                        h_sb[:, ic * ST_TOK + half * 512:ic * ST_TOK + (half + 1) * 512],
                        start=(ic == 0), stop=(ic == 31)),
                        [b_w2[slot], b_h]))
            p.group("tensor", pieces, [b_w2[slot], b_h], [b_po2])
            p.emit("scalar", lambda e, eo=eo: e.activation(
                o2_sb[:, eo * ST_TOK:(eo + 1) * ST_TOK], ps_o2[:], ACT.Identity,
                bias=b2_sb[:, eo:eo + 1]),
                reads=[b_po2, b_const], writes=[b_o2])
            if eo + W2RING < 8:
                dma_w2(s, eo + W2RING)
    def t5_unit(s, tt):
        if True:
            ti = s * STT + tt
            xsl = ti % X2RING
            pieces = []
            for eo in range(NCH):
                pieces.append((lambda e, eo=eo, tt=tt: e.matmul(
                    ps_o2[:, eo * 128:(eo + 1) * 128],
                    o2_sb[:, eo * ST_TOK + tt * 128:eo * ST_TOK + (tt + 1) * 128],
                    ident_sb[:], start=True, stop=True),
                    [b_o2, b_const]))
            p.group("tensor", pieces, [b_o2, b_const], [b_po2])
            fs = ffn_sb[:, (tt % 2) * E:(tt % 2 + 1) * E]
            p.emit("scalar", lambda e, fs=fs: e.copy(fs, ps_o2[:]),
                   reads=[b_po2], writes=[b_ffn[tt % 2]])
            ln_chain(fs, b_ffn[tt % 2], fs, b_ffn[tt % 2])
            p.emit("vector", lambda e, fs=fs, xsl=xsl: e.tensor_tensor(
                fs, fs, x2_sb[:, xsl * E:(xsl + 1) * E], op=ALU.add),
                reads=[b_ffn[tt % 2], b_x2[xsl]], writes=[b_ffn[tt % 2]])
            p.dma("sync", "out", o_d[ti * 128:(ti + 1) * 128, :], fs,
                  reads=[b_ffn[tt % 2]])

    def interleave(ua, ub):
        na, nb = len(ua), len(ub)
        ia = ib = 0
        while ia < na or ib < nb:
            if ia < na:
                ua[ia]()
                ia += 1
            while ib < nb and (na == 0 or ib * na <= ia * nb):
                ub[ib]()
                ib += 1

    def tiles_and_t5_units(s, skip=0):
        seq = []
        if s > 0:
            for i in range(3):
                seq.append(lambda s=s, i=i: t5_unit(s - 1, i))
        if skip == 0:
            seq.append(lambda s=s: tile_unit_a(s, 0))
        for tt in range(STT):
            if tt + 1 < STT and tt + 1 >= skip:
                seq.append(lambda s=s, tt=tt: tile_unit_a(s, tt + 1))
            if tt >= skip:
                seq.append(lambda s=s, tt=tt: tile_unit(s, tt))
        return seq

    for k4 in range(W1RING):
        dma_w1(0, k4)
    for s in range(NST):
        for u in tiles_and_t5_units(s, 6 if s > 0 else 0):
            u()
        t5tail = []
        if s > 0:
            t5tail = [lambda s=s, i=i: t5_unit(s - 1, i) for i in range(3, STT)]

        def w1_pair(s, ci):
            w1_unit(s, ci, 0)
            w1_unit(s, ci, 1)
        interleave([lambda s=s, ci=ci: w1_pair(s, ci) for ci in range(32)],
                   t5tail)
        hoist = []
        if s + 1 < NST:
            hoist = [lambda s=s: tile_unit_a(s + 1, 0),
                     lambda s=s: tile_unit_a(s + 1, 1),
                     lambda s=s: tile_unit(s + 1, 0),
                     lambda s=s: tile_unit_a(s + 1, 2),
                     lambda s=s: tile_unit(s + 1, 1),
                     lambda s=s: tile_unit_a(s + 1, 3),
                     lambda s=s: tile_unit(s + 1, 2),
                     lambda s=s: tile_unit_a(s + 1, 4),
                     lambda s=s: tile_unit(s + 1, 3),
                     lambda s=s: tile_unit_a(s + 1, 5),
                     lambda s=s: tile_unit(s + 1, 4),
                     lambda s=s: tile_unit(s + 1, 5)]
        interleave([lambda s=s, eo=eo: w2_unit(s, eo) for eo in range(8)],
                   hoist)
    for tt in range(STT):
        t5_unit(NST - 1, tt)

    sem, cntv = p.dma_sems["out"]
    p.wait("sync", (sem, cntv))
    p.flush()
    return nc


_NC_CACHE = {}


def _get_nc():
    if "nc" not in _NC_CACHE:
        _NC_CACHE["nc"] = build_program()
    return _NC_CACHE["nc"]


def _make_masks(h):
    i = np.arange(128)[:, None]
    out = np.zeros((5, 128, WW), np.float32)
    for si, qt in enumerate((0, 1, 15, NQT - 2, NQT - 1)):
        j = np.arange(WW)[None, :]
        band = np.abs(j - 256 - i) <= 256
        g = h * 4096 + qt * 128 - 256 + j
        valid = (g >= 0) & (g < SEQ)
        out[si] = (band & valid).astype(np.float32)
    return out.astype(ml_dtypes.bfloat16)


def kernel(L0_attn_hidden_states, L0_attn_Wq, L0_attn_bq, L0_attn_Wk, L0_attn_bk,
           L0_attn_Wv, L0_attn_bv, L0_ffn_W1, L0_ffn_b1, L0_ffn_W2, L0_ffn_b2):
    x = np.ascontiguousarray(np.asarray(L0_attn_hidden_states, dtype=np.float32))
    wq = np.asarray(L0_attn_Wq, np.float32)
    wk = np.asarray(L0_attn_Wk, np.float32)
    wv = np.asarray(L0_attn_Wv, np.float32)
    w1 = np.ascontiguousarray(np.asarray(L0_ffn_W1, np.float32))
    w2 = np.ascontiguousarray(np.asarray(L0_ffn_W2, np.float32))
    bq = np.asarray(L0_attn_bq, np.float32)
    bk = np.asarray(L0_attn_bk, np.float32)
    bv = np.asarray(L0_attn_bv, np.float32)
    b1 = np.asarray(L0_ffn_b1, np.float32)
    b2 = np.asarray(L0_ffn_b2, np.float32)

    wqt = np.ascontiguousarray(wq.T)
    wkt = np.ascontiguousarray(wk.T)
    wvt = np.ascontiguousarray(wv.T)
    bq_l = np.ascontiguousarray(bq.reshape(8, 128).T)
    bk_l = np.ascontiguousarray(bk.reshape(8, 128).T)
    bv_l = np.ascontiguousarray(bv.reshape(1, E))
    b1_l = np.ascontiguousarray(b1.reshape(32, 128).T)
    b2_l = np.ascontiguousarray(b2.reshape(8, 128).T)
    ident = np.eye(128, dtype=ml_dtypes.bfloat16)
    masks_h = [_make_masks(0), _make_masks(1)]

    in_maps = []
    for c in range(8):
        b, h = c // 2, c % 2
        xs = np.zeros((HT, E), np.float32)
        if h == 0:
            xs[256:] = x[b, 0:HT - 256]
        else:
            xs[:HT - 256] = x[b, SEQ - (HT - 256):SEQ]
        in_maps.append({
            "x": xs, "wqt": wqt, "wkt": wkt, "wvt": wvt, "w1": w1, "w2": w2,
            "bq": bq_l, "bk": bk_l, "bv": bv_l, "b1": b1_l, "b2": b2_l,
            "masks": masks_h[h], "ident": ident,
        })

    nc = _get_nc()
    res = run_bass_kernel_spmd(nc, in_maps, core_ids=list(range(8)))
    out = np.empty((4, SEQ, E), np.float32)
    for c in range(8):
        b, h = c // 2, c % 2
        out[b, h * 4096:(h + 1) * 4096] = np.asarray(res.results[c]["o"], np.float32)
    return out
